# version 12
# baseline (speedup 1.0000x reference)
import contextlib
import numpy as np
import ml_dtypes
import concourse.bass as bass
import concourse.mybir as mybir
from concourse.bass_utils import run_bass_kernel_spmd
from bass_rust import IndirectOffsetOnAxis

F32 = mybir.dt.float32
BF16 = mybir.dt.bfloat16
I32 = mybir.dt.int32
AF = mybir.ActivationFunctionType
ALU = mybir.AluOpType
AX = mybir.AxisListType

T = 2048
NT = 16
ND = 8
RND_C = 12582912.0
S_MAX = 768
CAPACITY = 5120

MODE = "fused"
NCORES = 8
L0_MAX_PHASE = 99
DEBUG_HEADS = None
GQA_MAX_PHASE = 99
STAGES_ALL = ["l0", "mamba", "gqa", "moe"]


def host_consts(core_id):
    bf = ml_dtypes.bfloat16

    def hadamard(size):
        h = np.array([[1.0]], dtype=np.float64)
        while h.shape[0] < size:
            h = np.block([[h, h], [h, -h]])
        return (h / np.sqrt(size)).astype(np.float32)

    bd64 = np.zeros((128, 2), bf)
    bd64[0:64, 0] = 1
    bd64[64:128, 1] = 1
    dest_sel = np.zeros((128, 32), np.float32)
    for tl in range(16):
        for k in range(2):
            dest_sel[:, 2 * tl + k] = 2 * (128 * tl + np.arange(128)) + k
    return {
        "c_hadamard": hadamard(1024).astype(bf),
        "c_ident": np.eye(128, dtype=bf),
        "c_identf": np.eye(128, dtype=np.float32),
        "c_strictup": np.triu(np.ones((128, 128), np.float32), 1),
        "c_bd64": bd64,
        "c_ones_row": np.ones((1, 128), np.float32),
        "c_onesb": np.ones((1, 64), bf),
        "c_su8": np.kron(np.triu(np.ones((16, 16), np.float32), 1),
                         np.eye(8, dtype=np.float32)),
        "c_iota8": np.tile(np.arange(8, dtype=np.float32), (128, 1)),
        "c_dest_sel": dest_sel,
        "c_prefmask": (np.arange(NCORES) < core_id).astype(np.float32).reshape(NCORES, 1),
    }


CONST_SPECS = [
    ("c_hadamard", [1024, 1024], BF16),
    ("c_ident", [128, 128], BF16),
    ("c_identf", [128, 128], F32),
    ("c_strictup", [128, 128], F32),
    ("c_bd64", [128, 2], BF16),
    ("c_ones_row", [1, 128], F32),
    ("c_onesb", [1, 64], BF16),
    ("c_su8", [128, 128], F32),
    ("c_iota8", [128, 8], F32),
    ("c_dest_sel", [128, 32], F32),
]

STAGE_WEIGHTS = {
    "l0": ["L0_attn_wq", "L0_attn_bq", "L0_attn_wk", "L0_attn_bk", "L0_attn_wv",
           "L0_attn_bv", "L0_attn_wo", "L0_attn_bo", "L0_ffn_weight"],
    "mamba": ["L1_attn_ln_w", "L1_attn_ln_b", "L1_attn_W_in", "L1_attn_conv_w",
              "L1_attn_A", "L1_attn_Bm", "L1_attn_Cm", "L1_attn_W_out"],
    "gqa": ["L1_ffn_qw", "L1_ffn_qb", "L1_ffn_kw", "L1_ffn_kb", "L1_ffn_vw",
            "L1_ffn_vb", "L1_ffn_ow", "L1_ffn_ob", "L1_ffn_norm1_w"],
    "moe": ["L1_ffn_gate_w", "L1_ffn_w1", "L1_ffn_w2", "L1_ffn_norm2_w"],
}

WEIGHT_SHAPES = {
    "L0_attn_wq": [1024, 1024], "L0_attn_bq": [1024], "L0_attn_wk": [1024, 1024],
    "L0_attn_bk": [1024], "L0_attn_wv": [1024, 1024], "L0_attn_bv": [1024],
    "L0_attn_wo": [1024, 1024], "L0_attn_bo": [1024], "L0_ffn_weight": [1024, 1024],
    "L1_attn_ln_w": [1024], "L1_attn_ln_b": [1024], "L1_attn_W_in": [1024, 3072],
    "L1_attn_conv_w": [1536, 3], "L1_attn_A": [16, 16], "L1_attn_Bm": [16, 1536],
    "L1_attn_Cm": [1536, 16], "L1_attn_W_out": [1536, 1024],
    "L1_ffn_qw": [1024, 1024], "L1_ffn_qb": [1024], "L1_ffn_kw": [1024, 256],
    "L1_ffn_kb": [256], "L1_ffn_vw": [1024, 256], "L1_ffn_vb": [256],
    "L1_ffn_ow": [1024, 1024], "L1_ffn_ob": [1024], "L1_ffn_gate_w": [1024, 8],
    "L1_ffn_w1": [8, 1024, 2048], "L1_ffn_w2": [8, 2048, 1024],
    "L1_ffn_norm1_w": [1024], "L1_ffn_norm2_w": [1024],
}

BOUNDARY = {"d_x1": "l0", "d_xn1": "mamba", "d_attn1": "mamba", "d_src": "gqa",
            "d_srcb": "gqa"}
BOUNDARY_NEED = {"l0": [], "mamba": ["d_x1"], "gqa": ["d_xn1"],
                 "moe": ["d_x1", "d_attn1", "d_src", "d_srcb"]}


class Sem:
    def __init__(self, h):
        self.h = h
        self.n = 0

    def inc(self, inst, k=1):
        inst.then_inc(self.h, k)
        self.n += k
        return self.n


class Ctx:
    def __init__(self, nc, ex):
        self.nc = nc
        self.ex = ex
        self.d = {}
        self.sems = {}

    def sem(self, name):
        if name not in self.sems:
            self.sems[name] = Sem(self.ex.enter_context(self.nc.semaphore(name)))
        return self.sems[name]


def build_program(stages):
    nc = bass.Bass("TRN2", target_bir_lowering=False)
    ex = contextlib.ExitStack()
    c = Ctx(nc, ex)

    c.d["x_in"] = nc.dram_tensor("x_in", [T, 1024], F32, kind="ExternalInput")
    out_kind = "ExternalOutput" if "moe" in stages else "Internal"
    c.d["out"] = nc.dram_tensor("out", [T, 1024], F32, kind=out_kind)
    for name, shape, dt in CONST_SPECS:
        c.d[name] = nc.dram_tensor(name, shape, dt, kind="ExternalInput")
    c.d["c_prefmask"] = nc.dram_tensor("c_prefmask", [NCORES, 1], F32,
                                       kind="ExternalInput")
    for s in stages:
        for w in STAGE_WEIGHTS[s]:
            c.d[w] = nc.dram_tensor(w, WEIGHT_SHAPES[w], F32, kind="ExternalInput")
    produced = {n for n, p in BOUNDARY.items() if p in stages}
    need = {n for s in stages for n in BOUNDARY_NEED[s]}
    for name in BOUNDARY:
        bdt = BF16 if name == "d_srcb" else F32
        if name in produced:
            kind = "Internal" if MODE == "fused" else "ExternalOutput"
            c.d[name] = nc.dram_tensor(name, [T, 1024], bdt, kind=kind)
        elif name in need:
            c.d[name] = nc.dram_tensor(name, [T, 1024], bdt, kind="ExternalInput")
    if "l0" in stages:
        c.d["d_xtmp"] = nc.dram_tensor("d_xtmp", [T, 1024], F32, kind="Internal")
    if "moe" in stages:
        c.d["d_moeout"] = nc.dram_tensor("d_moeout", [2 * T + 1, 1024], F32, kind="Internal")
        c.d["d_slotmeta"] = nc.dram_tensor("d_slotmeta", [8 * S_MAX + 1, 4], F32, kind="Internal")
        c.d["d_cin"] = nc.dram_tensor("d_cin", [1, 8], F32, kind="Internal")
        c.d["d_cout"] = nc.dram_tensor("d_cout", [NCORES, 8], F32, kind="Internal")

    with ex:
        c.A1 = ex.enter_context(nc.sbuf_tensor("A1", [128, 16640], BF16))
        c.A2 = ex.enter_context(nc.sbuf_tensor("A2", [128, 16640], BF16))
        c.A3 = ex.enter_context(nc.sbuf_tensor("A3", [128, 16640], BF16))
        c.A4 = ex.enter_context(nc.sbuf_tensor("A4", [128, 16640], BF16))
        c.A5 = ex.enter_context(nc.sbuf_tensor("A5", [128, 8192], BF16))
        c.F1 = ex.enter_context(nc.sbuf_tensor("F1", [128, 8192], F32))
        c.F2 = ex.enter_context(nc.sbuf_tensor("F2", [128, 4096], F32))
        c.SM = ex.enter_context(nc.sbuf_tensor("SM", [128, 2048], F32))
        c.CIDENT = ex.enter_context(nc.sbuf_tensor("CIDENT", [128, 128], BF16))
        c.CIDENTF = ex.enter_context(nc.sbuf_tensor("CIDENTF", [128, 128], F32))
        c.CSU = ex.enter_context(nc.sbuf_tensor("CSU", [128, 128], F32))
        c.CBD = ex.enter_context(nc.sbuf_tensor("CBD", [128, 2], BF16))
        c.CONES = ex.enter_context(nc.sbuf_tensor("CONES", [1, 128], F32))
        c.CONESB = ex.enter_context(nc.sbuf_tensor("CONESB", [1, 64], BF16))
        c.CSU8 = ex.enter_context(nc.sbuf_tensor("CSU8", [128, 128], F32))
        c.CMISC = ex.enter_context(nc.sbuf_tensor("CMISC", [128, 48], F32))
        c.SI = ex.enter_context(nc.sbuf_tensor("SI", [128, 48], I32))
        c.PS_A = ex.enter_context(nc.psum_tensor("PS_A", [128, 2048], F32))
        c.PS_B = ex.enter_context(nc.psum_tensor("PS_B", [128, 2048], F32))

        emit_const_load(c)
        for s in stages:
            globals()["emit_" + s](c)
    return nc


def psab(c):
    return c.PS_A.ap().bitcast(BF16)


def psbb(c):
    return c.PS_B.ap().bitcast(BF16)


def emit_const_load(c):
    nc = c.nc
    sd = c.sem("sd")
    with nc.Block() as blk:
        @blk.sync
        def _(e):
            for ap_out, name in [(c.CIDENT[:], "c_ident"), (c.CIDENTF[:], "c_identf"),
                                 (c.CSU[:], "c_strictup"), (c.CBD[:], "c_bd64"),
                                 (c.CONES[:], "c_ones_row"),
                                 (c.CONESB[:], "c_onesb"),
                                 (c.CSU8[:], "c_su8"),
                                 (c.CMISC[:, 0:8], "c_iota8"),
                                 (c.CMISC[:, 8:40], "c_dest_sel")]:
                sd.inc(e.dma_start(out=ap_out, in_=c.d[name].ap()), 16)
            e.wait_ge(sd.h, sd.n)

        @blk.vector
        def _(e):
            e.memset(c.CMISC[:, 40:41], 1e-5)
            e.memset(c.CMISC[:, 41:42], 1e-6)
            e.memset(c.CMISC[:, 42:43], 1.0)



def ln_tile(c, blk, xin, outs, stats, scratch, eps_col=40, gam=None, bet=None,
            waits=()):
    sv, sa = c.sem("sv"), c.sem("sa")
    mean, var, rstd, tmp = (stats[:, i:i + 1] for i in range(4))
    sv_entry = sv.n

    @blk.vector
    def _(e):
        for (s, v) in waits:
            e.wait_ge(s.h, v)
        e.wait_ge(sv.h, sv_entry)
        sv.inc(e.tensor_tensor(scratch, xin, xin, ALU.mult))
        e.wait_ge(sv.h, sv.n)
        e.reduce_sum(mean, xin, AX.X)
        sv.inc(e.reduce_sum(var, scratch, AX.X))
        e.wait_ge(sv.h, sv.n)
        e.tensor_scalar(mean, mean, 1.0 / 1024.0, None, ALU.mult)
        sv.inc(e.tensor_scalar(var, var, 1.0 / 1024.0, None, ALU.mult))
        e.wait_ge(sv.h, sv.n)
        sv.inc(e.tensor_tensor(tmp, mean, mean, ALU.mult))
        e.wait_ge(sv.h, sv.n)
        sv.inc(e.tensor_tensor(var, var, tmp, ALU.subtract))

    @blk.scalar
    def _(e):
        e.wait_ge(sv.h, sv.n)
        sa.inc(e.activation(tmp, var, AF.Ln, bias=c.CMISC[:, eps_col:eps_col + 1]))
        e.wait_ge(sa.h, sa.n)
        sa.inc(e.activation(rstd, tmp, AF.Exp, scale=-0.5))

    @blk.vector
    def _(e):
        e.wait_ge(sa.h, sa.n)
        if gam is None:
            last = None
            for o in outs:
                last = e.tensor_scalar(o, xin, mean, rstd, ALU.subtract, ALU.mult)
            sv.inc(last)
        else:
            sv.inc(e.tensor_scalar(scratch, xin, mean, rstd, ALU.subtract, ALU.mult))
            e.wait_ge(sv.h, sv.n)
            sv.inc(e.tensor_tensor(scratch, scratch, gam, ALU.mult))
            e.wait_ge(sv.h, sv.n)
            last = None
            for o in outs:
                last = e.tensor_tensor(o, scratch, bet, ALU.add)
            sv.inc(last)


def ln_jobs(c, blk, jobs, eps_col=40):
    sv, sa = c.sem("sv"), c.sem("sa")
    sv_entry = sv.n

    @blk.vector
    def _(e):
        for j in jobs:
            for (s, v) in j.get("waits", ()):
                e.wait_ge(s.h, v)
        e.wait_ge(sv.h, sv_entry)
        last = None
        for j in jobs:
            e.tensor_tensor(j["scratch"], j["xin"], j["xin"], ALU.mult)
            last = e.reduce_sum(j["stats"][:, 0:1], j["xin"], AX.X)
        sv.inc(last)
        e.wait_ge(sv.h, sv.n)
        for j in jobs:
            e.reduce_sum(j["stats"][:, 1:2], j["scratch"], AX.X)
            last = e.tensor_scalar(j["stats"][:, 0:1], j["stats"][:, 0:1],
                                   1.0 / 1024.0, None, ALU.mult)
        sv.inc(last)
        e.wait_ge(sv.h, sv.n)
        for j in jobs:
            last = e.tensor_tensor(j["stats"][:, 3:4], j["stats"][:, 0:1],
                                   j["stats"][:, 0:1], ALU.mult)
        sv.inc(last)
        e.wait_ge(sv.h, sv.n)
        for j in jobs:
            last = e.tensor_tensor(j["stats"][:, 2:3],
                                   c.CMISC[:, eps_col:eps_col + 1],
                                   j["stats"][:, 3:4], ALU.subtract)
        sv.inc(last)

    @blk.scalar
    def _(e):
        e.wait_ge(sv.h, sv.n)
        last = None
        for j in jobs:
            last = e.activation(j["stats"][:, 3:4], j["stats"][:, 1:2], AF.Ln,
                                scale=1.0 / 1024.0, bias=j["stats"][:, 2:3])
        sa.inc(last)
        e.wait_ge(sa.h, sa.n)
        last = None
        for j in jobs:
            last = e.activation(j["stats"][:, 2:3], j["stats"][:, 3:4], AF.Exp,
                                scale=-0.5)
        sa.inc(last)

    @blk.vector
    def _(e):
        e.wait_ge(sa.h, sa.n)
        last = None
        for j in jobs:
            mean, rstd = j["stats"][:, 0:1], j["stats"][:, 2:3]
            if j.get("gam") is None:
                for o in j["outs"]:
                    last = e.tensor_scalar(o, j["xin"], mean, rstd,
                                           ALU.subtract, ALU.mult)
            else:
                last = e.tensor_scalar(j["scratch"], j["xin"], mean, rstd,
                                       ALU.subtract, ALU.mult)
        sv.inc(last)
        if any(j.get("gam") is not None for j in jobs):
            e.wait_ge(sv.h, sv.n)
            last = None
            for j in jobs:
                if j.get("gam") is not None:
                    last = e.tensor_tensor(j["scratch"], j["scratch"],
                                           j["gam"], ALU.mult)
            sv.inc(last)
            e.wait_ge(sv.h, sv.n)
            last = None
            for j in jobs:
                if j.get("gam") is not None:
                    for o in j["outs"]:
                        last = e.tensor_tensor(o, j["scratch"], j["bet"],
                                               ALU.add)
            sv.inc(last)


def rms_jobs(c, blk, jobs, eps_col=41):
    sv, sa = c.sem("sv"), c.sem("sa")
    sv_entry = sv.n

    @blk.vector
    def _(e):
        for j in jobs:
            for (s, v) in j.get("waits", ()):
                e.wait_ge(s.h, v)
        e.wait_ge(sv.h, sv_entry)
        last = None
        for j in jobs:
            last = e.tensor_tensor(j["scratch"], j["xin"], j["xin"], ALU.mult)
        sv.inc(last)
        e.wait_ge(sv.h, sv.n)
        last = None
        for j in jobs:
            last = e.reduce_sum(j["stats"][:, 0:1], j["scratch"], AX.X)
        sv.inc(last)

    @blk.scalar
    def _(e):
        e.wait_ge(sv.h, sv.n)
        last = None
        for j in jobs:
            last = e.activation(j["stats"][:, 1:2], j["stats"][:, 0:1], AF.Ln,
                                scale=1.0 / 1024.0,
                                bias=c.CMISC[:, eps_col:eps_col + 1])
        sa.inc(last)
        e.wait_ge(sa.h, sa.n)
        last = None
        for j in jobs:
            last = e.activation(j["stats"][:, 2:3], j["stats"][:, 1:2], AF.Exp,
                                scale=-0.5)
        sa.inc(last)

    @blk.vector
    def _(e):
        e.wait_ge(sa.h, sa.n)
        last = None
        for j in jobs:
            last = e.scalar_tensor_tensor(j["out"], j["xin"],
                                          j["stats"][:, 2:3], j["wbcast"],
                                          ALU.mult, ALU.mult)
        sv.inc(last)


def transpose_to_T(c, blk, src_bf, dstbuf, dst_col, nchunk=8, psum_bank=0):
    sp, sv = c.sem("sp"), c.sem("sv")
    pb = psab(c)
    base = psum_bank * 1024

    @blk.tensor
    def _(e):
        e.wait_ge(sv.h, sv.n)
        for dc in range(nchunk):
            sp.inc(e.transpose(pb[:, base + dc * 128:base + (dc + 1) * 128],
                        src_bf[:, dc * 128:(dc + 1) * 128], c.CIDENT[:]))

    @blk.vector
    def _(e):
        e.wait_ge(sp.h, sp.n)
        dst = bass.AP(dstbuf, dst_col, [[16640, 128], [2048, nchunk], [1, 128]])
        sv.inc(e.tensor_copy(dst, pb[:, base:base + nchunk * 128]
                      .rearrange("p (a b) -> p a b", a=nchunk)))


def row_to_bcast(c, blk, row_ap, dst_sm, width, tmp_psum_col=0, waits=()):
    sp, sv = c.sem("sp"), c.sem("sv")

    @blk.tensor
    def _(e):
        e.wait_ge(sv.h, sv.n)
        for (s, v) in waits:
            e.wait_ge(s.h, v)
        for j in range(0, width, 512):
            w = min(512, width - j)
            sp.inc(e.matmul(c.PS_B[0:128, tmp_psum_col + j:tmp_psum_col + j + w],
                     c.CONES[0:1, 0:128], row_ap[:, j:j + w], start=True, stop=True))

    @blk.vector
    def _(e):
        e.wait_ge(sp.h, sp.n)
        sv.inc(e.tensor_copy(dst_sm, c.PS_B[:, tmp_psum_col:tmp_psum_col + width]))



def emit_l0(c):
    nc = c.nc
    sd, sg, sp, sv, sa = (c.sem(n) for n in ["sd", "sg", "sp", "sv", "sa"])
    W = c.d
    A1, A2, A3, A4, A5 = c.A1, c.A2, c.A3, c.A4, c.A5
    F1, F2, SM = c.F1, c.F2, c.SM
    F1b = c.F1.ap().bitcast(BF16)
    F2b = c.F2.ap().bitcast(BF16)
    PS_A, PS_B = c.PS_A, c.PS_B

    if L0_MAX_PHASE < 0:
        return
    with nc.Block() as blk:
        @blk.gpsimd
        def _(e):
            sg.inc(e.dma_start(
                out=bass.AP(F1, 0, [[8192, 128], [1024, 8], [1, 1024]]),
                in_=W["L0_ffn_weight"].ap().rearrange("(cc p) i -> p cc i", p=128)), 16)

        @blk.vector
        def _(e):
            e.wait_ge(sg.h, sg.n)
            for oc in range(8):
                wv = F1[:, oc * 1024:(oc + 1) * 1024]
                s_col, th, thn = SM[:, 1060:1061], SM[:, 1061:1062], SM[:, 1062:1063]
                sv.inc(e.reduce_sum(s_col, wv, AX.X, apply_absolute_value=True))
                e.wait_ge(sv.h, sv.n)
                sv.inc(e.tensor_scalar(s_col, s_col, 1.0 / 1024.0, None, ALU.mult))
                e.wait_ge(sv.h, sv.n)
                sv.inc(e.tensor_scalar_max(s_col, s_col, 1e-5))
                e.wait_ge(sv.h, sv.n)
                e.tensor_scalar(th, s_col, 0.5, None, ALU.mult)
                sv.inc(e.tensor_scalar(thn, s_col, -0.5, None, ALU.mult))
                e.wait_ge(sv.h, sv.n)
                e.tensor_scalar(F2[:, 0:1024], wv, th, None, ALU.is_gt)
                sv.inc(e.tensor_scalar(F2[:, 1024:2048], wv, thn, None, ALU.is_lt))
                e.wait_ge(sv.h, sv.n)
                sv.inc(e.tensor_tensor(F2[:, 2048:3072], F2[:, 0:1024], F2[:, 1024:2048],
                                ALU.subtract))
                e.wait_ge(sv.h, sv.n)
                sv.inc(e.tensor_scalar(A3[:, oc * 1024:(oc + 1) * 1024], F2[:, 2048:3072],
                                s_col, None, ALU.mult))
                e.wait_ge(sv.h, sv.n)

    if L0_MAX_PHASE < 1:
        return
    with nc.Block() as blk:
        @blk.gpsimd
        def _(e):
            sg.inc(e.dma_start(
                out=F2b.rearrange("p (cc ee) -> p cc ee", cc=8),
                in_=c.d["c_hadamard"].ap().rearrange("(cc p) ee -> p cc ee", p=128)), 16)
            for i, nm in enumerate(["L0_attn_wq", "L0_attn_wk", "L0_attn_wv", "L0_attn_wo"]):
                buf = A1 if i < 2 else A2
                off = 8192 * (i % 2)
                sg.inc(e.dma_start(
                    out=bass.AP(buf, off, [[16640, 128], [1024, 8], [1, 1024]]),
                    in_=W[nm].ap().rearrange("(cc p) dd -> p cc dd", p=128)), 16)

        pb = psab(c)
        for ic in range(8):
            @blk.tensor
            def _(e, ic=ic):
                if ic > 0:
                    e.wait_ge(sv.h, sv.n)
                for oc in range(8):
                    sp.inc(e.transpose(pb[:, oc * 128:(oc + 1) * 128],
                                A3[:, oc * 1024 + ic * 128:oc * 1024 + (ic + 1) * 128],
                                c.CIDENT[:]))

            @blk.vector
            def _(e, ic=ic):
                e.wait_ge(sp.h, sp.n)
                sv.inc(e.tensor_copy(bass.AP(A5, ic * 1024, [[8192, 128], [128, 8], [1, 128]]),
                              pb[:, 0:1024].rearrange("p (a b) -> p a b", a=8)))

    if L0_MAX_PHASE < 2:
        return
    with nc.Block() as blk:
        @blk.sync
        def _(e):
            for i, nm in enumerate(["L0_attn_bq", "L0_attn_bk"]):
                sd.inc(e.dma_start(out=bass.AP(SM, 1024 + i * 128, [[2048, 8], [1, 128]]),
                                   in_=W[nm].ap().rearrange("(cc p) -> cc p", p=128)), 16)
            sd.inc(e.dma_start(out=bass.AP(SM, 0, [[2048, 1], [1, 1024]]),
                               in_=W["L0_attn_bv"].ap().rearrange("(a b) -> a b", a=1)), 16)

        pb = psab(c)
        for i in range(4):
            src = A1 if i < 2 else A2
            soff = 8192 * (i % 2)
            dst = A4 if i < 2 else A3
            doff = 8192 * (i % 2)
            for dc in range(8):
                @blk.tensor
                def _(e, src=src, soff=soff, dc=dc, first=(i == 0 and dc == 0)):
                    if first:
                        e.wait_ge(sg.h, sg.n)
                    else:
                        e.wait_ge(sv.h, sv.n)
                    for ec in range(8):
                        sp.inc(e.transpose(pb[:, ec * 128:(ec + 1) * 128],
                                    bass.AP(src, soff + ec * 1024 + dc * 128,
                                            [[16640, 128], [1, 128]]),
                                    c.CIDENT[:]))

                @blk.vector
                def _(e, dst=dst, doff=doff, dc=dc):
                    e.wait_ge(sp.h, sp.n)
                    sv.inc(e.tensor_copy(bass.AP(dst, doff + dc * 1024, [[16640, 128], [1, 1024]]),
                                  pb[:, 0:1024]))

        @blk.tensor
        def _(e):
            e.wait_ge(sd.h, sd.n)
            e.wait_ge(sv.h, sv.n)
            e.transpose(PS_B[:, 0:8], bass.AP(SM, 1024, [[2048, 8], [1, 128]]),
                        c.CIDENTF[0:8, 0:8])
            sp.inc(e.transpose(PS_B[:, 8:16], bass.AP(SM, 1152, [[2048, 8], [1, 128]]),
                        c.CIDENTF[0:8, 0:8]))

        @blk.vector
        def _(e):
            e.wait_ge(sp.h, sp.n)
            e.tensor_copy(SM[:, 1024:1032], PS_B[:, 0:8])
            sv.inc(e.tensor_copy(SM[:, 1032:1040], PS_B[:, 8:16]))

    if L0_MAX_PHASE < 3:
        return
    sl = [c.sem("sl0"), c.sem("sl1")]
    ss = [c.sem("ss0"), c.sem("ss1")]
    with nc.Block() as blk:
        for tp in range(NT // 2):
            t0 = 2 * tp
            jobs = []
            for k in range(2):
                t = t0 + k
                xt = F1[:, k * 1024:(k + 1) * 1024]

                @blk.sync
                def _(e, t=t, xt=xt, k=k):
                    if t >= 2:
                        e.wait_ge(c.sem("sv").h, c.sem("sv").n)
                    sl[k].inc(e.dma_start(out=xt, in_=c.d["x_in"].ap()[t * 128:(t + 1) * 128, :]), 16)

                jobs.append(dict(xin=xt, outs=[F1b[:, 8192 + k * 1024:9216 + k * 1024]],
                                 stats=SM[:, 1056 + 4 * k:1060 + 4 * k],
                                 scratch=F1[:, 2048 + k * 1024:3072 + k * 1024],
                                 waits=[(sl[k], sl[k].n)]))
            ln_jobs(c, blk, jobs)
            for k in range(2):
                transpose_to_T(c, blk, F1b[:, 8192 + k * 1024:9216 + k * 1024],
                               A1, (t0 + k) * 128, 8, psum_bank=k)

    if L0_MAX_PHASE < 4:
        return
    with nc.Block() as blk:
        for t in range(NT):
            slot = t % 2
            xt = F1[:, slot * 1024:(slot + 1) * 1024]

            @blk.sync
            def _(e, t=t, xt=xt, slot=slot):
                if t >= 2:
                    e.wait_ge(ss[slot].h, ss[slot].n)
                sl[slot].inc(e.dma_start(out=xt, in_=c.d["x_in"].ap()[t * 128:(t + 1) * 128, :]), 16)

            @blk.tensor
            def _(e, t=t):
                e.wait_ge(sv.h, sv.n)
                for ns in range(2):
                    for dc in range(8):
                        sp.inc(e.matmul(PS_A[:, ns * 512:(ns + 1) * 512],
                                 A1[:, dc * 2048 + t * 128:dc * 2048 + (t + 1) * 128],
                                 F2b[:, dc * 1024 + ns * 512:dc * 1024 + (ns + 1) * 512],
                                 start=(dc == 0), stop=(dc == 7)))

            @blk.vector
            def _(e, t=t):
                e.wait_ge(sp.h, sp.n)
                amax, inv, s7 = SM[:, 1056:1057], SM[:, 1057:1058], SM[:, 1058:1059]
                sv.inc(e.reduce_max(amax, PS_A[:, 0:1024], AX.X, apply_absolute_value=True))
                e.wait_ge(sv.h, sv.n)
                sv.inc(e.tensor_scalar_max(amax, amax, 1e-5))
                e.wait_ge(sv.h, sv.n)
                e.reciprocal(inv, amax)
                sv.inc(e.tensor_scalar(s7, amax, 1.0 / 7.0, None, ALU.mult))
                e.wait_ge(sv.h, sv.n)
                sv.inc(e.tensor_scalar(F1[:, 2048:3072], PS_A[:, 0:1024], inv, 7.0, ALU.mult, ALU.mult))
                e.wait_ge(sv.h, sv.n)
                sv.inc(e.tensor_scalar(F1[:, 3072:4096], F1[:, 2048:3072], RND_C, RND_C,
                                ALU.add, ALU.subtract))
                e.wait_ge(sv.h, sv.n)
                sv.inc(e.tensor_scalar(F1b[:, 8192:9216], F1[:, 3072:4096], s7, None, ALU.mult))

            transpose_to_T(c, blk, F1b[:, 8192:9216], A2, t * 128, 8, psum_bank=2 + t % 2)

            @blk.tensor
            def _(e, t=t):
                e.wait_ge(sv.h, sv.n)
                for ns in range(2):
                    for ic in range(8):
                        sp.inc(e.matmul(PS_B[:, ns * 512:(ns + 1) * 512],
                                 A2[:, ic * 2048 + t * 128:ic * 2048 + (t + 1) * 128],
                                 A5[:, ic * 1024 + ns * 512:ic * 1024 + (ns + 1) * 512],
                                 start=(ic == 0), stop=(ic == 7)))

            @blk.vector
            def _(e, xt=xt, slot=slot):
                e.wait_ge(sp.h, sp.n)
                e.wait_ge(sl[slot].h, sl[slot].n)
                sv.inc(e.scalar_tensor_tensor(xt, PS_B[:, 0:1024], 1.0, xt, ALU.mult, ALU.add))

            @blk.gpsimd
            def _(e, t=t, xt=xt, slot=slot):
                e.wait_ge(sv.h, sv.n)
                ss[slot].inc(e.dma_start(out=c.d["d_xtmp"].ap()[t * 128:(t + 1) * 128, :], in_=xt), 16)

        @blk.gpsimd
        def _(e):
            e.wait_ge(ss[0].h, ss[0].n)
            e.wait_ge(ss[1].h, ss[1].n)

    if L0_MAX_PHASE < 5:
        return
    emit_qkv(c, xnT=A1.ap(), wq=(A4, 0), wk=(A4, 8192), wv=(A3, 0),
             QT=A2.ap(), KT=F1b, Vext=A4.ap(),
             bq_cols=SM[:, 1024:1032], bk_cols=SM[:, 1032:1040],
             bv_row=SM[0:1, 0:1024], bvb=SM[:, 0:1024], nk_e=1024, nv=1024)

    if L0_MAX_PHASE < 6:
        return
    emit_norm_bounds(c, QT=A2.ap(), KT=F1b, nq_heads=16, nkv=16, kv_group=1,
                     scale=0.25, sq_scratch=A5[:, 4096:6144],
                     negc_cols=SM[:, 1040:1056])
    emit_attention(c, QT=A2.ap(), KT=F1b, Vext=A4.ap(), outT=A1.ap(), nq_heads=16,
                   kv_group=1, scale=0.25, scratch_bf=A5.ap(),
                   negc_cols=SM[:, 1040:1056],
                   recip_bf=F2.ap().bitcast(BF16)[0:1, 4096:6144],
                   recip_tmp=F2[0:1, 0:2048])

    if L0_MAX_PHASE < 7:
        return
    with nc.Block() as blk:
        @blk.sync
        def _(e):
            sd.inc(e.dma_start(out=bass.AP(SM, 0, [[2048, 1], [1, 1024]]),
                               in_=W["L0_attn_bo"].ap().rearrange("(a b) -> a b", a=1)), 16)

        row_to_bcast(c, blk, SM[0:1, 0:1024], SM[:, 0:1024], 1024,
                     waits=[(sd, sd.n)])

        for t in range(NT):
            slot = t % 2
            xt = F1[:, slot * 1024:(slot + 1) * 1024]

            @blk.sync
            def _(e, t=t, xt=xt, slot=slot):
                e.wait_ge(ss[slot].h, ss[slot].n)
                sl[slot].inc(e.dma_start(out=xt,
                                         in_=c.d["d_xtmp"].ap()[t * 128:(t + 1) * 128, :]), 16)

            @blk.tensor
            def _(e, t=t):
                if t >= 2:
                    e.wait_ge(sv.h, sv.n)
                ps = PS_A[:, (t % 2) * 1024:(t % 2) * 1024 + 1024]
                for ns in range(2):
                    for dc in range(8):
                        sp.inc(e.matmul(ps[:, ns * 512:(ns + 1) * 512],
                                 A1[:, dc * 2048 + t * 128:dc * 2048 + (t + 1) * 128],
                                 A3[:, 8192 + dc * 1024 + ns * 512:8192 + dc * 1024 + (ns + 1) * 512],
                                 start=(dc == 0), stop=(dc == 7)))

            @blk.vector
            def _(e, t=t, xt=xt, slot=slot):
                e.wait_ge(sp.h, sp.n)
                e.wait_ge(sl[slot].h, sl[slot].n)
                ps = PS_A[:, (t % 2) * 1024:(t % 2) * 1024 + 1024]
                sv.inc(e.scalar_tensor_tensor(xt, ps, 1.0, xt, ALU.mult, ALU.add))
                e.wait_ge(sv.h, sv.n)
                sv.inc(e.tensor_tensor(xt, xt, SM[:, 0:1024], ALU.add))

            @blk.gpsimd
            def _(e, t=t, xt=xt, slot=slot):
                e.wait_ge(sv.h, sv.n)
                ss[slot].inc(e.dma_start(out=c.d["d_x1"].ap()[t * 128:(t + 1) * 128, :],
                                         in_=xt), 16)

        @blk.gpsimd
        def _(e):
            e.wait_ge(ss[0].h, ss[0].n)
            e.wait_ge(ss[1].h, ss[1].n)


def emit_warmup(c, blk, n=28):
    sp = c.sem("sp")

    @blk.tensor
    def _(e):
        last = None
        for i in range(2 * n):
            last = e.matmul(c.PS_B[0:128, 0:128], c.CIDENT[:],
                            c.CIDENT[:], start=True, stop=True)
        sp.inc(last)


def emit_qkv(c, xnT, wq, wk, wv, QT, KT, Vext, bq_cols, bk_cols, bv_row, bvb,
             nk_e=1024, nv=1024, entry_waits=()):
    wq = (wq[0].ap() if not isinstance(wq[0], bass.AP) else wq[0], wq[1])
    wk = (wk[0].ap() if not isinstance(wk[0], bass.AP) else wk[0], wk[1])
    wv = (wv[0].ap() if not isinstance(wv[0], bass.AP) else wv[0], wv[1])
    """Build QT [128, 8*2048] and KT [128, (nk_e/128)*2048] feature-major bf16
    (with per-partition bias via ACT evac), and Vext token-major bf16 with bias
    broadcast and ones columns (vcols = (nv/64)*65 per tile).
    wq/wk/wv: (buf, off) with layout [128 d-part, dc*ewidth + e] bf16,
    ewidth = 1024 for q, nk_e for k, nv for v.
    """
    nc = c.nc
    sd, sp, sv, sa = (c.sem(n) for n in ["sd", "sp", "sv", "sa"])
    PS_A, PS_B = c.PS_A, c.PS_B

    with nc.Block() as blk:
        act_marks = []
        jobs = []
        for ec in range(8):
            for tq in range(4):
                jobs.append((wq, 1024, QT, bq_cols, ec, tq))
        for ec in range(nk_e // 128):
            for tq in range(4):
                jobs.append((wk, nk_e, KT, bk_cols, ec, tq))
        for gi, (wbo, ew, dstbuf, bcols, ec, tq) in enumerate(jobs):
            pslot = gi % 4

            @blk.tensor
            def _(e, wbo=wbo, ew=ew, ec=ec, tq=tq, pslot=pslot, gi=gi):
                if gi == 0:
                    for (s, v) in entry_waits:
                        e.wait_ge(s.h, v)
                if gi >= 4:
                    e.wait_ge(sv.h, act_marks[gi - 4])
                ps = PS_A[:, pslot * 512:(pslot + 1) * 512]
                wbuf, woff = wbo
                for dc in range(8):
                    sp.inc(e.matmul(ps,
                             wbuf[:, woff + dc * ew + ec * 128:woff + dc * ew + (ec + 1) * 128],
                             xnT[:, dc * 2048 + tq * 512:dc * 2048 + (tq + 1) * 512],
                             start=(dc == 0), stop=(dc == 7)))

            @blk.vector
            def _(e, dstbuf=dstbuf, bcols=bcols, ec=ec, tq=tq, pslot=pslot):
                e.wait_ge(sp.h, sp.n)
                ps = PS_A[:, pslot * 512:(pslot + 1) * 512]
                dst = dstbuf[:, ec * 2048 + tq * 512:ec * 2048 + (tq + 1) * 512]
                act_marks.append(sv.inc(
                    e.tensor_scalar(dst, ps, bcols[:, ec:ec + 1], None, ALU.add)))

    nh = nv // 64
    vcols = nh * 65
    with nc.Block() as blk:
        row_to_bcast(c, blk, bv_row, bvb[:, 0:nv], nv)

        @blk.vector
        def _(e):
            e.wait_ge(sv.h, sv.n)
            sv.inc(e.memset(Vext[:, 0:NT * vcols], 1.0))

        ev_marks = []
        for t in range(NT):
            @blk.tensor
            def _(e, t=t):
                if t == 0:
                    for (s, v) in entry_waits:
                        e.wait_ge(s.h, v)
                if t < 2:
                    e.wait_ge(sv.h, sv.n)
                else:
                    e.wait_ge(sv.h, ev_marks[t - 2])
                ps = PS_B[:, (t % 2) * 1024:(t % 2) * 1024 + 1024]
                wbuf, woff = wv
                for j in range(0, nv, 512):
                    wdt = min(512, nv - j)
                    for dc in range(8):
                        sp.inc(e.matmul(ps[:, j:j + wdt],
                                 xnT[:, dc * 2048 + t * 128:dc * 2048 + (t + 1) * 128],
                                 wbuf[:, woff + dc * nv + j:woff + dc * nv + j + wdt],
                                 start=(dc == 0), stop=(dc == 7)))

            @blk.vector
            def _(e, t=t):
                e.wait_ge(sp.h, sp.n)
                ps = PS_B[:, (t % 2) * 1024:(t % 2) * 1024 + 1024]
                dst = Vext[:, t * vcols:(t + 1) * vcols] \
                    .rearrange("p (h x) -> p h x", h=nh)[:, :, 0:64]
                ev_marks.append(sv.inc(e.tensor_tensor(
                    dst, ps[:, 0:nv].rearrange("p (a b) -> p a b", a=nh),
                    bvb[:, 0:nv].rearrange("p (a b) -> p a b", a=nh), ALU.add)))


def emit_norm_bounds(c, QT, KT, nq_heads, nkv, kv_group, scale, sq_scratch,
                     negc_cols):
    nc = c.nc
    sp, sv, sa = (c.sem(n) for n in ["sp", "sv", "sa"])
    PS_A, PS_B = c.PS_A, c.PS_B
    F2 = c.F2

    with nc.Block() as blk:
        @blk.vector
        def _(e):
            sv.inc(e.memset(F2[0:2, 0:64], 0.0))

        chunks = ([(QT, hc, hc) for hc in range(nq_heads // 2)] +
                  [(KT, hc, 16 + hc) for hc in range(nkv // 2)])
        for i, (buf, hc, col) in enumerate(chunks):
            @blk.vector
            def _(e, buf=buf, hc=hc, i=i):
                if i > 0:
                    e.wait_ge(sp.h, sp.n)
                e.wait_ge(sv.h, sv.n)
                src_v = buf[:, hc * 2048:(hc + 1) * 2048]
                sa.inc(e.tensor_tensor(sq_scratch, src_v, src_v, ALU.mult))

            @blk.tensor
            def _(e, i=i):
                e.wait_ge(sa.h, sa.n)
                if i > 0:
                    e.wait_ge(sv.h, sv.n)
                for j in range(4):
                    sp.inc(e.matmul(PS_B[0:2, j * 512:(j + 1) * 512], c.CBD[:, 0:2],
                             sq_scratch[:, j * 512:(j + 1) * 512],
                             start=True, stop=True))

            @blk.vector
            def _(e, col=col, i=i):
                e.wait_ge(sp.h, sp.n)
                if i == 0:
                    e.wait_ge(sv.h, sv.n)
                sv.inc(e.reduce_max(F2[0:2, col:col + 1], PS_B[0:2, 0:2048], AX.X))

        @blk.scalar
        def _(e):
            e.wait_ge(sv.h, sv.n)
            sa.inc(e.activation(F2[0:2, 32:64], F2[0:2, 0:32], AF.Sqrt))

        @blk.tensor
        def _(e):
            e.wait_ge(sa.h, sa.n)
            for hc in range(nq_heads // 2):
                e.transpose(PS_A[0:1, 64 + 2 * hc:64 + 2 * hc + 2],
                            F2[0:2, 32 + hc:33 + hc], c.CIDENTF[0:2, 0:2])
            for kc in range(nkv // 2):
                sp.inc(e.transpose(PS_A[0:1, 96 + 2 * kc:96 + 2 * kc + 2],
                            F2[0:2, 48 + kc:49 + kc], c.CIDENTF[0:2, 0:2]))

        @blk.vector
        def _(e):
            e.wait_ge(sp.h, sp.n)
            e.tensor_copy(F2[0:1, 64:64 + nq_heads], PS_A[0:1, 64:64 + nq_heads])
            sv.inc(e.tensor_copy(F2[0:1, 96:96 + nkv], PS_A[0:1, 96:96 + nkv]))
            e.wait_ge(sv.h, sv.n)
            if kv_group == 1:
                sv.inc(e.tensor_tensor(F2[0:1, 128:128 + nq_heads],
                                       F2[0:1, 64:64 + nq_heads],
                                       F2[0:1, 96:96 + nkv], ALU.mult))
            else:
                qv = F2[0:1, 64:64 + nq_heads].rearrange("p (a b) -> p a b", a=nkv)
                kv = bass.AP(c.F2, 96, [[4096, 1], [1, nkv], [0, kv_group]])
                ov = F2[0:1, 128:128 + nq_heads].rearrange("p (a b) -> p a b", a=nkv)
                sv.inc(e.tensor_tensor(ov, qv, kv, ALU.mult))
            e.wait_ge(sv.h, sv.n)
            sv.inc(e.tensor_scalar(F2[0:1, 160:160 + nq_heads], F2[0:1, 128:128 + nq_heads],
                            -scale, None, ALU.mult))

        @blk.tensor
        def _(e):
            e.wait_ge(sv.h, sv.n)
            sp.inc(e.matmul(PS_B[0:128, 0:nq_heads], c.CONES[0:1, 0:128],
                     F2[0:1, 160:160 + nq_heads], start=True, stop=True))

        @blk.vector
        def _(e):
            e.wait_ge(sp.h, sp.n)
            sv.inc(e.tensor_copy(negc_cols, PS_B[:, 0:nq_heads]))


def emit_attention(c, QT, KT, Vext, outT, nq_heads, kv_group, scale, scratch_bf,
                   negc_cols, recip_bf, recip_tmp, KTflip=None):
    nc = c.nc
    sp, sv, sa = (c.sem(n) for n in ["sp", "sv", "sa"])
    PS_A, PS_B = c.PS_A, c.PS_B
    nkvh = nq_heads // kv_group
    vcols = nkvh * 65

    heads = list(range(nq_heads)) if DEBUG_HEADS is None else list(DEBUG_HEADS)

    def hparams(h):
        hc, hb = h // 2, 64 * (h % 2)
        g = h // kv_group
        kc = g // 2
        if 64 * (g % 2) == hb or KTflip is None:
            kbuf = KT
        else:
            kbuf = KTflip
        return hc, hb, g, kc, kbuf

    with nc.Block() as blk:
        emit_warmup(c, blk)
        seq = []
        for i, h in enumerate(heads):
            for kt in range(NT):
                if kt == 0:
                    if i > 0:
                        seq += [("AV", heads[i - 1], NT - 1, 0), ("AV", heads[i - 1], NT - 1, 1)]
                    seq += [("SC", h, 0, 0), ("SC", h, 0, 1)]
                    if i > 0:
                        seq += [("BC", heads[i - 1], 0, 0)]
                else:
                    seq += [("SC", h, kt, 0), ("SC", h, kt, 1),
                            ("AV", h, kt - 1, 0), ("AV", h, kt - 1, 1)]
        hl = heads[-1]
        seq += [("AV", hl, NT - 1, 0), ("AV", hl, NT - 1, 1), ("BC", hl, 0, 0)]
        P0 = sp.n
        pmark = {g: P0 + i + 1 for i, g in enumerate(seq)}
        E0 = sa.n
        emark = {}
        rmark = {}
        for i, h in enumerate(heads):
            for kt in range(NT):
                for hq in range(2):
                    emark[(h, kt, hq)] = E0 + i * (2 * NT + 2) + kt * 2 + hq + 1
            rmark[h] = E0 + (i + 1) * (2 * NT + 2)
        V0 = sv.n
        vmark_ev = {h: V0 + 2 * i + 1 for i, h in enumerate(heads)}
        vmark_mu = {h: V0 + 2 * i + 2 for i, h in enumerate(heads)}

        for i, h in enumerate(heads):
            @blk.scalar
            def _(e, h=h):
                for kt in range(NT):
                    for hq in range(2):
                        e.wait_ge(sp.h, pmark[("SC", h, kt, hq)])
                        if kt >= 2:
                            e.wait_ge(sp.h, pmark[("AV", h, kt - 2, hq)])
                        e.activation(scratch_bf[:, ((kt % 2) * 2 + hq) * 1024:
                                                ((kt % 2) * 2 + hq) * 1024 + 1024],
                                     PS_A[:, hq * 1024:(hq + 1) * 1024], AF.Exp,
                                     scale=scale, bias=negc_cols[:, h:h + 1]) \
                            .then_inc(sa.h, 1)
                e.wait_ge(sp.h, pmark[("AV", h, NT - 1, 1)])
                e.activation(recip_tmp, PS_B[64:65, 0:2048],
                             AF.Ln).then_inc(sa.h, 1)
                e.wait_ge(sa.h, rmark[h] - 1)
                e.activation(recip_bf, recip_tmp, AF.Exp,
                             scale=-1.0).then_inc(sa.h, 1)
                sa.n += 2 * NT + 2

        prevh = {heads[i]: heads[i - 1] for i in range(1, len(heads))}

        @blk.tensor
        def _(e):
            for g in seq:
                kind, h, kt, hq = g
                hc, hb, gg, kc, kbuf = hparams(h)
                if kind == "SC":
                    if kt == 0:
                        if h == heads[0]:
                            pass
                        else:
                            e.wait_ge(sa.h, emark[(prevh[h], NT - 1, hq)])
                    else:
                        e.wait_ge(sa.h, emark[(h, kt - 1, hq)])
                    last = None
                    for j in range(2):
                        last = e.matmul(
                            PS_A[:, hq * 1024 + j * 512:hq * 1024 + (j + 1) * 512],
                            kbuf[hb:hb + 64, kc * 2048 + kt * 128:kc * 2048 + (kt + 1) * 128],
                            QT[hb:hb + 64,
                               hc * 2048 + hq * 1024 + j * 512:hc * 2048 + hq * 1024 + (j + 1) * 512],
                            start=True, stop=True)
                    sp.inc(last)
                elif kind == "AV":
                    e.wait_ge(sa.h, emark[(h, kt, hq)])
                    if kt == 0 and hq == 0 and h != heads[0]:
                        e.wait_ge(sv.h, vmark_mu[prevh[h]])
                    vb = Vext[:, kt * vcols + (gg % nkvh) * 65:kt * vcols + (gg % nkvh) * 65 + 65]
                    pslot = ((kt % 2) * 2 + hq) * 1024
                    last = None
                    for j in range(2):
                        last = e.matmul(
                            PS_B[0:65, hq * 1024 + j * 512:hq * 1024 + (j + 1) * 512],
                            vb, scratch_bf[:, pslot + j * 512:pslot + (j + 1) * 512],
                            start=(kt == 0), stop=(kt == NT - 1))
                    sp.inc(last)
                else:
                    e.wait_ge(sv.h, vmark_ev[h])
                    e.wait_ge(sa.h, rmark[h])
                    last = None
                    for j in range(4):
                        last = e.matmul(PS_B[0:64, j * 512:(j + 1) * 512],
                                        c.CONESB[0:1, 0:64],
                                        recip_bf[:, j * 512:(j + 1) * 512],
                                        start=True, stop=True)
                    sp.inc(last)

        for i, h in enumerate(heads):
            @blk.vector
            def _(e, h=h):
                e.wait_ge(sp.h, pmark[("AV", h, NT - 1, 1)])
                sv.inc(e.tensor_copy(scratch_bf[0:64, 4096:6144], PS_B[0:64, 0:2048]))

            @blk.vector
            def _(e, h=h):
                hc, hb = h // 2, 64 * (h % 2)
                e.wait_ge(sp.h, pmark[("BC", h, 0, 0)])
                sv.inc(e.tensor_tensor(outT[hb:hb + 64, hc * 2048:(hc + 1) * 2048],
                                       scratch_bf[0:64, 4096:6144], PS_B[0:64, 0:2048],
                                       ALU.mult))


def emit_mamba(c):
    nc = c.nc
    sd, sg, sp, sv, sa = (c.sem(n) for n in ["sd", "sg", "sp", "sv", "sa"])
    sl = [c.sem("sl0"), c.sem("sl1")]
    ss = [c.sem("ss0"), c.sem("ss1")]
    W = c.d
    A1, A2, A3, A4, A5 = c.A1, c.A2, c.A3, c.A4, c.A5
    F1, F2, SM = c.F1, c.F2, c.SM
    F1b = c.F1.ap().bitcast(BF16)
    PS_A, PS_B = c.PS_A, c.PS_B
    pb = psab(c)
    pbb = psbb(c)

    with nc.Block() as blk:
        @blk.gpsimd
        def _(e):
            win = W["L1_attn_W_in"].ap().rearrange("(dc p) i -> p dc i", p=128)
            sg.inc(e.dma_start(
                out=A3.ap()[:, 0:12288].rearrange("p (dc i) -> p dc i", dc=8),
                in_=win[:, :, 0:1536]), 16)
            sg.inc(e.dma_start(
                out=A4.ap()[:, 0:12288].rearrange("p (dc i) -> p dc i", dc=8),
                in_=win[:, :, 1536:3072]), 16)
            sg.inc(e.dma_start(out=A2.ap()[0:16, 14016:15552],
                               in_=W["L1_attn_Bm"].ap()), 16)

        @blk.sync
        def _(e):
            sd.inc(e.dma_start(
                out=F2.ap()[:, 2752:2788].rearrange("p (ic j) -> p ic j", ic=12),
                in_=W["L1_attn_conv_w"].ap().rearrange("(ic p) j -> p ic j", p=128)), 16)
            sd.inc(e.dma_start(out=F2[0:16, 2704:2720], in_=W["L1_attn_A"].ap()), 16)
            sd.inc(e.dma_start(out=F2[0:1, 2944:3968],
                               in_=W["L1_attn_ln_w"].ap().rearrange("(a b) -> a b", a=1)), 16)
            sd.inc(e.dma_start(out=F2[0:1, 3100:3100] if False else F2[0:2, 3100:3100],
                               in_=W["L1_attn_ln_w"].ap().rearrange("(a b) -> a b", a=1)), 16) \
                if False else None
            sd.inc(e.dma_start(out=SM[0:1, 0:1024],
                               in_=W["L1_attn_ln_b"].ap().rearrange("(a b) -> a b", a=1)), 16)

        row_to_bcast(c, blk, SM[0:1, 0:1024], SM[:, 1024:2048], 1024,
                     waits=[(sd, sd.n)])
        row_to_bcast(c, blk, F2[0:1, 2944:3968], SM[:, 0:1024], 1024,
                     waits=[(sd, sd.n)])

        @blk.tensor
        def _(e):
            e.wait_ge(sg.h, sg.n)
            for ic in range(12):
                inst = e.transpose(pbb[:, 2048 + ic * 16:2048 + (ic + 1) * 16],
                                   A2[0:16, 14016 + ic * 128:14016 + (ic + 1) * 128],
                                   c.CIDENT[0:16, 0:16])
            sp.inc(inst)

        @blk.vector
        def _(e):
            e.wait_ge(sp.h, sp.n)
            sv.inc(e.tensor_copy(A2[:, 10240:10432], pbb[:, 2048:2240]))

    with nc.Block() as blk:
        for t in range(NT + 1):
            slot = t % 2
            pslot = (t - 1) % 2
            xt = F1[:, slot * 1024:(slot + 1) * 1024]
            xn1f = F1[:, 2048 + slot * 1024:3072 + slot * 1024]
            xn1p = F1[:, 2048 + pslot * 1024:3072 + pslot * 1024]

            if t < NT:
                @blk.sync
                def _(e, t=t, xt=xt, slot=slot):
                    e.wait_ge(ss[slot].h, ss[slot].n)
                    sl[slot].inc(e.dma_start(out=xt,
                                             in_=c.d["d_x1"].ap()[t * 128:(t + 1) * 128, :]), 16)

            jobs = []
            if t < NT:
                jobs.append(dict(xin=xt, outs=[xn1f], stats=F2[:, 2790:2794],
                                 scratch=F1[:, 4096:5120],
                                 waits=[(sl[slot], sl[slot].n), (ss[slot], ss[slot].n)]))
            if t >= 1:
                jobs.append(dict(xin=xn1p, outs=[F1b[:, 10240:11264]],
                                 stats=F2[:, 2794:2798], scratch=F1[:, 6144:7168],
                                 gam=SM[:, 0:1024], bet=SM[:, 1024:2048]))
            ln_jobs(c, blk, jobs)

            if t < NT:
                @blk.gpsimd
                def _(e, t=t, slot=slot, xn1f=xn1f):
                    e.wait_ge(sv.h, sv.n)
                    ss[slot].inc(e.dma_start(out=c.d["d_xn1"].ap()[t * 128:(t + 1) * 128, :],
                                             in_=xn1f), 16)

            if t >= 1:
                transpose_to_T(c, blk, F1b[:, 10240:11264], A1, (t - 1) * 128,
                               8, psum_bank=pslot)

        @blk.gpsimd
        def _(e):
            e.wait_ge(ss[0].h, ss[0].n)
            e.wait_ge(ss[1].h, ss[1].n)

    def sg_slot(gc):
        if gc < 8:
            return (A3, gc * 2048)
        return (A2, 2048 + (gc - 8) * 2048)

    with nc.Block() as blk:
        last_read_mark = [None]
        for ic in range(24):
            data = ic < 12

            @blk.tensor
            def _(e, ic=ic):
                if last_read_mark[0] is not None:
                    eng, mark = last_read_mark[0]
                    e.wait_ge(eng.h, mark)
                src = A3 if ic < 12 else A4
                ii = ic % 12
                for j in range(4):
                    for dc in range(8):
                        sp.inc(e.matmul(
                            PS_A[:, j * 512:(j + 1) * 512],
                            src[:, dc * 1536 + ii * 128:dc * 1536 + (ii + 1) * 128],
                            A1[:, dc * 2048 + j * 512:dc * 2048 + (j + 1) * 512],
                            start=(dc == 0), stop=(dc == 7)))

            if data:
                @blk.vector
                def _(e, ic=ic):
                    e.wait_ge(sp.h, sp.n)
                    wj = lambda j: F2[:, 2752 + ic * 3 + j:2752 + ic * 3 + j + 1]
                    cv = F1[:, 4096:6144]
                    sv.inc(e.tensor_scalar(cv, PS_A[:, 0:2048], wj(2), None, ALU.mult))
                    e.wait_ge(sv.h, sv.n)
                    sv.inc(e.scalar_tensor_tensor(cv[:, 1:2048], PS_A[:, 0:2047], wj(1),
                                                  cv[:, 1:2048], ALU.mult, ALU.add))
                    e.wait_ge(sv.h, sv.n)
                    sv.inc(e.scalar_tensor_tensor(cv[:, 2:2048], PS_A[:, 0:2046], wj(0),
                                                  cv[:, 2:2048], ALU.mult, ALU.add))
                    last_read_mark[0] = (sv, sv.n)

                @blk.scalar
                def _(e):
                    e.wait_ge(sv.h, sv.n)
                    sa.inc(e.activation(A2[:, 0:2048], F1[:, 4096:6144], AF.Silu))

                @blk.tensor
                def _(e, ic=ic):
                    e.wait_ge(sa.h, sa.n)
                    for j in range(4):
                        sp.inc(e.matmul(
                            PS_B[0:16, j * 512:(j + 1) * 512],
                            A2[:, 10240 + ic * 16:10240 + (ic + 1) * 16],
                            A2[:, j * 512:(j + 1) * 512],
                            start=(ic == 0), stop=(ic == 11),
                            skip_group_check=True))
            else:
                @blk.scalar
                def _(e, ic=ic):
                    e.wait_ge(sp.h, sp.n)
                    gc = ic - 12
                    buf, off = sg_slot(gc)
                    sa.inc(e.activation(buf[:, off:off + 2048], PS_A[:, 0:2048],
                                        AF.Sigmoid))
                    last_read_mark[0] = (sa, sa.n)

            if ic == 11:
                @blk.vector
                def _(e):
                    e.wait_ge(sp.h, sp.n)
                    sv.inc(e.tensor_copy(F2[0:16, 0:2048], PS_B[0:16, 0:2048]))

    emit_scan(c)

    with nc.Block() as blk:
        @blk.gpsimd
        def _(e):
            sg.inc(e.dma_start(
                out=A4.ap()[:, 0:12288].rearrange("p (ic o) -> p ic o", ic=12),
                in_=W["L1_attn_W_out"].ap().rearrange("(ic p) o -> p ic o", p=128)), 16)
            sg.inc(e.dma_start(out=A2.ap()[0:16, 14016:14208],
                               in_=W["L1_attn_Cm"].ap().rearrange("(ic p) s -> p (ic s)", p=128)
                               if False else W["L1_attn_Cm"].ap()
                               .rearrange("(a b) s -> a (b s)", a=16) if False else
                               W["L1_attn_Cm"].ap()), 16) if False else None
            sg.inc(e.dma_start(
                out=A2.ap()[:, 15552:15744].rearrange("p (ic s) -> p ic s", ic=12),
                in_=W["L1_attn_Cm"].ap().rearrange("(ic p) s -> p ic s", p=128)), 16)

        @blk.tensor
        def _(e):
            e.wait_ge(sg.h, sg.n)
            for ic in range(12):
                inst = e.transpose(pbb[0:16, 1536 + ic * 128:1536 + (ic + 1) * 128],
                                   A2[:, 15552 + ic * 16:15552 + (ic + 1) * 16],
                                   c.CIDENT[:])
            sp.inc(inst)

        @blk.vector
        def _(e):
            e.wait_ge(sp.h, sp.n)
            sv.inc(e.tensor_copy(A2[0:16, 10432:11968], pbb[0:16, 1536:3072]))
            e.wait_ge(sv.h, sv.n)
            sv.inc(e.tensor_copy(A2[0:16, 11968:14016], F2[0:16, 0:2048]))

        def g_slot(gc):
            if gc == 0:
                return (A2, 0)
            if gc <= 8:
                return (A3, (gc - 1) * 2048)
            return (A2, 2048 + (gc - 9) * 2048)

        gmark = [None]
        for gc in range(12):
            @blk.tensor
            def _(e, gc=gc):
                e.wait_ge(sv.h, sv.n)
                if gmark[0] is not None:
                    e.wait_ge(sv.h, gmark[0])
                for j in range(4):
                    sp.inc(e.matmul(PS_A[:, j * 512:(j + 1) * 512],
                                    A2[0:16, 10432 + gc * 128:10432 + (gc + 1) * 128],
                                    A2[0:16, 11968 + j * 512:11968 + (j + 1) * 512],
                                    start=True, stop=True))

            @blk.vector
            def _(e, gc=gc):
                e.wait_ge(sp.h, sp.n)
                sbuf, soff = sg_slot(gc)
                gbuf, goff = g_slot(gc)
                sv.inc(e.tensor_tensor(gbuf[:, goff:goff + 2048],
                                       sbuf[:, soff:soff + 2048],
                                       PS_A[:, 0:2048], ALU.mult))
                gmark[0] = sv.n

        for t in range(NT):
            slot = t % 2
            xt = F1[:, slot * 1024:(slot + 1) * 1024]

            @blk.sync
            def _(e, t=t, xt=xt, slot=slot):
                e.wait_ge(ss[slot].h, ss[slot].n)
                sl[slot].inc(e.dma_start(out=xt,
                                         in_=c.d["d_xn1"].ap()[t * 128:(t + 1) * 128, :]), 16)

            @blk.tensor
            def _(e, t=t):
                e.wait_ge(sv.h, sv.n)
                ps = PS_B[:, (t % 2) * 1024:(t % 2) * 1024 + 1024]
                for ns in range(2):
                    for ic in range(12):
                        gbuf, goff = g_slot(ic)
                        sp.inc(e.matmul(
                            ps[:, ns * 512:(ns + 1) * 512],
                            gbuf[:, goff + t * 128:goff + (t + 1) * 128],
                            A4[:, ic * 1024 + ns * 512:ic * 1024 + (ns + 1) * 512],
                            start=(ic == 0), stop=(ic == 11)))

            @blk.vector
            def _(e, t=t, xt=xt, slot=slot):
                e.wait_ge(sp.h, sp.n)
                e.wait_ge(sl[slot].h, sl[slot].n)
                ps = PS_B[:, (t % 2) * 1024:(t % 2) * 1024 + 1024]
                sv.inc(e.scalar_tensor_tensor(xt, ps, 1.0, xt, ALU.mult, ALU.add))

            @blk.gpsimd
            def _(e, t=t, xt=xt, slot=slot):
                e.wait_ge(sv.h, sv.n)
                ss[slot].inc(e.dma_start(out=c.d["d_attn1"].ap()[t * 128:(t + 1) * 128, :],
                                         in_=xt), 16)

        @blk.gpsimd
        def _(e):
            e.wait_ge(ss[0].h, ss[0].n)
            e.wait_ge(ss[1].h, ss[1].n)


def emit_scan(c):
    nc = c.nc
    sg, sp, sv = (c.sem(n) for n in ["sg", "sp", "sv"])
    F1, F2 = c.F1, c.F2
    PS_A, PS_B = c.PS_A, c.PS_B
    IDF16 = c.CIDENTF[0:16, 0:16]

    with nc.Block() as blk:
        @blk.tensor
        def _(e):
            e.wait_ge(sv.h, sv.n)
            sp.inc(e.transpose(PS_B[0:16, 0:16], F2[0:16, 2704:2720], IDF16))

        @blk.vector
        def _(e):
            e.wait_ge(sp.h, sp.n)
            sv.inc(e.tensor_tensor(F2[0:16, 2064:2080], PS_B[0:16, 0:16],
                                   c.CIDENTF[0:16, 0:16], ALU.add))
            e.wait_ge(sv.h, sv.n)
            sv.inc(e.tensor_copy(F2[0:16, 2048:2064], c.CIDENTF[0:16, 0:16]))

        @blk.tensor
        def _(e):
            e.wait_ge(sv.h, sv.n)
            sp.inc(e.transpose(PS_B[0:16, 16:32], F2[0:16, 2064:2080], IDF16))

        @blk.vector
        def _(e):
            e.wait_ge(sp.h, sp.n)
            sv.inc(e.tensor_copy(F2[0:16, 2720:2736], PS_B[0:16, 16:32]))

        for j in range(2, 17):
            dst = (2048 + 16 * j) if j <= 15 else 2304 + 15 * 16
            @blk.tensor
            def _(e, j=j):
                e.wait_ge(sv.h, sv.n)
                sp.inc(e.matmul(PS_B[0:16, 32:48], F2[0:16, 2720:2736],
                                F2[0:16, 2048 + 16 * (j - 1):2048 + 16 * j],
                                start=True, stop=True))

            @blk.vector
            def _(e, dst=dst):
                e.wait_ge(sp.h, sp.n)
                sv.inc(e.tensor_copy(F2[0:16, dst:dst + 16], PS_B[0:16, 32:48]))

        @blk.vector
        def _(e):
            e.wait_ge(sv.h, sv.n)
            sv.inc(e.tensor_copy(F2[0:16, 2304:2544], F2[0:16, 2064:2304]))

        @blk.tensor
        def _(e):
            e.wait_ge(sv.h, sv.n)
            sp.inc(e.transpose(PS_B[0:16, 48:64], F2[0:16, 2544:2560], IDF16))

        @blk.vector
        def _(e):
            e.wait_ge(sp.h, sp.n)
            sv.inc(e.tensor_copy(F2[0:16, 2736:2752], PS_B[0:16, 48:64]))
            e.wait_ge(sv.h, sv.n)
            e.tensor_copy(F2[0:16, 2560:2576], c.CIDENTF[0:16, 0:16])
            sv.inc(e.tensor_copy(F2[0:16, 2576:2592], F2[0:16, 2544:2560]))
        for j in range(2, 9):
            dst = (2560 + 16 * j) if j <= 7 else 2688
            @blk.tensor
            def _(e, j=j):
                e.wait_ge(sv.h, sv.n)
                sp.inc(e.matmul(PS_B[0:16, 32:48], F2[0:16, 2736:2752],
                                F2[0:16, 2560 + 16 * (j - 1):2560 + 16 * j],
                                start=True, stop=True))

            @blk.vector
            def _(e, dst=dst):
                e.wait_ge(sp.h, sp.n)
                sv.inc(e.tensor_copy(F2[0:16, dst:dst + 16], PS_B[0:16, 32:48]))

        @blk.vector
        def _(e):
            e.wait_ge(sv.h, sv.n)
            sv.inc(e.memset(F1[:, 4096:5008], 0.0))

        @blk.gpsimd
        def _(e):
            e.wait_ge(sv.h, sv.n)
            for k in range(2):
                for il in range(8):
                    i = 8 * k + il
                    w = (16 - i) * 16
                    sg.inc(e.dma_start(
                        out=F1.ap()[16 * il:16 * (il + 1),
                                    4096 + k * 256 + i * 16:4096 + k * 256 + 256],
                        in_=F2.ap()[0:16, 2048:2048 + w]), 16)
            for il in range(8):
                w = (8 - il) * 16
                sg.inc(e.dma_start(
                    out=F1.ap()[16 * il:16 * (il + 1),
                                4864 + il * 16:4864 + 128],
                    in_=F2.ap()[0:16, 2560:2560 + w]), 16)
            for k in range(2):
                for il in range(8):
                    sg.inc(e.dma_start(
                        out=F1.ap()[16 * il:16 * (il + 1),
                                    4608 + k * 128:4608 + k * 128 + 128],
                        in_=bass.AP(c.F2, 8 * k + il, [[4096, 16], [16, 128]]),
                        allow_slow_non_contiguous=True), 16)

        @blk.tensor
        def _(e):
            e.wait_ge(sg.h, sg.n)
            e.matmul(PS_A[:, 0:256], F1[:, 4608:4736], F1[:, 4096:4352],
                     start=True, stop=False)
            sp.inc(e.matmul(PS_A[:, 0:256], F1[:, 4736:4864], F1[:, 4352:4608],
                            start=False, stop=True))

        @blk.vector
        def _(e):
            e.wait_ge(sp.h, sp.n)
            sv.inc(e.tensor_copy(F1[:, 5008:5024], PS_A[:, 240:256]))

        @blk.tensor
        def _(e):
            e.wait_ge(sv.h, sv.n)
            sp.inc(e.transpose(PS_B[0:16, 64:192], F1[:, 5008:5024], c.CIDENTF[:]))

        @blk.vector
        def _(e):
            e.wait_ge(sp.h, sp.n)
            sv.inc(e.tensor_copy(F1[0:16, 5024:5152], PS_B[0:16, 64:192]))

        @blk.gpsimd
        def _(e):
            e.wait_ge(sv.h, sv.n)
            for il in range(8):
                sg.inc(e.dma_start(
                    out=F1.ap()[16 * il:16 * (il + 1), 4992:5008],
                    in_=bass.AP(c.F1, 5024 + il, [[8192, 16], [8, 16]]),
                    allow_slow_non_contiguous=True), 16)

        @blk.tensor
        def _(e):
            e.wait_ge(sg.h, sg.n)
            sp.inc(e.matmul(PS_B[0:16, 512:640], F1[:, 4992:5008], F1[:, 4864:4992],
                            start=True, stop=True))

        @blk.vector
        def _(e):
            e.wait_ge(sp.h, sp.n)
            sv.inc(e.tensor_copy(F1[0:16, 5168:5296], PS_B[0:16, 512:640]))

        @blk.tensor
        def _(e):
            e.wait_ge(sv.h, sv.n)
            sp.inc(e.transpose(PS_B[0:16, 640:656], F1[0:16, 5280:5296], IDF16))

        @blk.vector
        def _(e):
            e.wait_ge(sp.h, sp.n)
            sv.inc(e.tensor_copy(F1[0:16, 5296:5312], PS_B[0:16, 640:656]))
            e.wait_ge(sv.h, sv.n)
            sv.inc(e.memset(F1[0:16, 5312:5313], 0.0))

        for C in range(1, 16):
            @blk.tensor
            def _(e, C=C):
                e.wait_ge(sv.h, sv.n)
                sp.inc(e.matmul(PS_B[0:16, 656:657], F2[0:16, 2688:2704],
                                F1[0:16, 5312 + C - 1:5312 + C], start=True, stop=True))

            @blk.vector
            def _(e, C=C):
                e.wait_ge(sp.h, sp.n)
                sv.inc(e.tensor_tensor(F1[0:16, 5312 + C:5313 + C],
                                       PS_B[0:16, 656:657],
                                       F1[0:16, 5296 + C - 1:5297 + C - 1] if False else
                                       F1[0:16, 5296 + (C - 1):5296 + C], ALU.add))

        for I in range(8):
            @blk.tensor
            def _(e, I=I):
                e.wait_ge(sv.h, sv.n)
                sp.inc(e.matmul(PS_B[0:16, 672:688], F2[0:16, 2560 + 16 * I:2576 + 16 * I],
                                F1[0:16, 5312:5328], start=True, stop=True))

            @blk.vector
            def _(e, I=I):
                e.wait_ge(sp.h, sp.n)
                dst = bass.AP(c.F1, 5328 + I, [[8192, 16], [8, 16]])
                sv.inc(e.tensor_copy(dst, PS_B[0:16, 672:688]))
        for I in range(1, 8):
            @blk.tensor
            def _(e, I=I):
                e.wait_ge(sv.h, sv.n)
                sp.inc(e.transpose(PS_B[0:16, 688:704],
                                   F1[0:16, 5168 + (I - 1) * 16:5168 + I * 16], IDF16))

            @blk.vector
            def _(e, I=I):
                e.wait_ge(sp.h, sp.n)
                dst = bass.AP(c.F1, 5328 + I, [[8192, 16], [8, 16]])
                sv.inc(e.tensor_tensor(dst, dst, PS_B[0:16, 688:704], ALU.add))

        @blk.tensor
        def _(e):
            e.wait_ge(sv.h, sv.n)
            sp.inc(e.matmul(PS_A[:, 0:256], F1[0:16, 5328:5456], F2[0:16, 2304:2560],
                            start=False, stop=True, skip_group_check=True))

        @blk.vector
        def _(e):
            e.wait_ge(sp.h, sp.n)
            sv.inc(e.tensor_copy(F1[:, 5456:5712], PS_A[:, 0:256]))
        for j in range(16):
            @blk.tensor
            def _(e, j=j):
                e.wait_ge(sv.h, sv.n)
                sp.inc(e.transpose(PS_B[0:16, 704:832],
                                   F1[:, 5456 + j * 16:5456 + (j + 1) * 16], c.CIDENTF[:]))

            @blk.vector
            def _(e, j=j):
                e.wait_ge(sp.h, sp.n)
                dst = bass.AP(c.F2, j, [[4096, 16], [16, 128]])
                sv.inc(e.tensor_copy(dst, PS_B[0:16, 704:832]))


def rms_tile(c, blk, xin, out, wbcast, stats, scratch, eps_col=41, waits=()):
    sv, sa = c.sem("sv"), c.sem("sa")
    var, rstd = stats[:, 0:1], stats[:, 1:2]
    sv_entry = sv.n

    @blk.vector
    def _(e):
        for (s, v) in waits:
            e.wait_ge(s.h, v)
        e.wait_ge(sv.h, sv_entry)
        sv.inc(e.tensor_tensor(scratch, xin, xin, ALU.mult))
        e.wait_ge(sv.h, sv.n)
        sv.inc(e.reduce_sum(var, scratch, AX.X))
        e.wait_ge(sv.h, sv.n)
        sv.inc(e.tensor_scalar(var, var, 1.0 / 1024.0, None, ALU.mult))

    @blk.scalar
    def _(e):
        e.wait_ge(sv.h, sv.n)
        sa.inc(e.activation(stats[:, 2:3], var, AF.Ln, bias=c.CMISC[:, eps_col:eps_col + 1]))
        e.wait_ge(sa.h, sa.n)
        sa.inc(e.activation(rstd, stats[:, 2:3], AF.Exp, scale=-0.5))

    @blk.vector
    def _(e):
        e.wait_ge(sa.h, sa.n)
        sv.inc(e.tensor_scalar(out, xin, rstd, None, ALU.mult))
        e.wait_ge(sv.h, sv.n)
        sv.inc(e.tensor_tensor(out, out, wbcast, ALU.mult))


def emit_gqa(c):
    nc = c.nc
    sd, sg, sp, sv, sa = (c.sem(n) for n in ["sd", "sg", "sp", "sv", "sa"])
    sl = [c.sem("sl0"), c.sem("sl1")]
    ss = [c.sem("ss0"), c.sem("ss1")]
    W = c.d
    A1, A2, A3, A4, A5 = c.A1, c.A2, c.A3, c.A4, c.A5
    F1, F2, SM = c.F1, c.F2, c.SM
    F1b = c.F1.ap().bitcast(BF16)
    PS_A, PS_B = c.PS_A, c.PS_B

    with nc.Block() as blk:
        @blk.gpsimd
        def _(e):
            sg.inc(e.dma_start(
                out=A2.ap()[:, 0:8192].rearrange("p (dc ee) -> p dc ee", dc=8),
                in_=W["L1_ffn_qw"].ap().rearrange("(dc p) ee -> p dc ee", p=128)), 16)
            sg.inc(e.dma_start(
                out=A2.ap()[:, 8192:16384].rearrange("p (dc ee) -> p dc ee", dc=8),
                in_=W["L1_ffn_ow"].ap().rearrange("(dc p) ee -> p dc ee", p=128)), 16)
            sg.inc(e.dma_start(
                out=A5.ap()[:, 0:2048].rearrange("p (dc ee) -> p dc ee", dc=8),
                in_=W["L1_ffn_kw"].ap().rearrange("(dc p) ee -> p dc ee", p=128)), 16)
            sg.inc(e.dma_start(
                out=A5.ap()[:, 2048:4096].rearrange("p (dc ee) -> p dc ee", dc=8),
                in_=W["L1_ffn_vw"].ap().rearrange("(dc p) ee -> p dc ee", p=128)), 16)

        @blk.sync
        def _(e):
            sd.inc(e.dma_start(out=bass.AP(c.SM, 1024, [[2048, 8], [1, 128]]),
                               in_=W["L1_ffn_qb"].ap().rearrange("(cc p) -> cc p", p=128)), 16)
            sd.inc(e.dma_start(out=bass.AP(c.SM, 1152, [[2048, 2], [1, 128]]),
                               in_=W["L1_ffn_kb"].ap().rearrange("(cc p) -> cc p", p=128)), 16)
            sd.inc(e.dma_start(out=SM[0:1, 0:256],
                               in_=W["L1_ffn_vb"].ap().rearrange("(a b) -> a b", a=1)), 16)
            sd.inc(e.dma_start(out=F2[0:1, 2944:3968],
                               in_=W["L1_ffn_norm1_w"].ap().rearrange("(a b) -> a b", a=1)), 16)

        @blk.tensor
        def _(e):
            e.wait_ge(sd.h, sd.n)
            e.wait_ge(sv.h, sv.n)
            e.transpose(PS_B[:, 0:8], bass.AP(c.SM, 1024, [[2048, 8], [1, 128]]),
                        c.CIDENTF[0:8, 0:8])
            sp.inc(e.transpose(PS_B[:, 8:10], bass.AP(c.SM, 1152, [[2048, 2], [1, 128]]),
                               c.CIDENTF[0:2, 0:2]))

        @blk.vector
        def _(e):
            e.wait_ge(sp.h, sp.n)
            e.tensor_copy(F2[:, 3970:3978], PS_B[:, 0:8])
            sv.inc(e.tensor_copy(F2[:, 3978:3980], PS_B[:, 8:10]))

        row_to_bcast(c, blk, F2[0:1, 2944:3968], SM[:, 1024:2048], 1024,
                     waits=[(sd, sd.n)])

    if GQA_MAX_PHASE < 1:
        return
    with nc.Block() as blk:
        for t in range(NT):
            slot = t % 2
            xt = F1[:, slot * 1024:(slot + 1) * 1024]

            @blk.sync
            def _(e, t=t, xt=xt, slot=slot):
                if t >= 2:
                    e.wait_ge(sv.h, sv.n)
                sl[slot].inc(e.dma_start(out=xt,
                                         in_=c.d["d_xn1"].ap()[t * 128:(t + 1) * 128, :]), 16)

            @blk.vector
            def _(e, xt=xt, slot=slot):
                e.wait_ge(sl[slot].h, sl[slot].n)
                e.wait_ge(sv.h, sv.n)
                sv.inc(e.tensor_copy(F1b[:, 10240:11264], xt))

            transpose_to_T(c, blk, F1b[:, 10240:11264], A1, t * 128, 8, psum_bank=t % 2)

    if GQA_MAX_PHASE < 2:
        return
    emit_qkv(c, xnT=A1.ap(), wq=(A2, 0), wk=(A5, 0), wv=(A5, 2048),
             QT=A4.ap(), KT=F1b[:, 0:4096], Vext=F1b[:, 8192:12352],
             bq_cols=F2[:, 3970:3978], bk_cols=F2[:, 3978:3980],
             bv_row=SM[0:1, 0:256], bvb=SM[:, 0:256], nk_e=256, nv=256,
             entry_waits=[(sg, sg.n)])

    with nc.Block() as blk:
        @blk.vector
        def _(e):
            for ch in range(2):
                e.tensor_copy(F1b[64:128, 4096 + ch * 2048:4096 + (ch + 1) * 2048],
                              F1b[0:64, ch * 2048:(ch + 1) * 2048])
                sv.inc(e.tensor_copy(F1b[0:64, 4096 + ch * 2048:4096 + (ch + 1) * 2048],
                                     F1b[64:128, ch * 2048:(ch + 1) * 2048]))

    if GQA_MAX_PHASE < 3:
        return
    emit_norm_bounds(c, QT=A4.ap(), KT=F1b[:, 0:4096], nq_heads=16, nkv=4,
                     kv_group=4, scale=0.125, sq_scratch=A5[:, 4096:6144],
                     negc_cols=F2[:, 3980:3996])
    emit_attention(c, QT=A4.ap(), KT=F1b[:, 0:4096], Vext=F1b[:, 8192:12352],
                   outT=A3.ap(), nq_heads=16, kv_group=4, scale=0.125,
                   scratch_bf=A5.ap(), negc_cols=F2[:, 3980:3996],
                   recip_bf=F2.ap().bitcast(BF16)[0:1, 0:2048],
                   recip_tmp=F2[0:1, 1024:3072], KTflip=F1b[:, 4096:8192])

    if GQA_MAX_PHASE < 5:
        return
    with nc.Block() as blk:
        @blk.sync
        def _(e):
            sd.inc(e.dma_start(out=SM[0:1, 0:1024],
                               in_=W["L1_ffn_ob"].ap().rearrange("(a b) -> a b", a=1)), 16)

        row_to_bcast(c, blk, SM[0:1, 0:1024], SM[:, 0:1024], 1024,
                     waits=[(sd, sd.n)])

        sb = [c.sem("sgg0"), c.sem("sgg1")]
        for t in range(NT):
            slot = t % 2
            xt = F1[:, slot * 1024:(slot + 1) * 1024]

            @blk.sync
            def _(e, t=t, xt=xt, slot=slot):
                e.wait_ge(ss[slot].h, ss[slot].n)
                sl[slot].inc(e.dma_start(out=xt,
                                         in_=c.d["d_xn1"].ap()[t * 128:(t + 1) * 128, :]), 16)

            @blk.tensor
            def _(e, t=t):
                e.wait_ge(sv.h, sv.n)
                ps = PS_A[:, (t % 2) * 1024:(t % 2) * 1024 + 1024]
                for ns in range(2):
                    for dc in range(8):
                        sp.inc(e.matmul(
                            ps[:, ns * 512:(ns + 1) * 512],
                            A3[:, dc * 2048 + t * 128:dc * 2048 + (t + 1) * 128],
                            A2[:, 8192 + dc * 1024 + ns * 512:8192 + dc * 1024 + (ns + 1) * 512],
                            start=(dc == 0), stop=(dc == 7)))

            @blk.vector
            def _(e, t=t, xt=xt, slot=slot):
                e.wait_ge(sp.h, sp.n)
                e.wait_ge(sl[slot].h, sl[slot].n)
                ps = PS_A[:, (t % 2) * 1024:(t % 2) * 1024 + 1024]
                sv.inc(e.scalar_tensor_tensor(xt, ps, 1.0, xt, ALU.mult, ALU.add))
                e.wait_ge(sv.h, sv.n)
                sv.inc(e.tensor_tensor(xt, xt, SM[:, 0:1024], ALU.add))

            rms_tile(c, blk, xt, xt, SM[:, 1024:2048], F2[:, 2790:2794],
                     F1[:, 4096:5120])

            @blk.vector
            def _(e, t=t, xt=xt, slot=slot):
                e.wait_ge(sv.h, sv.n)
                if t >= 2:
                    e.wait_ge(sb[slot].h, sb[slot].n)
                sv.inc(e.tensor_copy(F1.ap().bitcast(BF16)[:, 10240 + slot * 1024:11264 + slot * 1024], xt))

            @blk.gpsimd
            def _(e, t=t, xt=xt, slot=slot):
                e.wait_ge(sv.h, sv.n)
                sb[slot].inc(e.dma_start(out=c.d["d_srcb"].ap()[t * 128:(t + 1) * 128, :],
                                         in_=F1.ap().bitcast(BF16)[:, 10240 + slot * 1024:11264 + slot * 1024]), 16)
                ss[slot].inc(e.dma_start(out=c.d["d_src"].ap()[t * 128:(t + 1) * 128, :],
                                         in_=xt), 16)

        @blk.gpsimd
        def _(e):
            e.wait_ge(ss[0].h, ss[0].n)
            e.wait_ge(ss[1].h, ss[1].n)
            e.wait_ge(sb[0].h, sb[0].n)
            e.wait_ge(sb[1].h, sb[1].n)


def emit_moe(c):
    nc = c.nc
    sd, sg, sp, sv, sa = (c.sem(n) for n in ["sd", "sg", "sp", "sv", "sa"])
    sl = [c.sem("sl0"), c.sem("sl1")]
    ss = [c.sem("ss0"), c.sem("ss1")]
    sge = [c.sem("sge0"), c.sem("sge1")]
    scc = c.sem("scc")
    W = c.d
    A1, A2, A3, A4, A5 = c.A1, c.A2, c.A3, c.A4, c.A5
    F1, F2, SM = c.F1, c.F2, c.SM
    F1b = c.F1.ap().bitcast(BF16)
    F2b = c.F2.ap().bitcast(BF16)
    PS_A, PS_B = c.PS_A, c.PS_B
    SI = c.SI

    BIG = 8.0 * S_MAX

    with nc.Block() as blk:
        @blk.gpsimd
        def _(e):
            sg.inc(e.dma_start(
                out=A5.ap()[:, 7168:7232].rearrange("p (dc ee) -> p dc ee", dc=8),
                in_=W["L1_ffn_gate_w"].ap().rearrange("(dc p) ee -> p dc ee", p=128)), 16)

        @blk.sync
        def _(e):
            sd.inc(e.dma_start(out=F2[0:1, 640:1664],
                               in_=W["L1_ffn_norm2_w"].ap().rearrange("(a b) -> a b", a=1)), 16)
            sd.inc(e.dma_start(out=F2[0:NCORES, 2700:2701], in_=c.d["c_prefmask"].ap()), 16)

        row_to_bcast(c, blk, F2[0:1, 640:1664], SM[:, 1024:2048], 1024,
                     waits=[(sd, sd.n)])

        @blk.vector
        def _(e):
            e.wait_ge(sv.h, sv.n)
            e.memset(F1[:, 0:1024], 0.0)
            e.memset(F2[0:1, 400:408], 0.0)
            e.memset(F2[0:1, 424:432], 0.0)
            sv.inc(e.tensor_scalar(F2[:, 248:256], c.CMISC[:, 0:8], 999.0, None,
                                   ALU.subtract))
            e.wait_ge(sv.h, sv.n)
            sv.inc(e.memset(F2[:, 620:624], 0.0))
            e.wait_ge(sv.h, sv.n)
            sv.inc(e.memset(F2[:, 622:623], 2.0 * T))
            e.wait_ge(sv.h, sv.n)
            e.tensor_copy(bass.AP(F2, 1538, [[4096, 128], [4, 32], [0, 1]]),
                          bass.AP(c.CMISC, 8, [[48, 128], [1, 32], [0, 1]]))
            e.tensor_scalar(bass.AP(F2, 1536, [[4096, 128], [8, 16], [0, 1]]),
                            bass.AP(c.CMISC, 8, [[48, 128], [2, 16], [0, 1]]),
                            0.5, None, ALU.mult)
            e.tensor_scalar(bass.AP(F2, 1540, [[4096, 128], [8, 16], [0, 1]]),
                            bass.AP(c.CMISC, 9, [[48, 128], [2, 16], [0, 1]]),
                            0.5, -0.5, ALU.mult, ALU.add)
            sv.inc(e.memset(bass.AP(F2, 1539, [[4096, 128], [4, 32], [0, 1]]), 0.0))

        @blk.gpsimd
        def _(e):
            e.wait_ge(sv.h, sv.n)
            for i in range(32):
                sg.inc(e.dma_start(out=c.d["d_moeout"].ap()[i * 128:(i + 1) * 128, :],
                                   in_=F1[:, 0:1024]), 16)
            for i in range(8 * S_MAX // 128):
                sg.inc(e.dma_start(out=c.d["d_slotmeta"].ap()[i * 128:(i + 1) * 128, :],
                                   in_=F2[:, 620:624]), 16)

    with nc.Block() as blk:
        for t in range(NT):
            slot = t % 2
            xt = F1[:, 1024 + slot * 1024:2048 + slot * 1024]

            @blk.sync
            def _(e, t=t, xt=xt, slot=slot):
                if t >= 2:
                    e.wait_ge(sv.h, sv.n)
                sl[slot].inc(e.dma_start(out=xt,
                                         in_=c.d["d_src"].ap()[t * 128:(t + 1) * 128, :]), 16)

            @blk.vector
            def _(e, xt=xt, slot=slot):
                e.wait_ge(sl[slot].h, sl[slot].n)
                e.wait_ge(sv.h, sv.n)
                sv.inc(e.tensor_copy(F1b[:, 14336:15360], xt))

            transpose_to_T(c, blk, F1b[:, 14336:15360], A1, t * 128, 8, psum_bank=t % 2)

        @blk.tensor
        def _(e):
            e.wait_ge(sv.h, sv.n)
            e.wait_ge(sg.h, sg.n)
            for dc in range(8):
                for j in range(4):
                    sp.inc(e.matmul(PS_B[0:8, j * 512:(j + 1) * 512],
                                    A5[:, 7168 + dc * 8:7168 + (dc + 1) * 8],
                                    A1[:, dc * 2048 + j * 512:dc * 2048 + (j + 1) * 512],
                                    start=(dc == 0), stop=(dc == 7)))

        A5f = c.A5.ap().bitcast(F32)
        @blk.vector
        def _(e):
            e.wait_ge(sp.h, sp.n)
            sv.inc(e.tensor_copy(A5f[0:8, 0:2048], PS_B[0:8, 0:2048]))

        for tl in range(NT):
            @blk.tensor
            def _(e, tl=tl):
                e.wait_ge(sv.h, sv.n)
                sp.inc(e.transpose(PS_A[0:128, 8 * (tl % 2):8 * (tl % 2) + 8],
                                   A5f[0:8, tl * 128:(tl + 1) * 128],
                                   c.CIDENTF[0:8, 0:8]))

            @blk.vector
            def _(e, tl=tl):
                e.wait_ge(sp.h, sp.n)
                sv.inc(e.tensor_copy(F2[:, 8 * tl:8 * tl + 8],
                                     PS_A[:, 8 * (tl % 2):8 * (tl % 2) + 8]))

    def slab(e_idx, hh, which):
        buf = (A1 if which == 0 else A2) if e_idx % 2 == 0 else (A3 if which == 0 else A4)
        return buf, hh * 8192

    sgg = [c.sem("sgg0"), c.sem("sgg1")]
    sgs = [c.sem("sgs0"), c.sem("sgs1")]
    smm = [c.sem("smm0"), c.sem("smm1")]

    def emit_slab_loads(e, e_idx):
        pe_par = e_idx % 2
        for hh in range(2):
            b1, o1 = slab(e_idx, hh, 0)
            sge[pe_par].inc(e.dma_start(
                out=b1.ap()[:, o1:o1 + 8192].rearrange("p (dc h) -> p dc h", dc=8),
                in_=W["L1_ffn_w1"].ap()[e_idx].rearrange("(dc p) h -> p dc h", p=128)
                [:, :, hh * 1024:(hh + 1) * 1024]), 16)
            b2, o2 = slab(e_idx, hh, 1)
            sge[pe_par].inc(e.dma_start(
                out=b2.ap()[:, o2:o2 + 8192].rearrange("p (hc d) -> p hc d", hc=8),
                in_=W["L1_ffn_w2"].ap()[e_idx].rearrange("(hc p) d -> p hc d", p=128)
                [:, hh * 8:(hh + 1) * 8, :]), 16)

    def blk3(buf, w, off, n1=16, n2=8):
        return bass.AP(buf, off, [[w, 128], [n2, n1], [1, n2]])

    def exp3i(buf, w, off, n1=16, n2=8):
        return bass.AP(buf, off, [[w, 128], [1, n1], [0, n2]])

    def exp3o(buf, w, off, n1=16, n2=8):
        return bass.AP(buf, off, [[w, 128], [0, n1], [1, n2]])

    SMw, F2w = 2048, 4096
    probs3 = blk3(SM, 2048, 0)
    with nc.Block() as blk:
        @blk.gpsimd
        def _(e):
            emit_slab_loads(e, 0)

        @blk.scalar
        def _(e):
            e.wait_ge(sv.h, sv.n)
            sa.inc(e.activation(F2[:, 0:128], F2[:, 0:128], AF.Sigmoid))

        pr3 = blk3(F2, F2w, 0)
        t83 = blk3(F2, F2w, 1024)
        io3 = exp3o(F2, F2w, 248)
        i83 = exp3o(c.CMISC, 48, 0)

        @blk.vector
        def _(e):
            e.wait_ge(sa.h, sa.n)
            sv.inc(e.tensor_reduce(blk3(SM, SMw, 256, 16, 1), pr3, AX.X, ALU.max))
            e.wait_ge(sv.h, sv.n)
            sv.inc(e.tensor_tensor(t83, pr3, exp3i(SM, SMw, 256), ALU.is_ge))
            e.wait_ge(sv.h, sv.n)
            sv.inc(e.tensor_tensor(t83, t83, io3, ALU.mult))
            e.wait_ge(sv.h, sv.n)
            sv.inc(e.tensor_scalar(F2[:, 1024:1152], F2[:, 1024:1152], 999.0,
                                   None, ALU.add))
            e.wait_ge(sv.h, sv.n)
            sv.inc(e.tensor_reduce(blk3(SM, SMw, 320, 16, 1), t83, AX.X, ALU.min))
            e.wait_ge(sv.h, sv.n)
            sv.inc(e.tensor_tensor(blk3(SM, SMw, 0), i83,
                                   exp3i(SM, SMw, 320), ALU.is_equal))
            e.wait_ge(sv.h, sv.n)
            sv.inc(e.scalar_tensor_tensor(F2[:, 1024:1152], SM[:, 0:128], -1e9,
                                          F2[:, 0:128], ALU.mult, ALU.add))
            e.wait_ge(sv.h, sv.n)
            sv.inc(e.tensor_reduce(blk3(SM, SMw, 272, 16, 1), t83, AX.X, ALU.max))
            e.wait_ge(sv.h, sv.n)
            sv.inc(e.tensor_tensor(t83, t83, exp3i(SM, SMw, 272), ALU.is_ge))
            e.wait_ge(sv.h, sv.n)
            sv.inc(e.tensor_tensor(t83, t83, io3, ALU.mult))
            e.wait_ge(sv.h, sv.n)
            sv.inc(e.tensor_scalar(F2[:, 1024:1152], F2[:, 1024:1152], 999.0,
                                   None, ALU.add))
            e.wait_ge(sv.h, sv.n)
            sv.inc(e.tensor_reduce(blk3(SM, SMw, 336, 16, 1), t83, AX.X, ALU.min))
            e.wait_ge(sv.h, sv.n)
            sv.inc(e.tensor_tensor(blk3(SM, SMw, 128), i83,
                                   exp3i(SM, SMw, 336), ALU.is_equal))
            e.wait_ge(sv.h, sv.n)
            sv.inc(e.tensor_tensor(SM[:, 288:304], SM[:, 256:272],
                                   SM[:, 272:288], ALU.add))
            e.wait_ge(sv.h, sv.n)
            sv.inc(e.tensor_scalar(SM[:, 288:304], SM[:, 288:304], 1e-6,
                                   None, ALU.add))
            e.wait_ge(sv.h, sv.n)
            sv.inc(e.reciprocal(SM[:, 304:320], SM[:, 288:304]))
            e.wait_ge(sv.h, sv.n)
            sv.inc(e.tensor_tensor(SM[:, 288:304], SM[:, 256:272],
                                   SM[:, 304:320], ALU.mult))
            e.wait_ge(sv.h, sv.n)
            sv.inc(e.tensor_tensor(SM[:, 304:320], SM[:, 272:288],
                                   SM[:, 304:320], ALU.mult))
            e.wait_ge(sv.h, sv.n)
            sv.inc(e.tensor_tensor(SM[:, 384:512], SM[:, 0:128],
                                   SM[:, 128:256], ALU.add))

        @blk.tensor
        def _(e):
            e.wait_ge(sv.h, sv.n)
            sp.inc(e.matmul(PS_A[0:1, 128:256], c.CMISC[:, 42:43],
                            SM[:, 384:512], start=True, stop=True))

        @blk.vector
        def _(e):
            e.wait_ge(sp.h, sp.n)
            sv.inc(e.tensor_copy(F2[0:1, 440:568], PS_A[0:1, 128:256]))
            e.wait_ge(sv.h, sv.n)
            sv.inc(e.tensor_reduce(bass.AP(F2, 400, [[F2w, 1], [1, 8], [0, 1]]),
                                   bass.AP(F2, 440, [[F2w, 1], [1, 8], [8, 16]]),
                                   AX.X, ALU.add))

    with nc.Block() as blk:
        @blk.gpsimd
        def _(e):
            e.wait_ge(c.sem("sv").h, c.sem("sv").n)
            sg.inc(e.dma_start(out=c.d["d_cin"].ap(), in_=F2[0:1, 400:408]), 16)
            e.wait_ge(sg.h, sg.n)
            e.collective_compute(
                "AllGather", ALU.bypass, replica_groups=[list(range(NCORES))],
                ins=[c.d["d_cin"].ap().opt()], outs=[c.d["d_cout"].ap().opt()],
            ).then_inc(scc.h, 1)
            scc.n += 1
            e.wait_ge(scc.h, scc.n)
            sg.inc(e.dma_start(out=F2[0:NCORES, 408:416], in_=c.d["d_cout"].ap()), 16)

        @blk.tensor
        def _(e):
            e.wait_ge(sg.h, sg.n)
            e.wait_ge(sd.h, sd.n)
            sp.inc(e.matmul(PS_A[0:1, 24:32], F2[0:NCORES, 2700:2701],
                            F2[0:NCORES, 408:416], start=True, stop=True))

        @blk.vector
        def _(e):
            e.wait_ge(sp.h, sp.n)
            sv.inc(e.tensor_copy(F2[0:1, 416:424], PS_A[0:1, 24:32]))

        @blk.tensor
        def _(e):
            e.wait_ge(sv.h, sv.n)
            sp.inc(e.matmul(PS_A[:, 32:40], c.CONES[0:1, 0:128], F2[0:1, 416:424],
                            start=True, stop=True))

        @blk.vector
        def _(e):
            e.wait_ge(sp.h, sp.n)
            sv.inc(e.tensor_copy(F2[:, 432:440], PS_A[:, 32:40]))

    with nc.Block() as blk:
        @blk.tensor
        def _(e):
            sp.inc(e.transpose(PS_A[:, 384:385], F2[0:1, 440:568],
                               c.CIDENTF[0:1, 0:1]))

        @blk.vector
        def _(e):
            e.wait_ge(sp.h, sp.n)
            sv.inc(e.tensor_copy(F2[:, 1408:1409], PS_A[:, 384:385]))

        @blk.tensor
        def _(e):
            e.wait_ge(sv.h, sv.n)
            sp.inc(e.matmul(PS_A[0:1, 256:384], F2[:, 1408:1409], c.CSU8[:],
                            start=True, stop=True))

        @blk.vector
        def _(e):
            e.wait_ge(sp.h, sp.n)
            sv.inc(e.tensor_copy(F2[0:1, 1152:1280], PS_A[0:1, 256:384]))

        @blk.tensor
        def _(e):
            e.wait_ge(sv.h, sv.n)
            e.matmul(PS_A[:, 0:128], c.CSU[:], SM[:, 384:512],
                     start=True, stop=False, skip_group_check=True)
            sp.inc(e.matmul(PS_A[:, 0:128], c.CONES[0:1, 0:128],
                            F2[0:1, 1152:1280], start=False, stop=True,
                            skip_group_check=True))

        t8f = F2[:, 1024:1152]
        cpb3 = exp3o(F2, F2w, 432)

        @blk.vector
        def _(e):
            e.wait_ge(sp.h, sp.n)
            sv.inc(e.tensor_tensor(t8f, PS_A[:, 0:128], SM[:, 0:128], ALU.mult))
            e.wait_ge(sv.h, sv.n)
            sv.inc(e.tensor_reduce(blk3(SM, SMw, 512, 16, 1),
                                   blk3(F2, F2w, 1024), AX.X, ALU.add))
            e.wait_ge(sv.h, sv.n)
            sv.inc(e.tensor_tensor(t8f, PS_A[:, 0:128], SM[:, 128:256], ALU.mult))
            e.wait_ge(sv.h, sv.n)
            sv.inc(e.tensor_reduce(blk3(SM, SMw, 528, 16, 1),
                                   blk3(F2, F2w, 1024), AX.X, ALU.add))
            e.wait_ge(sv.h, sv.n)
            sv.inc(e.tensor_tensor(blk3(F2, F2w, 1024), cpb3,
                                   blk3(SM, SMw, 0), ALU.mult))
            e.wait_ge(sv.h, sv.n)
            sv.inc(e.tensor_reduce(blk3(SM, SMw, 544, 16, 1),
                                   blk3(F2, F2w, 1024), AX.X, ALU.add))
            e.wait_ge(sv.h, sv.n)
            sv.inc(e.tensor_tensor(blk3(F2, F2w, 1024), cpb3,
                                   blk3(SM, SMw, 128), ALU.mult))
            e.wait_ge(sv.h, sv.n)
            sv.inc(e.tensor_reduce(blk3(SM, SMw, 560, 16, 1),
                                   blk3(F2, F2w, 1024), AX.X, ALU.add))
            e.wait_ge(sv.h, sv.n)
            e.tensor_tensor(SM[:, 576:592], SM[:, 512:528], SM[:, 544:560],
                            ALU.add)
            sv.inc(e.tensor_tensor(SM[:, 592:608], SM[:, 528:544],
                                   SM[:, 560:576], ALU.add))
            e.wait_ge(sv.h, sv.n)
            sv.inc(e.tensor_scalar(SM[:, 608:624], SM[:, 576:592],
                                   float(CAPACITY), None, ALU.is_lt))
            e.wait_ge(sv.h, sv.n)
            sv.inc(e.tensor_tensor(SM[:, 352:368], SM[:, 288:304],
                                   SM[:, 608:624], ALU.mult))
            e.wait_ge(sv.h, sv.n)
            sv.inc(e.tensor_scalar(SM[:, 608:624], SM[:, 592:608],
                                   float(CAPACITY), None, ALU.is_lt))
            e.wait_ge(sv.h, sv.n)
            sv.inc(e.tensor_tensor(SM[:, 368:384], SM[:, 304:320],
                                   SM[:, 608:624], ALU.mult))
            e.wait_ge(sv.h, sv.n)
            BIG = 8.0 * S_MAX
            e.scalar_tensor_tensor(SM[:, 624:640], SM[:, 320:336],
                                   float(S_MAX), SM[:, 512:528],
                                   ALU.mult, ALU.add)
            sv.inc(e.scalar_tensor_tensor(SM[:, 640:656], SM[:, 336:352],
                                          float(S_MAX), SM[:, 528:544],
                                          ALU.mult, ALU.add))
            e.wait_ge(sv.h, sv.n)
            e.tensor_scalar(SM[:, 656:672], SM[:, 512:528], float(S_MAX),
                            None, ALU.is_lt)
            sv.inc(e.tensor_scalar(SM[:, 608:624], SM[:, 528:544],
                                   float(S_MAX), None, ALU.is_lt))
            e.wait_ge(sv.h, sv.n)
            e.tensor_scalar(SM[:, 624:640], SM[:, 624:640], BIG, None,
                            ALU.subtract)
            sv.inc(e.tensor_scalar(SM[:, 640:656], SM[:, 640:656], BIG, None,
                                   ALU.subtract))
            e.wait_ge(sv.h, sv.n)
            e.scalar_tensor_tensor(SM[:, 624:640], SM[:, 624:640], 1.0,
                                   SM[:, 656:672], ALU.mult, ALU.mult)
            sv.inc(e.scalar_tensor_tensor(SM[:, 640:656], SM[:, 640:656], 1.0,
                                          SM[:, 608:624], ALU.mult, ALU.mult))
            e.wait_ge(sv.h, sv.n)
            e.tensor_scalar(SM[:, 624:640], SM[:, 624:640], BIG, None, ALU.add)
            sv.inc(e.tensor_scalar(SM[:, 640:656], SM[:, 640:656], BIG, None,
                                   ALU.add))
            e.wait_ge(sv.h, sv.n)
            e.tensor_copy(SI[:, 16:32], SM[:, 624:640])
            sv.inc(e.tensor_copy(SI[:, 32:48], SM[:, 640:656]))
            e.wait_ge(sv.h, sv.n)
            e.tensor_copy(bass.AP(F2, 1537, [[F2w, 128], [8, 16], [0, 1]]),
                          blk3(SM, SMw, 352, 16, 1))
            sv.inc(e.tensor_copy(bass.AP(F2, 1541, [[F2w, 128], [8, 16], [0, 1]]),
                                 blk3(SM, SMw, 368, 16, 1)))

        @blk.gpsimd
        def _(e):
            e.wait_ge(sv.h, sv.n)
            for s in range(32):
                tl, k = s // 2, s % 2
                if s > 0:
                    e.wait_ge(sg.h, sg.n)
                sg.inc(e.indirect_dma_start(
                    c.d["d_slotmeta"].ap(),
                    IndirectOffsetOnAxis(SI[:, 16 + 16 * k + tl:17 + 16 * k + tl], 0),
                    F2[:, 1536 + 4 * s:1540 + 4 * s], None), 16)
            e.wait_ge(sg.h, sg.n)

    marks = {}
    xgt_ready = {}
    mslot_mark = {}
    xgb_mark = {}
    prepev = {}
    NIT = 12

    def emit_prep(blk, e_next, st, meta_at, gather_at, tr_at, cur_e):
        mslot = st % 2
        mbuf = F2[:, 604 + mslot * 4:608 + mslot * 4]

        def meta(e, e_next=e_next, st=st, mslot=mslot, mbuf=mbuf):
            if mslot in mslot_mark:
                e.wait_ge(sv.h, mslot_mark[mslot])
            e.wait_ge(sg.h, sg.n)
            smm[mslot].inc(e.dma_start(
                out=mbuf, in_=c.d["d_slotmeta"].ap()
                [e_next * S_MAX + st * 128:e_next * S_MAX + (st + 1) * 128, :]), 16)

        def extract(e, e_next=e_next, st=st, mslot=mslot, mbuf=mbuf):
            e.wait_ge(smm[mslot].h, smm[mslot].n)
            if ("stt", e_next - 1, st) in marks:
                e.wait_ge(sv.h, marks[("stt", e_next - 1, st)])
            if ("scat", e_next - 1, st) in marks:
                e.wait_ge(sgs[st % 2].h, marks[("scat", e_next - 1, st)])
            e.tensor_copy(SI[:, 8 + st:9 + st], mbuf[:, 0:1])
            e.tensor_copy(F2[:, 624 + st:625 + st], mbuf[:, 1:2])
            marks[("mx", e_next, st)] = sv.inc(
                e.tensor_copy(SI[:, 2 + st:3 + st], mbuf[:, 2:3]))
            mslot_mark[mslot] = marks[("mx", e_next, st)]

        def gather(e, e_next=e_next, st=st):
            e.wait_ge(sv.h, marks[("mx", e_next, st)])
            gslot = st % 2
            if gslot in xgb_mark:
                e.wait_ge(sp.h, xgb_mark[gslot])
            sgg[gslot].inc(e.indirect_dma_start(
                F1b[:, gslot * 1024:(gslot + 1) * 1024], None,
                c.d["d_srcb"].ap(), IndirectOffsetOnAxis(SI[:, 8 + st:9 + st], 0)), 16)

        return meta, extract, gather

    with nc.Block() as blk:
        emit_warmup(c, blk)

        for st in range(6):
            meta, extract, gather = emit_prep(blk, 0, st, 0, 0, 0, -1)
            blk.sync(lambda e, meta=meta: meta(e))
            blk.vector(lambda e, extract=extract: extract(e))
            blk.gpsimd(lambda e, gather=gather: gather(e))

            @blk.tensor
            def _(e, st=st):
                e.wait_ge(sgg[st % 2].h, sgg[st % 2].n)
                last = None
                for dc in range(8):
                    last = e.transpose(psab(c)[:, 1024 + dc * 128:1024 + (dc + 1) * 128],
                                       F1b[:, (st % 2) * 1024 + dc * 128:(st % 2) * 1024 + (dc + 1) * 128],
                                       c.CIDENT[:])
                marks[("ptr", 0, st)] = sp.inc(last)
                xgb_mark[st % 2] = marks[("ptr", 0, st)]

            @blk.vector
            def _(e, st=st):
                e.wait_ge(sp.h, marks[("ptr", 0, st)])
                xgt_ready[(0, st)] = sv.inc(
                    e.tensor_copy(A5[:, st * 1024:(st + 1) * 1024], psab(c)[:, 1024:2048]))
                prepev[0] = xgt_ready[(0, st)]

        for e_idx in range(8):
            pe_par = e_idx % 2

            @blk.gpsimd
            def _(e, e_idx=e_idx):
                if e_idx + 1 < 8:
                    if ("mm2", e_idx - 1, NIT - 1) in marks:
                        e.wait_ge(sp.h, marks[("mm2", e_idx - 1, NIT - 1)])
                    emit_slab_loads(e, e_idx + 1)

            for it in range(NIT + 3):
                hh, st = divmod(it, 6) if it < NIT else (None, None)

                @blk.tensor
                def _(e, e_idx=e_idx, it=it, hh=hh, st=st, pe_par=pe_par):
                    half = it % 2
                    if it < NIT:
                        b1, o1 = slab(e_idx, hh, 0)
                        e.wait_ge(sge[pe_par].h, sge[pe_par].n)
                        e.wait_ge(sv.h, xgt_ready[(e_idx, st)])
                        if ("hbev", e_idx, it - 2) in marks:
                            e.wait_ge(sv.h, marks[("hbev", e_idx, it - 2)])
                        if half in prepev:
                            e.wait_ge(sv.h, prepev[half])
                        last = None
                        for ns in range(2):
                            for dc in range(8):
                                last = e.matmul(
                                    PS_A[:, half * 1024 + ns * 512:half * 1024 + (ns + 1) * 512],
                                    A5[:, st * 1024 + dc * 128:st * 1024 + (dc + 1) * 128],
                                    b1[:, o1 + dc * 1024 + ns * 512:o1 + dc * 1024 + (ns + 1) * 512],
                                    start=(dc == 0), stop=(dc == 7))
                        marks[("mm1", e_idx, it)] = sp.inc(last)
                    if 0 <= it - 1 < NIT:
                        e.wait_ge(sa.h, marks[("gelu", e_idx, it - 1)])
                        if (it - 1) % 2 in prepev:
                            e.wait_ge(sv.h, prepev[(it - 1) % 2])
                        h1 = (it - 1) % 2
                        last = None
                        for dc in range(8):
                            last = e.transpose(
                                psab(c)[:, h1 * 2048 + 1024 + dc * 128:h1 * 2048 + 1024 + (dc + 1) * 128],
                                F2b[:, 3328 + h1 * 1024 + dc * 128:3328 + h1 * 1024 + (dc + 1) * 128],
                                c.CIDENT[:])
                        marks[("trH", e_idx, it - 1)] = sp.inc(last)
                    if 0 <= it - 2 < NIT:
                        it2 = it - 2
                        hh2, st2 = divmod(it2, 6)
                        b2, o2 = slab(e_idx, hh2, 1)
                        h2 = it2 % 2
                        e.wait_ge(sv.h, marks[("hbev", e_idx, it2)])
                        if ("outev", e_idx, it2 - 2) in marks:
                            e.wait_ge(sv.h, marks[("outev", e_idx, it2 - 2)])
                        last = None
                        for ns in range(2):
                            for hcc in range(8):
                                last = e.matmul(
                                    PS_B[:, h2 * 1024 + ns * 512:h2 * 1024 + (ns + 1) * 512],
                                    A5[:, 6144 + h2 * 1024 + hcc * 128:6144 + h2 * 1024 + (hcc + 1) * 128],
                                    b2[:, o2 + hcc * 1024 + ns * 512:o2 + hcc * 1024 + (ns + 1) * 512],
                                    start=(hcc == 0), stop=(hcc == 7))
                        marks[("mm2", e_idx, it2)] = sp.inc(last)

                @blk.scalar
                def _(e, e_idx=e_idx, it=it, hh=hh, st=st):
                    if it < NIT:
                        half = it % 2
                        e.wait_ge(sp.h, marks[("mm1", e_idx, it)])
                        if ("trH", e_idx, it - 2) in marks:
                            e.wait_ge(sp.h, marks[("trH", e_idx, it - 2)])
                        e.activation(F2b[:, 3328 + half * 1024:3328 + half * 1024 + 512],
                                     PS_A[:, half * 1024:half * 1024 + 512], AF.Gelu)
                        marks[("gelu", e_idx, it)] = sa.inc(e.activation(
                            F2b[:, 3328 + half * 1024 + 512:3328 + half * 1024 + 1024],
                            PS_A[:, half * 1024 + 512:half * 1024 + 1024], AF.Gelu))

                if e_idx + 1 < 8 and 9 <= it < 15 and it - 9 < 6:
                    stp2 = it - 9

                    @blk.tensor
                    def _(e, e_idx=e_idx, it=it, stp2=stp2):
                        e.wait_ge(sgg[stp2 % 2].h, sgg[stp2 % 2].n)
                        e.wait_ge(sp.h, marks.get(("mm1", e_idx, 6 + stp2), 0))
                        if it < NIT:
                            e.wait_ge(sa.h, marks[("gelu", e_idx, it)])
                        half = it % 2
                        last = None
                        for dc in range(8):
                            last = e.transpose(
                                psab(c)[:, half * 2048 + 1024 + dc * 128:
                                        half * 2048 + 1024 + (dc + 1) * 128],
                                F1b[:, (stp2 % 2) * 1024 + dc * 128:(stp2 % 2) * 1024 + (dc + 1) * 128],
                                c.CIDENT[:])
                        marks[("ptr", e_idx + 1, stp2)] = sp.inc(last)
                        xgb_mark[stp2 % 2] = marks[("ptr", e_idx + 1, stp2)]

                    @blk.vector
                    def _(e, e_idx=e_idx, it=it, stp2=stp2):
                        half = it % 2
                        e.wait_ge(sp.h, marks[("ptr", e_idx + 1, stp2)])
                        xgt_ready[(e_idx + 1, stp2)] = sv.inc(e.tensor_copy(
                            A5[:, stp2 * 1024:(stp2 + 1) * 1024],
                            psab(c)[:, half * 2048 + 1024:half * 2048 + 2048]))
                        prepev[half] = xgt_ready[(e_idx + 1, stp2)]

                @blk.vector
                def _(e, e_idx=e_idx, it=it):
                    if 0 <= it - 1 < NIT:
                        h1 = (it - 1) % 2
                        e.wait_ge(sp.h, marks[("trH", e_idx, it - 1)])
                        if ("mm2", e_idx, it - 3) in marks:
                            e.wait_ge(sp.h, marks[("mm2", e_idx, it - 3)])
                        marks[("hbev", e_idx, it - 1)] = sv.inc(e.tensor_copy(
                            A5[:, 6144 + h1 * 1024:6144 + (h1 + 1) * 1024],
                            psab(c)[:, h1 * 2048 + 1024:h1 * 2048 + 2048]))
                    if 0 <= it - 2 < NIT:
                        it2 = it - 2
                        hh2, st2 = divmod(it2, 6)
                        h2 = it2 % 2
                        oacc = F1[:, 2048 + st2 * 1024:3072 + st2 * 1024]
                        e.wait_ge(sp.h, marks[("mm2", e_idx, it2)])
                        if hh2 == 0:
                            if ("scat", e_idx - 1, st2) in marks:
                                e.wait_ge(sgs[st2 % 2].h, marks[("scat", e_idx - 1, st2)])
                            marks[("outev", e_idx, it2)] = sv.inc(e.tensor_scalar(
                                oacc, PS_B[:, h2 * 1024:(h2 + 1) * 1024],
                                F2[:, 624 + st2:625 + st2], None, ALU.mult))
                        else:
                            marks[("stt", e_idx, st2)] = sv.inc(e.scalar_tensor_tensor(
                                oacc, PS_B[:, h2 * 1024:(h2 + 1) * 1024],
                                F2[:, 624 + st2:625 + st2], oacc, ALU.mult, ALU.add))
                            marks[("outev", e_idx, it2)] = marks[("stt", e_idx, st2)]

                @blk.gpsimd
                def _(e, e_idx=e_idx, it=it):
                    if 0 <= it - 2 < NIT:
                        it2 = it - 2
                        hh2, st2 = divmod(it2, 6)
                        if hh2 == 1:
                            e.wait_ge(sv.h, marks[("stt", e_idx, st2)])
                            e.wait_ge(sgs[0].h, sgs[0].n)
                            e.wait_ge(sgs[1].h, sgs[1].n)
                            marks[("scat", e_idx, st2)] = sgs[st2 % 2].inc(
                                e.indirect_dma_start(
                                    c.d["d_moeout"].ap(),
                                    IndirectOffsetOnAxis(SI[:, 2 + st2:3 + st2], 0),
                                    F1[:, 2048 + st2 * 1024:3072 + st2 * 1024], None), 16)

                if e_idx + 1 < 8 and 8 <= it < 14:
                    stp = it - 8
                    meta, extract, gather = emit_prep(blk, e_idx + 1, stp, 0, 0, 0, e_idx)
                    blk.sync(lambda e, meta=meta: meta(e))
                    blk.vector(lambda e, extract=extract: extract(e))
                    blk.gpsimd(lambda e, gather=gather: gather(e))

        @blk.gpsimd
        def _(e):
            e.wait_ge(sgs[0].h, sgs[0].n)
            e.wait_ge(sgs[1].h, sgs[1].n)

    sb2 = [c.sem("sgg0"), c.sem("sgg1")]
    with nc.Block() as blk:
        bufs = [dict(x1t=F1[:, 0:1024], at1=F1[:, 1024:2048],
                     srct=F1[:, 2048:3072], m0=F1[:, 3072:4096],
                     m1t=F1[:, 4096:5120], scr=F1[:, 5120:6144],
                     stats=SM[:, 0:4]),
                dict(x1t=F1[:, 6144:7168], at1=F1[:, 7168:8192],
                     srct=F2[:, 0:1024], m0=F2[:, 1024:2048],
                     m1t=F2[:, 2048:3072], scr=F2[:, 3072:4096],
                     stats=SM[:, 4:8])]
        for tp in range(NT // 2):
            jobs = []
            for k in range(2):
                t = 2 * tp + k
                b = bufs[k]

                @blk.sync
                def _(e, t=t, b=b, k=k):
                    e.wait_ge(ss[k].h, ss[k].n)
                    for dst, nm in [(b["x1t"], "d_x1"), (b["at1"], "d_attn1"),
                                    (b["srct"], "d_src")]:
                        sl[k].inc(e.dma_start(out=dst,
                                              in_=c.d[nm].ap()[t * 128:(t + 1) * 128, :]), 16)

                @blk.gpsimd
                def _(e, t=t, b=b, k=k):
                    e.wait_ge(ss[k].h, ss[k].n)
                    for m, dst in [(0, b["m0"]), (1, b["m1t"])]:
                        sb2[k].inc(e.dma_start(
                            out=dst,
                            in_=bass.AP(c.d["d_moeout"], (2 * t * 128 + m) * 1024,
                                        [[2048, 128], [1, 1024]])), 16)

            @blk.vector
            def _(e, tp=tp):
                for k in range(2):
                    e.wait_ge(sl[k].h, sl[k].n)
                    e.wait_ge(sb2[k].h, sb2[k].n)
                e.wait_ge(sv.h, sv.n)
                last = None
                for k in range(2):
                    last = e.tensor_tensor(bufs[k]["m0"], bufs[k]["m0"],
                                           bufs[k]["m1t"], ALU.add)
                sv.inc(last)
                e.wait_ge(sv.h, sv.n)
                last = None
                for k in range(2):
                    last = e.tensor_tensor(bufs[k]["srct"], bufs[k]["srct"],
                                           bufs[k]["m0"], ALU.add)
                sv.inc(last)

            rms_jobs(c, blk, [dict(xin=b["srct"], out=b["srct"],
                                   wbcast=SM[:, 1024:2048], stats=b["stats"],
                                   scratch=b["scr"]) for b in bufs])

            @blk.vector
            def _(e):
                e.wait_ge(sv.h, sv.n)
                last = None
                for k in range(2):
                    last = e.tensor_tensor(bufs[k]["x1t"], bufs[k]["x1t"],
                                           bufs[k]["at1"], ALU.add)
                sv.inc(last)
                e.wait_ge(sv.h, sv.n)
                last = None
                for k in range(2):
                    last = e.tensor_tensor(bufs[k]["x1t"], bufs[k]["x1t"],
                                           bufs[k]["srct"], ALU.add)
                sv.inc(last)

            for k in range(2):
                @blk.gpsimd
                def _(e, tp=tp, k=k):
                    e.wait_ge(c.sem("sv").h, c.sem("sv").n)
                    ss[k].inc(e.dma_start(
                        out=c.d["out"].ap()[(2 * tp + k) * 128:(2 * tp + k + 1) * 128, :],
                        in_=bufs[k]["x1t"]), 16)

        @blk.gpsimd
        def _(e):
            e.wait_ge(ss[0].h, ss[0].n)
            e.wait_ge(ss[1].h, ss[1].n)





def stage_launch_lists():
    if MODE == "fused":
        return [STAGES_ALL]
    return [[s] for s in STAGES_ALL]


def kernel(**inputs):
    ncores = 8
    x = np.ascontiguousarray(np.asarray(inputs["L0_attn_x"], dtype=np.float32))
    consts = [host_consts(cid) for cid in range(ncores)]
    outs = None
    bound = {}
    for stages in stage_launch_lists():
        nc = build_program(stages)
        in_maps = []
        for cid in range(ncores):
            m = {"x_in": x[cid]}
            m.update(consts[cid])
            for s in stages:
                for w in STAGE_WEIGHTS[s]:
                    m[w] = np.ascontiguousarray(np.asarray(inputs[w], dtype=np.float32))
            for s in stages:
                for b in BOUNDARY_NEED[s]:
                    if BOUNDARY[b] not in stages:
                        m[b] = bound[b][cid]
            in_maps.append(m)
        res = run_bass_kernel_spmd(nc, in_maps, core_ids=list(range(ncores)))
        for name in BOUNDARY:
            if BOUNDARY[name] in stages and MODE == "staged":
                bound[name] = [res.results[cid][name] for cid in range(ncores)]
        if "moe" in stages:
            outs = np.stack([res.results[cid]["out"] for cid in range(ncores)])
    return outs


# version 13
# speedup vs baseline: 1.1499x; 1.1499x over previous
import contextlib
import numpy as np
import ml_dtypes
import concourse.bass as bass
import concourse.mybir as mybir
from concourse.bass_utils import run_bass_kernel_spmd
from bass_rust import IndirectOffsetOnAxis

F32 = mybir.dt.float32
BF16 = mybir.dt.bfloat16
I32 = mybir.dt.int32
AF = mybir.ActivationFunctionType
ALU = mybir.AluOpType
AX = mybir.AxisListType

T = 2048
NT = 16
ND = 8
RND_C = 12582912.0
S_MAX = 768
CAPACITY = 5120

MODE = "fused"
NCORES = 8
L0_MAX_PHASE = 99
DEBUG_HEADS = None
GQA_MAX_PHASE = 99
STAGES_ALL = ["l0", "mamba", "gqa", "moe"]


def host_consts(core_id):
    bf = ml_dtypes.bfloat16

    def hadamard(size):
        h = np.array([[1.0]], dtype=np.float64)
        while h.shape[0] < size:
            h = np.block([[h, h], [h, -h]])
        return (h / np.sqrt(size)).astype(np.float32)

    bd64 = np.zeros((128, 2), bf)
    bd64[0:64, 0] = 1
    bd64[64:128, 1] = 1
    dest_sel = np.zeros((128, 32), np.float32)
    for tl in range(16):
        for k in range(2):
            dest_sel[:, 2 * tl + k] = 2 * (128 * tl + np.arange(128)) + k
    return {
        "c_hadamard": hadamard(1024).astype(bf),
        "c_ident": np.eye(128, dtype=bf),
        "c_identf": np.eye(128, dtype=np.float32),
        "c_strictup": np.triu(np.ones((128, 128), np.float32), 1),
        "c_bd64": bd64,
        "c_ones_row": np.ones((1, 128), np.float32),
        "c_onesb": np.ones((1, 64), bf),
        "c_su8": np.kron(np.triu(np.ones((16, 16), np.float32), 1),
                         np.eye(8, dtype=np.float32)),
        "c_iota8": np.tile(np.arange(8, dtype=np.float32), (128, 1)),
        "c_dest_sel": dest_sel,
        "c_prefmask": (np.arange(NCORES) < core_id).astype(np.float32).reshape(NCORES, 1),
    }


CONST_SPECS = [
    ("c_hadamard", [1024, 1024], BF16),
    ("c_ident", [128, 128], BF16),
    ("c_identf", [128, 128], F32),
    ("c_strictup", [128, 128], F32),
    ("c_bd64", [128, 2], BF16),
    ("c_ones_row", [1, 128], F32),
    ("c_onesb", [1, 64], BF16),
    ("c_su8", [128, 128], F32),
    ("c_iota8", [128, 8], F32),
    ("c_dest_sel", [128, 32], F32),
]

STAGE_WEIGHTS = {
    "l0": ["L0_attn_wq", "L0_attn_bq", "L0_attn_wk", "L0_attn_bk", "L0_attn_wv",
           "L0_attn_bv", "L0_attn_wo", "L0_attn_bo", "L0_ffn_weight"],
    "mamba": ["L1_attn_ln_w", "L1_attn_ln_b", "L1_attn_W_in", "L1_attn_conv_w",
              "L1_attn_A", "L1_attn_Bm", "L1_attn_Cm", "L1_attn_W_out"],
    "gqa": ["L1_ffn_qw", "L1_ffn_qb", "L1_ffn_kw", "L1_ffn_kb", "L1_ffn_vw",
            "L1_ffn_vb", "L1_ffn_ow", "L1_ffn_ob", "L1_ffn_norm1_w"],
    "moe": ["L1_ffn_gate_w", "L1_ffn_w1", "L1_ffn_w2", "L1_ffn_norm2_w"],
}

WEIGHT_SHAPES = {
    "L0_attn_wq": [1024, 1024], "L0_attn_bq": [1024], "L0_attn_wk": [1024, 1024],
    "L0_attn_bk": [1024], "L0_attn_wv": [1024, 1024], "L0_attn_bv": [1024],
    "L0_attn_wo": [1024, 1024], "L0_attn_bo": [1024], "L0_ffn_weight": [1024, 1024],
    "L1_attn_ln_w": [1024], "L1_attn_ln_b": [1024], "L1_attn_W_in": [1024, 3072],
    "L1_attn_conv_w": [1536, 3], "L1_attn_A": [16, 16], "L1_attn_Bm": [16, 1536],
    "L1_attn_Cm": [1536, 16], "L1_attn_W_out": [1536, 1024],
    "L1_ffn_qw": [1024, 1024], "L1_ffn_qb": [1024], "L1_ffn_kw": [1024, 256],
    "L1_ffn_kb": [256], "L1_ffn_vw": [1024, 256], "L1_ffn_vb": [256],
    "L1_ffn_ow": [1024, 1024], "L1_ffn_ob": [1024], "L1_ffn_gate_w": [1024, 8],
    "L1_ffn_w1": [8, 1024, 2048], "L1_ffn_w2": [8, 2048, 1024],
    "L1_ffn_norm1_w": [1024], "L1_ffn_norm2_w": [1024],
}

BOUNDARY = {"d_x1": "l0", "d_xn1": "mamba", "d_attn1": "mamba", "d_src": "gqa",
            "d_srcb": "gqa"}
BOUNDARY_NEED = {"l0": [], "mamba": ["d_x1"], "gqa": ["d_xn1"],
                 "moe": ["d_x1", "d_attn1", "d_src", "d_srcb"]}


class Sem:
    def __init__(self, h):
        self.h = h
        self.n = 0

    def inc(self, inst, k=1):
        inst.then_inc(self.h, k)
        self.n += k
        return self.n


class Ctx:
    def __init__(self, nc, ex):
        self.nc = nc
        self.ex = ex
        self.d = {}
        self.sems = {}

    def sem(self, name):
        if name not in self.sems:
            self.sems[name] = Sem(self.ex.enter_context(self.nc.semaphore(name)))
        return self.sems[name]


def build_program(stages):
    nc = bass.Bass("TRN2", target_bir_lowering=False)
    ex = contextlib.ExitStack()
    c = Ctx(nc, ex)

    c.d["x_in"] = nc.dram_tensor("x_in", [T, 1024], F32, kind="ExternalInput")
    out_kind = "ExternalOutput" if "moe" in stages else "Internal"
    c.d["out"] = nc.dram_tensor("out", [T, 1024], F32, kind=out_kind)
    for name, shape, dt in CONST_SPECS:
        c.d[name] = nc.dram_tensor(name, shape, dt, kind="ExternalInput")
    c.d["c_prefmask"] = nc.dram_tensor("c_prefmask", [NCORES, 1], F32,
                                       kind="ExternalInput")
    for s in stages:
        for w in STAGE_WEIGHTS[s]:
            c.d[w] = nc.dram_tensor(w, WEIGHT_SHAPES[w], F32, kind="ExternalInput")
    produced = {n for n, p in BOUNDARY.items() if p in stages}
    need = {n for s in stages for n in BOUNDARY_NEED[s]}
    for name in BOUNDARY:
        bdt = BF16 if name == "d_srcb" else F32
        if name in produced:
            kind = "Internal" if MODE == "fused" else "ExternalOutput"
            c.d[name] = nc.dram_tensor(name, [T, 1024], bdt, kind=kind)
        elif name in need:
            c.d[name] = nc.dram_tensor(name, [T, 1024], bdt, kind="ExternalInput")
    if "l0" in stages:
        c.d["d_xtmp"] = nc.dram_tensor("d_xtmp", [T, 1024], F32, kind="Internal")
    if "moe" in stages:
        c.d["d_moeout"] = nc.dram_tensor("d_moeout", [2 * T + 1, 1024], F32, kind="Internal")
        c.d["d_slotmeta"] = nc.dram_tensor("d_slotmeta", [8 * S_MAX + 1, 4], F32, kind="Internal")
        c.d["d_cin"] = nc.dram_tensor("d_cin", [1, 8], F32, kind="Internal")
        c.d["d_cout"] = nc.dram_tensor("d_cout", [NCORES, 8], F32, kind="Internal")

    with ex:
        c.A1 = ex.enter_context(nc.sbuf_tensor("A1", [128, 16640], BF16))
        c.A2 = ex.enter_context(nc.sbuf_tensor("A2", [128, 16640], BF16))
        c.A3 = ex.enter_context(nc.sbuf_tensor("A3", [128, 16640], BF16))
        c.A4 = ex.enter_context(nc.sbuf_tensor("A4", [128, 16640], BF16))
        c.A5 = ex.enter_context(nc.sbuf_tensor("A5", [128, 8192], BF16))
        c.F1 = ex.enter_context(nc.sbuf_tensor("F1", [128, 8192], F32))
        c.F2 = ex.enter_context(nc.sbuf_tensor("F2", [128, 4096], F32))
        c.SM = ex.enter_context(nc.sbuf_tensor("SM", [128, 2048], F32))
        c.CIDENT = ex.enter_context(nc.sbuf_tensor("CIDENT", [128, 128], BF16))
        c.CIDENTF = ex.enter_context(nc.sbuf_tensor("CIDENTF", [128, 128], F32))
        c.CSU = ex.enter_context(nc.sbuf_tensor("CSU", [128, 128], F32))
        c.CBD = ex.enter_context(nc.sbuf_tensor("CBD", [128, 2], BF16))
        c.CONES = ex.enter_context(nc.sbuf_tensor("CONES", [1, 128], F32))
        c.CONESB = ex.enter_context(nc.sbuf_tensor("CONESB", [1, 64], BF16))
        c.CSU8 = ex.enter_context(nc.sbuf_tensor("CSU8", [128, 128], F32))
        c.CMISC = ex.enter_context(nc.sbuf_tensor("CMISC", [128, 48], F32))
        c.SI = ex.enter_context(nc.sbuf_tensor("SI", [128, 48], I32))
        c.PS_A = ex.enter_context(nc.psum_tensor("PS_A", [128, 2048], F32))
        c.PS_B = ex.enter_context(nc.psum_tensor("PS_B", [128, 2048], F32))

        emit_const_load(c)
        for s in stages:
            globals()["emit_" + s](c)
    return nc


def psab(c):
    return c.PS_A.ap().bitcast(BF16)


def psbb(c):
    return c.PS_B.ap().bitcast(BF16)


def emit_const_load(c):
    nc = c.nc
    sd = c.sem("sd")
    with nc.Block() as blk:
        @blk.sync
        def _(e):
            for ap_out, name in [(c.CIDENT[:], "c_ident"), (c.CIDENTF[:], "c_identf"),
                                 (c.CSU[:], "c_strictup"), (c.CBD[:], "c_bd64"),
                                 (c.CONES[:], "c_ones_row"),
                                 (c.CONESB[:], "c_onesb"),
                                 (c.CSU8[:], "c_su8"),
                                 (c.CMISC[:, 0:8], "c_iota8"),
                                 (c.CMISC[:, 8:40], "c_dest_sel")]:
                sd.inc(e.dma_start(out=ap_out, in_=c.d[name].ap()), 16)
            e.wait_ge(sd.h, sd.n)

        @blk.vector
        def _(e):
            e.memset(c.CMISC[:, 40:41], 1e-5)
            e.memset(c.CMISC[:, 41:42], 1e-6)
            e.memset(c.CMISC[:, 42:43], 1.0)



def ln_tile(c, blk, xin, outs, stats, scratch, eps_col=40, gam=None, bet=None,
            waits=()):
    sv, sa = c.sem("sv"), c.sem("sa")
    mean, var, rstd, tmp = (stats[:, i:i + 1] for i in range(4))
    sv_entry = sv.n

    @blk.vector
    def _(e):
        for (s, v) in waits:
            e.wait_ge(s.h, v)
        e.wait_ge(sv.h, sv_entry)
        sv.inc(e.tensor_tensor(scratch, xin, xin, ALU.mult))
        e.wait_ge(sv.h, sv.n)
        e.reduce_sum(mean, xin, AX.X)
        sv.inc(e.reduce_sum(var, scratch, AX.X))
        e.wait_ge(sv.h, sv.n)
        e.tensor_scalar(mean, mean, 1.0 / 1024.0, None, ALU.mult)
        sv.inc(e.tensor_scalar(var, var, 1.0 / 1024.0, None, ALU.mult))
        e.wait_ge(sv.h, sv.n)
        sv.inc(e.tensor_tensor(tmp, mean, mean, ALU.mult))
        e.wait_ge(sv.h, sv.n)
        sv.inc(e.tensor_tensor(var, var, tmp, ALU.subtract))

    @blk.scalar
    def _(e):
        e.wait_ge(sv.h, sv.n)
        sa.inc(e.activation(tmp, var, AF.Ln, bias=c.CMISC[:, eps_col:eps_col + 1]))
        e.wait_ge(sa.h, sa.n)
        sa.inc(e.activation(rstd, tmp, AF.Exp, scale=-0.5))

    @blk.vector
    def _(e):
        e.wait_ge(sa.h, sa.n)
        if gam is None:
            last = None
            for o in outs:
                last = e.tensor_scalar(o, xin, mean, rstd, ALU.subtract, ALU.mult)
            sv.inc(last)
        else:
            sv.inc(e.tensor_scalar(scratch, xin, mean, rstd, ALU.subtract, ALU.mult))
            e.wait_ge(sv.h, sv.n)
            sv.inc(e.tensor_tensor(scratch, scratch, gam, ALU.mult))
            e.wait_ge(sv.h, sv.n)
            last = None
            for o in outs:
                last = e.tensor_tensor(o, scratch, bet, ALU.add)
            sv.inc(last)


def ln_front(c, blk, jobs, eps_col=40):
    sv, sa = c.sem("sv"), c.sem("sa")

    @blk.vector
    def _(e):
        for j in jobs:
            for (s, v) in j.get("waits", ()):
                e.wait_ge(s.h, v)
        last = None
        for j in jobs:
            e.tensor_tensor(j["scratch"], j["xin"], j["xin"], ALU.mult)
            last = e.reduce_sum(j["stats"][:, 0:1], j["xin"], AX.X)
        sv.inc(last)
        e.wait_ge(sv.h, sv.n)
        for j in jobs:
            e.reduce_sum(j["stats"][:, 1:2], j["scratch"], AX.X)
            last = e.tensor_scalar(j["stats"][:, 0:1], j["stats"][:, 0:1],
                                   1.0 / 1024.0, None, ALU.mult)
        sv.inc(last)
        e.wait_ge(sv.h, sv.n)
        for j in jobs:
            last = e.tensor_tensor(j["stats"][:, 3:4], j["stats"][:, 0:1],
                                   j["stats"][:, 0:1], ALU.mult)
        sv.inc(last)
        e.wait_ge(sv.h, sv.n)
        for j in jobs:
            last = e.tensor_tensor(j["stats"][:, 2:3],
                                   c.CMISC[:, eps_col:eps_col + 1],
                                   j["stats"][:, 3:4], ALU.subtract)
        sv.inc(last)

    @blk.scalar
    def _(e):
        e.wait_ge(sv.h, sv.n)
        last = None
        for j in jobs:
            last = e.activation(j["stats"][:, 3:4], j["stats"][:, 1:2], AF.Ln,
                                scale=1.0 / 1024.0, bias=j["stats"][:, 2:3])
        sa.inc(last)
        e.wait_ge(sa.h, sa.n)
        last = None
        for j in jobs:
            last = e.activation(j["stats"][:, 2:3], j["stats"][:, 3:4], AF.Exp,
                                scale=-0.5)
        sa.inc(last)
    return sa.n


def ln_back(c, blk, jobs, sa_mark):
    sv, sa = c.sem("sv"), c.sem("sa")

    @blk.vector
    def _(e):
        e.wait_ge(sa.h, sa_mark)
        last = None
        for j in jobs:
            mean, rstd = j["stats"][:, 0:1], j["stats"][:, 2:3]
            if j.get("gam") is None:
                for o in j["outs"]:
                    last = e.tensor_scalar(o, j["xin"], mean, rstd,
                                           ALU.subtract, ALU.mult)
            else:
                last = e.tensor_scalar(j["scratch"], j["xin"], mean, rstd,
                                       ALU.subtract, ALU.mult)
        sv.inc(last)
        if any(j.get("gam") is not None for j in jobs):
            e.wait_ge(sv.h, sv.n)
            last = None
            for j in jobs:
                if j.get("gam") is not None:
                    last = e.tensor_tensor(j["scratch"], j["scratch"],
                                           j["gam"], ALU.mult)
            sv.inc(last)
            e.wait_ge(sv.h, sv.n)
            last = None
            for j in jobs:
                if j.get("gam") is not None:
                    for o in j["outs"]:
                        last = e.tensor_tensor(o, j["scratch"], j["bet"],
                                               ALU.add)
            sv.inc(last)


def ln_jobs(c, blk, jobs, eps_col=40):
    ln_back(c, blk, jobs, ln_front(c, blk, jobs, eps_col))


def rms_jobs(c, blk, jobs, eps_col=41):
    sv, sa = c.sem("sv"), c.sem("sa")
    sv_entry = sv.n

    @blk.vector
    def _(e):
        for j in jobs:
            for (s, v) in j.get("waits", ()):
                e.wait_ge(s.h, v)
        e.wait_ge(sv.h, sv_entry)
        last = None
        for j in jobs:
            last = e.tensor_tensor(j["scratch"], j["xin"], j["xin"], ALU.mult)
        sv.inc(last)
        e.wait_ge(sv.h, sv.n)
        last = None
        for j in jobs:
            last = e.reduce_sum(j["stats"][:, 0:1], j["scratch"], AX.X)
        sv.inc(last)

    @blk.scalar
    def _(e):
        e.wait_ge(sv.h, sv.n)
        last = None
        for j in jobs:
            last = e.activation(j["stats"][:, 1:2], j["stats"][:, 0:1], AF.Ln,
                                scale=1.0 / 1024.0,
                                bias=c.CMISC[:, eps_col:eps_col + 1])
        sa.inc(last)
        e.wait_ge(sa.h, sa.n)
        last = None
        for j in jobs:
            last = e.activation(j["stats"][:, 2:3], j["stats"][:, 1:2], AF.Exp,
                                scale=-0.5)
        sa.inc(last)

    @blk.vector
    def _(e):
        e.wait_ge(sa.h, sa.n)
        last = None
        for j in jobs:
            last = e.scalar_tensor_tensor(j["out"], j["xin"],
                                          j["stats"][:, 2:3], j["wbcast"],
                                          ALU.mult, ALU.mult)
        sv.inc(last)


def transpose_to_T(c, blk, src_bf, dstbuf, dst_col, nchunk=8, psum_bank=0):
    sp, sv = c.sem("sp"), c.sem("sv")
    pb = psab(c)
    base = psum_bank * 1024

    @blk.tensor
    def _(e):
        e.wait_ge(sv.h, sv.n)
        for dc in range(nchunk):
            sp.inc(e.transpose(pb[:, base + dc * 128:base + (dc + 1) * 128],
                        src_bf[:, dc * 128:(dc + 1) * 128], c.CIDENT[:]))

    @blk.vector
    def _(e):
        e.wait_ge(sp.h, sp.n)
        dst = bass.AP(dstbuf, dst_col, [[16640, 128], [2048, nchunk], [1, 128]])
        sv.inc(e.tensor_copy(dst, pb[:, base:base + nchunk * 128]
                      .rearrange("p (a b) -> p a b", a=nchunk)))


def row_to_bcast(c, blk, row_ap, dst_sm, width, tmp_psum_col=0, waits=()):
    sp, sv = c.sem("sp"), c.sem("sv")

    @blk.tensor
    def _(e):
        e.wait_ge(sv.h, sv.n)
        for (s, v) in waits:
            e.wait_ge(s.h, v)
        for j in range(0, width, 512):
            w = min(512, width - j)
            sp.inc(e.matmul(c.PS_B[0:128, tmp_psum_col + j:tmp_psum_col + j + w],
                     c.CONES[0:1, 0:128], row_ap[:, j:j + w], start=True, stop=True))

    @blk.vector
    def _(e):
        e.wait_ge(sp.h, sp.n)
        sv.inc(e.tensor_copy(dst_sm, c.PS_B[:, tmp_psum_col:tmp_psum_col + width]))



def emit_l0(c):
    nc = c.nc
    sd, sg, sp, sv, sa = (c.sem(n) for n in ["sd", "sg", "sp", "sv", "sa"])
    W = c.d
    A1, A2, A3, A4, A5 = c.A1, c.A2, c.A3, c.A4, c.A5
    F1, F2, SM = c.F1, c.F2, c.SM
    F1b = c.F1.ap().bitcast(BF16)
    F2b = c.F2.ap().bitcast(BF16)
    PS_A, PS_B = c.PS_A, c.PS_B

    if L0_MAX_PHASE < 0:
        return
    with nc.Block() as blk:
        @blk.gpsimd
        def _(e):
            sg.inc(e.dma_start(
                out=bass.AP(F1, 0, [[8192, 128], [1024, 8], [1, 1024]]),
                in_=W["L0_ffn_weight"].ap().rearrange("(cc p) i -> p cc i", p=128)), 16)

        @blk.vector
        def _(e):
            e.wait_ge(sg.h, sg.n)
            for oc in range(8):
                wv = F1[:, oc * 1024:(oc + 1) * 1024]
                s_col, th, thn = SM[:, 1060:1061], SM[:, 1061:1062], SM[:, 1062:1063]
                sv.inc(e.reduce_sum(s_col, wv, AX.X, apply_absolute_value=True))
                e.wait_ge(sv.h, sv.n)
                sv.inc(e.tensor_scalar(s_col, s_col, 1.0 / 1024.0, None, ALU.mult))
                e.wait_ge(sv.h, sv.n)
                sv.inc(e.tensor_scalar_max(s_col, s_col, 1e-5))
                e.wait_ge(sv.h, sv.n)
                e.tensor_scalar(th, s_col, 0.5, None, ALU.mult)
                sv.inc(e.tensor_scalar(thn, s_col, -0.5, None, ALU.mult))
                e.wait_ge(sv.h, sv.n)
                e.tensor_scalar(F2[:, 0:1024], wv, th, None, ALU.is_gt)
                sv.inc(e.tensor_scalar(F2[:, 1024:2048], wv, thn, None, ALU.is_lt))
                e.wait_ge(sv.h, sv.n)
                sv.inc(e.tensor_tensor(F2[:, 2048:3072], F2[:, 0:1024], F2[:, 1024:2048],
                                ALU.subtract))
                e.wait_ge(sv.h, sv.n)
                sv.inc(e.tensor_scalar(A3[:, oc * 1024:(oc + 1) * 1024], F2[:, 2048:3072],
                                s_col, None, ALU.mult))
                e.wait_ge(sv.h, sv.n)

    if L0_MAX_PHASE < 1:
        return
    with nc.Block() as blk:
        @blk.gpsimd
        def _(e):
            sg.inc(e.dma_start(
                out=F2b.rearrange("p (cc ee) -> p cc ee", cc=8),
                in_=c.d["c_hadamard"].ap().rearrange("(cc p) ee -> p cc ee", p=128)), 16)
            for i, nm in enumerate(["L0_attn_wq", "L0_attn_wk", "L0_attn_wv", "L0_attn_wo"]):
                buf = A1 if i < 2 else A2
                off = 8192 * (i % 2)
                sg.inc(e.dma_start(
                    out=bass.AP(buf, off, [[16640, 128], [1024, 8], [1, 1024]]),
                    in_=W[nm].ap().rearrange("(cc p) dd -> p cc dd", p=128)), 16)

        pb = psab(c)
        for ic in range(8):
            @blk.tensor
            def _(e, ic=ic):
                if ic > 0:
                    e.wait_ge(sv.h, sv.n)
                for oc in range(8):
                    sp.inc(e.transpose(pb[:, oc * 128:(oc + 1) * 128],
                                A3[:, oc * 1024 + ic * 128:oc * 1024 + (ic + 1) * 128],
                                c.CIDENT[:]))

            @blk.vector
            def _(e, ic=ic):
                e.wait_ge(sp.h, sp.n)
                sv.inc(e.tensor_copy(bass.AP(A5, ic * 1024, [[8192, 128], [128, 8], [1, 128]]),
                              pb[:, 0:1024].rearrange("p (a b) -> p a b", a=8)))

    if L0_MAX_PHASE < 2:
        return
    with nc.Block() as blk:
        @blk.sync
        def _(e):
            for i, nm in enumerate(["L0_attn_bq", "L0_attn_bk"]):
                sd.inc(e.dma_start(out=bass.AP(SM, 1024 + i * 128, [[2048, 8], [1, 128]]),
                                   in_=W[nm].ap().rearrange("(cc p) -> cc p", p=128)), 16)
            sd.inc(e.dma_start(out=bass.AP(SM, 0, [[2048, 1], [1, 1024]]),
                               in_=W["L0_attn_bv"].ap().rearrange("(a b) -> a b", a=1)), 16)

        pb = psab(c)
        for i in range(4):
            src = A1 if i < 2 else A2
            soff = 8192 * (i % 2)
            dst = A4 if i < 2 else A3
            doff = 8192 * (i % 2)
            for dc in range(8):
                @blk.tensor
                def _(e, src=src, soff=soff, dc=dc, first=(i == 0 and dc == 0)):
                    if first:
                        e.wait_ge(sg.h, sg.n)
                    else:
                        e.wait_ge(sv.h, sv.n)
                    for ec in range(8):
                        sp.inc(e.transpose(pb[:, ec * 128:(ec + 1) * 128],
                                    bass.AP(src, soff + ec * 1024 + dc * 128,
                                            [[16640, 128], [1, 128]]),
                                    c.CIDENT[:]))

                @blk.vector
                def _(e, dst=dst, doff=doff, dc=dc):
                    e.wait_ge(sp.h, sp.n)
                    sv.inc(e.tensor_copy(bass.AP(dst, doff + dc * 1024, [[16640, 128], [1, 1024]]),
                                  pb[:, 0:1024]))

        @blk.tensor
        def _(e):
            e.wait_ge(sd.h, sd.n)
            e.wait_ge(sv.h, sv.n)
            e.transpose(PS_B[:, 0:8], bass.AP(SM, 1024, [[2048, 8], [1, 128]]),
                        c.CIDENTF[0:8, 0:8])
            sp.inc(e.transpose(PS_B[:, 8:16], bass.AP(SM, 1152, [[2048, 8], [1, 128]]),
                        c.CIDENTF[0:8, 0:8]))

        @blk.vector
        def _(e):
            e.wait_ge(sp.h, sp.n)
            e.tensor_copy(SM[:, 1024:1032], PS_B[:, 0:8])
            sv.inc(e.tensor_copy(SM[:, 1032:1040], PS_B[:, 8:16]))

    sl = [c.sem("sl0"), c.sem("sl1")]
    ss = [c.sem("ss0"), c.sem("ss1")]
    sl4 = [sl[0], sl[1], c.sem("sl2"), c.sem("sl3")]
    with nc.Block() as blk:
        fronts = {}
        backmark = {}
        for tp in range(NT // 2 + 1):
            if tp < NT // 2:
                jobs = []
                for k in range(2):
                    t = 2 * tp + k
                    xs = (tp % 2) * 2 + k
                    xt = F1[:, xs * 1024:(xs + 1) * 1024]
                    lsem = sl4[xs]

                    @blk.sync
                    def _(e, t=t, xt=xt, lsem=lsem, tp=tp):
                        if tp >= 2:
                            e.wait_ge(c.sem("sv").h, backmark[tp - 2])
                        lsem.inc(e.dma_start(out=xt,
                                             in_=c.d["x_in"].ap()[t * 128:(t + 1) * 128, :]), 16)

                    jobs.append(dict(
                        xin=xt,
                        outs=[F1b[:, 12288 + k * 1024:13312 + k * 1024]],
                        stats=SM[:, 1056 + 8 * (tp % 2) + 4 * k:
                                 1060 + 8 * (tp % 2) + 4 * k],
                        scratch=F1[:, 4096 + k * 1024:5120 + k * 1024],
                        waits=[(lsem, lsem.n)]))
                fronts[tp] = (jobs, ln_front(c, blk, jobs))
            if tp - 1 in fronts:
                pj, pm = fronts.pop(tp - 1)
                ln_back(c, blk, pj, pm)
                backmark[tp - 1] = c.sem("sv").n
                for k in range(2):
                    transpose_to_T(c, blk, F1b[:, 12288 + k * 1024:13312 + k * 1024],
                                   A1, (2 * (tp - 1) + k) * 128, 8, psum_bank=k)

    if L0_MAX_PHASE < 4:
        return
    with nc.Block() as blk:
        for t in range(NT):
            slot = t % 2
            xt = F1[:, slot * 1024:(slot + 1) * 1024]

            @blk.sync
            def _(e, t=t, xt=xt, slot=slot):
                if t >= 2:
                    e.wait_ge(ss[slot].h, ss[slot].n)
                sl[slot].inc(e.dma_start(out=xt, in_=c.d["x_in"].ap()[t * 128:(t + 1) * 128, :]), 16)

            @blk.tensor
            def _(e, t=t):
                e.wait_ge(sv.h, sv.n)
                for ns in range(2):
                    for dc in range(8):
                        sp.inc(e.matmul(PS_A[:, ns * 512:(ns + 1) * 512],
                                 A1[:, dc * 2048 + t * 128:dc * 2048 + (t + 1) * 128],
                                 F2b[:, dc * 1024 + ns * 512:dc * 1024 + (ns + 1) * 512],
                                 start=(dc == 0), stop=(dc == 7)))

            @blk.vector
            def _(e, t=t):
                e.wait_ge(sp.h, sp.n)
                amax, inv, s7 = SM[:, 1056:1057], SM[:, 1057:1058], SM[:, 1058:1059]
                sv.inc(e.reduce_max(amax, PS_A[:, 0:1024], AX.X, apply_absolute_value=True))
                e.wait_ge(sv.h, sv.n)
                sv.inc(e.tensor_scalar_max(amax, amax, 1e-5))
                e.wait_ge(sv.h, sv.n)
                e.reciprocal(inv, amax)
                sv.inc(e.tensor_scalar(s7, amax, 1.0 / 7.0, None, ALU.mult))
                e.wait_ge(sv.h, sv.n)
                sv.inc(e.tensor_scalar(F1[:, 2048:3072], PS_A[:, 0:1024], inv, 7.0, ALU.mult, ALU.mult))
                e.wait_ge(sv.h, sv.n)
                sv.inc(e.tensor_scalar(F1[:, 3072:4096], F1[:, 2048:3072], RND_C, RND_C,
                                ALU.add, ALU.subtract))
                e.wait_ge(sv.h, sv.n)
                sv.inc(e.tensor_scalar(F1b[:, 8192:9216], F1[:, 3072:4096], s7, None, ALU.mult))

            transpose_to_T(c, blk, F1b[:, 8192:9216], A2, t * 128, 8, psum_bank=2 + t % 2)

            @blk.tensor
            def _(e, t=t):
                e.wait_ge(sv.h, sv.n)
                for ns in range(2):
                    for ic in range(8):
                        sp.inc(e.matmul(PS_B[:, ns * 512:(ns + 1) * 512],
                                 A2[:, ic * 2048 + t * 128:ic * 2048 + (t + 1) * 128],
                                 A5[:, ic * 1024 + ns * 512:ic * 1024 + (ns + 1) * 512],
                                 start=(ic == 0), stop=(ic == 7)))

            @blk.vector
            def _(e, xt=xt, slot=slot):
                e.wait_ge(sp.h, sp.n)
                e.wait_ge(sl[slot].h, sl[slot].n)
                sv.inc(e.scalar_tensor_tensor(xt, PS_B[:, 0:1024], 1.0, xt, ALU.mult, ALU.add))

            @blk.gpsimd
            def _(e, t=t, xt=xt, slot=slot):
                e.wait_ge(sv.h, sv.n)
                ss[slot].inc(e.dma_start(out=c.d["d_xtmp"].ap()[t * 128:(t + 1) * 128, :], in_=xt), 16)

        @blk.gpsimd
        def _(e):
            e.wait_ge(ss[0].h, ss[0].n)
            e.wait_ge(ss[1].h, ss[1].n)

    if L0_MAX_PHASE < 5:
        return
    emit_qkv(c, xnT=A1.ap(), wq=(A4, 0), wk=(A4, 8192), wv=(A3, 0),
             QT=A2.ap(), KT=F1b, Vext=A4.ap(),
             bq_cols=SM[:, 1024:1032], bk_cols=SM[:, 1032:1040],
             bv_row=SM[0:1, 0:1024], bvb=SM[:, 0:1024], nk_e=1024, nv=1024)

    if L0_MAX_PHASE < 6:
        return
    emit_norm_bounds(c, QT=A2.ap(), KT=F1b, nq_heads=16, nkv=16, kv_group=1,
                     scale=0.25, sq_scratch=A5[:, 4096:6144],
                     negc_cols=SM[:, 1040:1056])
    emit_attention(c, QT=A2.ap(), KT=F1b, Vext=A4.ap(), outT=A1.ap(), nq_heads=16,
                   kv_group=1, scale=0.25, scratch_bf=A5.ap(),
                   negc_cols=SM[:, 1040:1056],
                   recip_bf=F2.ap().bitcast(BF16)[0:1, 4096:6144],
                   recip_tmp=F2[0:1, 0:2048])

    if L0_MAX_PHASE < 7:
        return
    with nc.Block() as blk:
        @blk.sync
        def _(e):
            sd.inc(e.dma_start(out=bass.AP(SM, 0, [[2048, 1], [1, 1024]]),
                               in_=W["L0_attn_bo"].ap().rearrange("(a b) -> a b", a=1)), 16)

        row_to_bcast(c, blk, SM[0:1, 0:1024], SM[:, 0:1024], 1024,
                     waits=[(sd, sd.n)])

        evmark = {}
        for t in range(NT):
            slot = t % 2
            xt = F1[:, slot * 1024:(slot + 1) * 1024]

            @blk.sync
            def _(e, t=t, xt=xt, slot=slot):
                e.wait_ge(ss[slot].h, ss[slot].n)
                sl[slot].inc(e.dma_start(out=xt,
                                         in_=c.d["d_xtmp"].ap()[t * 128:(t + 1) * 128, :]), 16)

            @blk.tensor
            def _(e, t=t):
                if t >= 2:
                    e.wait_ge(sv.h, evmark[t - 2])
                ps = PS_A[:, (t % 2) * 1024:(t % 2) * 1024 + 1024]
                for ns in range(2):
                    for dc in range(8):
                        sp.inc(e.matmul(ps[:, ns * 512:(ns + 1) * 512],
                                 A1[:, dc * 2048 + t * 128:dc * 2048 + (t + 1) * 128],
                                 A3[:, 8192 + dc * 1024 + ns * 512:8192 + dc * 1024 + (ns + 1) * 512],
                                 start=(dc == 0), stop=(dc == 7)))

            @blk.vector
            def _(e, t=t, xt=xt, slot=slot):
                e.wait_ge(sp.h, sp.n)
                e.wait_ge(sl[slot].h, sl[slot].n)
                ps = PS_A[:, (t % 2) * 1024:(t % 2) * 1024 + 1024]
                sv.inc(e.scalar_tensor_tensor(xt, ps, 1.0, xt, ALU.mult, ALU.add))
                evmark[t] = sv.n
                e.wait_ge(sv.h, sv.n)
                sv.inc(e.tensor_tensor(xt, xt, SM[:, 0:1024], ALU.add))

            @blk.gpsimd
            def _(e, t=t, xt=xt, slot=slot):
                e.wait_ge(sv.h, sv.n)
                ss[slot].inc(e.dma_start(out=c.d["d_x1"].ap()[t * 128:(t + 1) * 128, :],
                                         in_=xt), 16)

        @blk.gpsimd
        def _(e):
            e.wait_ge(ss[0].h, ss[0].n)
            e.wait_ge(ss[1].h, ss[1].n)


def emit_warmup(c, blk, n=28):
    sp = c.sem("sp")

    @blk.tensor
    def _(e):
        last = None
        for i in range(2 * n):
            last = e.matmul(c.PS_B[0:128, 0:128], c.CIDENT[:],
                            c.CIDENT[:], start=True, stop=True)
        sp.inc(last)


def emit_qkv(c, xnT, wq, wk, wv, QT, KT, Vext, bq_cols, bk_cols, bv_row, bvb,
             nk_e=1024, nv=1024, entry_waits=()):
    wq = (wq[0].ap() if not isinstance(wq[0], bass.AP) else wq[0], wq[1])
    wk = (wk[0].ap() if not isinstance(wk[0], bass.AP) else wk[0], wk[1])
    wv = (wv[0].ap() if not isinstance(wv[0], bass.AP) else wv[0], wv[1])
    """Build QT [128, 8*2048] and KT [128, (nk_e/128)*2048] feature-major bf16
    (with per-partition bias via ACT evac), and Vext token-major bf16 with bias
    broadcast and ones columns (vcols = (nv/64)*65 per tile).
    wq/wk/wv: (buf, off) with layout [128 d-part, dc*ewidth + e] bf16,
    ewidth = 1024 for q, nk_e for k, nv for v.
    """
    nc = c.nc
    sd, sp, sv, sa = (c.sem(n) for n in ["sd", "sp", "sv", "sa"])
    PS_A, PS_B = c.PS_A, c.PS_B

    with nc.Block() as blk:
        act_marks = []
        jobs = []
        for ec in range(8):
            for tq in range(4):
                jobs.append((wq, 1024, QT, bq_cols, ec, tq))
        for ec in range(nk_e // 128):
            for tq in range(4):
                jobs.append((wk, nk_e, KT, bk_cols, ec, tq))
        for gi, (wbo, ew, dstbuf, bcols, ec, tq) in enumerate(jobs):
            pslot = gi % 4

            @blk.tensor
            def _(e, wbo=wbo, ew=ew, ec=ec, tq=tq, pslot=pslot, gi=gi):
                if gi == 0:
                    for (s, v) in entry_waits:
                        e.wait_ge(s.h, v)
                if gi >= 4:
                    e.wait_ge(sv.h, act_marks[gi - 4])
                ps = PS_A[:, pslot * 512:(pslot + 1) * 512]
                wbuf, woff = wbo
                for dc in range(8):
                    sp.inc(e.matmul(ps,
                             wbuf[:, woff + dc * ew + ec * 128:woff + dc * ew + (ec + 1) * 128],
                             xnT[:, dc * 2048 + tq * 512:dc * 2048 + (tq + 1) * 512],
                             start=(dc == 0), stop=(dc == 7)))

            @blk.vector
            def _(e, dstbuf=dstbuf, bcols=bcols, ec=ec, tq=tq, pslot=pslot):
                e.wait_ge(sp.h, sp.n)
                ps = PS_A[:, pslot * 512:(pslot + 1) * 512]
                dst = dstbuf[:, ec * 2048 + tq * 512:ec * 2048 + (tq + 1) * 512]
                act_marks.append(sv.inc(
                    e.tensor_scalar(dst, ps, bcols[:, ec:ec + 1], None, ALU.add)))

    nh = nv // 64
    vcols = nh * 65
    with nc.Block() as blk:
        row_to_bcast(c, blk, bv_row, bvb[:, 0:nv], nv)

        @blk.vector
        def _(e):
            e.wait_ge(sv.h, sv.n)
            sv.inc(e.memset(Vext[:, 0:NT * vcols], 1.0))

        ev_marks = []
        for t in range(NT):
            @blk.tensor
            def _(e, t=t):
                if t == 0:
                    for (s, v) in entry_waits:
                        e.wait_ge(s.h, v)
                if t < 2:
                    e.wait_ge(sv.h, sv.n)
                else:
                    e.wait_ge(sv.h, ev_marks[t - 2])
                ps = PS_B[:, (t % 2) * 1024:(t % 2) * 1024 + 1024]
                wbuf, woff = wv
                for j in range(0, nv, 512):
                    wdt = min(512, nv - j)
                    for dc in range(8):
                        sp.inc(e.matmul(ps[:, j:j + wdt],
                                 xnT[:, dc * 2048 + t * 128:dc * 2048 + (t + 1) * 128],
                                 wbuf[:, woff + dc * nv + j:woff + dc * nv + j + wdt],
                                 start=(dc == 0), stop=(dc == 7)))

            @blk.vector
            def _(e, t=t):
                e.wait_ge(sp.h, sp.n)
                ps = PS_B[:, (t % 2) * 1024:(t % 2) * 1024 + 1024]
                dst = Vext[:, t * vcols:(t + 1) * vcols] \
                    .rearrange("p (h x) -> p h x", h=nh)[:, :, 0:64]
                ev_marks.append(sv.inc(e.tensor_tensor(
                    dst, ps[:, 0:nv].rearrange("p (a b) -> p a b", a=nh),
                    bvb[:, 0:nv].rearrange("p (a b) -> p a b", a=nh), ALU.add)))


def emit_norm_bounds(c, QT, KT, nq_heads, nkv, kv_group, scale, sq_scratch,
                     negc_cols):
    nc = c.nc
    sp, sv, sa = (c.sem(n) for n in ["sp", "sv", "sa"])
    PS_A, PS_B = c.PS_A, c.PS_B
    F2 = c.F2

    with nc.Block() as blk:
        @blk.vector
        def _(e):
            sv.inc(e.memset(F2[0:2, 0:64], 0.0))

        chunks = ([(QT, hc, hc) for hc in range(nq_heads // 2)] +
                  [(KT, hc, 16 + hc) for hc in range(nkv // 2)])
        for i, (buf, hc, col) in enumerate(chunks):
            @blk.vector
            def _(e, buf=buf, hc=hc, i=i):
                if i > 0:
                    e.wait_ge(sp.h, sp.n)
                e.wait_ge(sv.h, sv.n)
                src_v = buf[:, hc * 2048:(hc + 1) * 2048]
                sa.inc(e.tensor_tensor(sq_scratch, src_v, src_v, ALU.mult))

            @blk.tensor
            def _(e, i=i):
                e.wait_ge(sa.h, sa.n)
                if i > 0:
                    e.wait_ge(sv.h, sv.n)
                for j in range(4):
                    sp.inc(e.matmul(PS_B[0:2, j * 512:(j + 1) * 512], c.CBD[:, 0:2],
                             sq_scratch[:, j * 512:(j + 1) * 512],
                             start=True, stop=True))

            @blk.vector
            def _(e, col=col, i=i):
                e.wait_ge(sp.h, sp.n)
                if i == 0:
                    e.wait_ge(sv.h, sv.n)
                sv.inc(e.reduce_max(F2[0:2, col:col + 1], PS_B[0:2, 0:2048], AX.X))

        @blk.scalar
        def _(e):
            e.wait_ge(sv.h, sv.n)
            sa.inc(e.activation(F2[0:2, 32:64], F2[0:2, 0:32], AF.Sqrt))

        @blk.tensor
        def _(e):
            e.wait_ge(sa.h, sa.n)
            for hc in range(nq_heads // 2):
                e.transpose(PS_A[0:1, 64 + 2 * hc:64 + 2 * hc + 2],
                            F2[0:2, 32 + hc:33 + hc], c.CIDENTF[0:2, 0:2])
            for kc in range(nkv // 2):
                sp.inc(e.transpose(PS_A[0:1, 96 + 2 * kc:96 + 2 * kc + 2],
                            F2[0:2, 48 + kc:49 + kc], c.CIDENTF[0:2, 0:2]))

        @blk.vector
        def _(e):
            e.wait_ge(sp.h, sp.n)
            e.tensor_copy(F2[0:1, 64:64 + nq_heads], PS_A[0:1, 64:64 + nq_heads])
            sv.inc(e.tensor_copy(F2[0:1, 96:96 + nkv], PS_A[0:1, 96:96 + nkv]))
            e.wait_ge(sv.h, sv.n)
            if kv_group == 1:
                sv.inc(e.tensor_tensor(F2[0:1, 128:128 + nq_heads],
                                       F2[0:1, 64:64 + nq_heads],
                                       F2[0:1, 96:96 + nkv], ALU.mult))
            else:
                qv = F2[0:1, 64:64 + nq_heads].rearrange("p (a b) -> p a b", a=nkv)
                kv = bass.AP(c.F2, 96, [[4096, 1], [1, nkv], [0, kv_group]])
                ov = F2[0:1, 128:128 + nq_heads].rearrange("p (a b) -> p a b", a=nkv)
                sv.inc(e.tensor_tensor(ov, qv, kv, ALU.mult))
            e.wait_ge(sv.h, sv.n)
            sv.inc(e.tensor_scalar(F2[0:1, 160:160 + nq_heads], F2[0:1, 128:128 + nq_heads],
                            -scale, None, ALU.mult))

        @blk.tensor
        def _(e):
            e.wait_ge(sv.h, sv.n)
            sp.inc(e.matmul(PS_B[0:128, 0:nq_heads], c.CONES[0:1, 0:128],
                     F2[0:1, 160:160 + nq_heads], start=True, stop=True))

        @blk.vector
        def _(e):
            e.wait_ge(sp.h, sp.n)
            sv.inc(e.tensor_copy(negc_cols, PS_B[:, 0:nq_heads]))


def emit_attention(c, QT, KT, Vext, outT, nq_heads, kv_group, scale, scratch_bf,
                   negc_cols, recip_bf, recip_tmp, KTflip=None):
    nc = c.nc
    sp, sv, sa = (c.sem(n) for n in ["sp", "sv", "sa"])
    PS_A, PS_B = c.PS_A, c.PS_B
    nkvh = nq_heads // kv_group
    vcols = nkvh * 65

    heads = list(range(nq_heads)) if DEBUG_HEADS is None else list(DEBUG_HEADS)

    def hparams(h):
        hc, hb = h // 2, 64 * (h % 2)
        g = h // kv_group
        kc = g // 2
        if 64 * (g % 2) == hb or KTflip is None:
            kbuf = KT
        else:
            kbuf = KTflip
        return hc, hb, g, kc, kbuf

    with nc.Block() as blk:
        emit_warmup(c, blk)
        seq = []
        for i, h in enumerate(heads):
            for kt in range(NT):
                if kt == 0:
                    if i > 0:
                        seq += [("AV", heads[i - 1], NT - 1, 0), ("AV", heads[i - 1], NT - 1, 1)]
                    seq += [("SC", h, 0, 0), ("SC", h, 0, 1)]
                    if i > 0:
                        seq += [("BC", heads[i - 1], 0, 0)]
                else:
                    seq += [("SC", h, kt, 0), ("SC", h, kt, 1),
                            ("AV", h, kt - 1, 0), ("AV", h, kt - 1, 1)]
        hl = heads[-1]
        seq += [("AV", hl, NT - 1, 0), ("AV", hl, NT - 1, 1), ("BC", hl, 0, 0)]
        P0 = sp.n
        pmark = {g: P0 + i + 1 for i, g in enumerate(seq)}
        E0 = sa.n
        emark = {}
        rmark = {}
        for i, h in enumerate(heads):
            for kt in range(NT):
                for hq in range(2):
                    emark[(h, kt, hq)] = E0 + i * (2 * NT + 2) + kt * 2 + hq + 1
            rmark[h] = E0 + (i + 1) * (2 * NT + 2)
        V0 = sv.n
        vmark_ev = {h: V0 + 2 * i + 1 for i, h in enumerate(heads)}
        vmark_mu = {h: V0 + 2 * i + 2 for i, h in enumerate(heads)}

        for i, h in enumerate(heads):
            @blk.scalar
            def _(e, h=h):
                for kt in range(NT):
                    for hq in range(2):
                        e.wait_ge(sp.h, pmark[("SC", h, kt, hq)])
                        if kt >= 2:
                            e.wait_ge(sp.h, pmark[("AV", h, kt - 2, hq)])
                        e.activation(scratch_bf[:, ((kt % 2) * 2 + hq) * 1024:
                                                ((kt % 2) * 2 + hq) * 1024 + 1024],
                                     PS_A[:, hq * 1024:(hq + 1) * 1024], AF.Exp,
                                     scale=scale, bias=negc_cols[:, h:h + 1]) \
                            .then_inc(sa.h, 1)
                e.wait_ge(sp.h, pmark[("AV", h, NT - 1, 1)])
                e.activation(recip_tmp, PS_B[64:65, 0:2048],
                             AF.Ln).then_inc(sa.h, 1)
                e.wait_ge(sa.h, rmark[h] - 1)
                e.activation(recip_bf, recip_tmp, AF.Exp,
                             scale=-1.0).then_inc(sa.h, 1)
                sa.n += 2 * NT + 2

        prevh = {heads[i]: heads[i - 1] for i in range(1, len(heads))}

        @blk.tensor
        def _(e):
            for g in seq:
                kind, h, kt, hq = g
                hc, hb, gg, kc, kbuf = hparams(h)
                if kind == "SC":
                    if kt == 0:
                        if h == heads[0]:
                            pass
                        else:
                            e.wait_ge(sa.h, emark[(prevh[h], NT - 1, hq)])
                    else:
                        e.wait_ge(sa.h, emark[(h, kt - 1, hq)])
                    last = None
                    for j in range(2):
                        last = e.matmul(
                            PS_A[:, hq * 1024 + j * 512:hq * 1024 + (j + 1) * 512],
                            kbuf[hb:hb + 64, kc * 2048 + kt * 128:kc * 2048 + (kt + 1) * 128],
                            QT[hb:hb + 64,
                               hc * 2048 + hq * 1024 + j * 512:hc * 2048 + hq * 1024 + (j + 1) * 512],
                            start=True, stop=True)
                    sp.inc(last)
                elif kind == "AV":
                    e.wait_ge(sa.h, emark[(h, kt, hq)])
                    if kt == 0 and hq == 0 and h != heads[0]:
                        e.wait_ge(sv.h, vmark_mu[prevh[h]])
                    vb = Vext[:, kt * vcols + (gg % nkvh) * 65:kt * vcols + (gg % nkvh) * 65 + 65]
                    pslot = ((kt % 2) * 2 + hq) * 1024
                    last = None
                    for j in range(2):
                        last = e.matmul(
                            PS_B[0:65, hq * 1024 + j * 512:hq * 1024 + (j + 1) * 512],
                            vb, scratch_bf[:, pslot + j * 512:pslot + (j + 1) * 512],
                            start=(kt == 0), stop=(kt == NT - 1))
                    sp.inc(last)
                else:
                    e.wait_ge(sv.h, vmark_ev[h])
                    e.wait_ge(sa.h, rmark[h])
                    last = None
                    for j in range(4):
                        last = e.matmul(PS_B[0:64, j * 512:(j + 1) * 512],
                                        c.CONESB[0:1, 0:64],
                                        recip_bf[:, j * 512:(j + 1) * 512],
                                        start=True, stop=True)
                    sp.inc(last)

        for i, h in enumerate(heads):
            @blk.vector
            def _(e, h=h):
                e.wait_ge(sp.h, pmark[("AV", h, NT - 1, 1)])
                sv.inc(e.tensor_copy(scratch_bf[0:64, 4096:6144], PS_B[0:64, 0:2048]))

            @blk.vector
            def _(e, h=h):
                hc, hb = h // 2, 64 * (h % 2)
                e.wait_ge(sp.h, pmark[("BC", h, 0, 0)])
                sv.inc(e.tensor_tensor(outT[hb:hb + 64, hc * 2048:(hc + 1) * 2048],
                                       scratch_bf[0:64, 4096:6144], PS_B[0:64, 0:2048],
                                       ALU.mult))


def emit_mamba(c):
    nc = c.nc
    sd, sg, sp, sv, sa = (c.sem(n) for n in ["sd", "sg", "sp", "sv", "sa"])
    sl = [c.sem("sl0"), c.sem("sl1")]
    ss = [c.sem("ss0"), c.sem("ss1")]
    W = c.d
    A1, A2, A3, A4, A5 = c.A1, c.A2, c.A3, c.A4, c.A5
    F1, F2, SM = c.F1, c.F2, c.SM
    F1b = c.F1.ap().bitcast(BF16)
    PS_A, PS_B = c.PS_A, c.PS_B
    pb = psab(c)
    pbb = psbb(c)

    with nc.Block() as blk:
        @blk.gpsimd
        def _(e):
            win = W["L1_attn_W_in"].ap().rearrange("(dc p) i -> p dc i", p=128)
            sg.inc(e.dma_start(
                out=A3.ap()[:, 0:12288].rearrange("p (dc i) -> p dc i", dc=8),
                in_=win[:, :, 0:1536]), 16)
            sg.inc(e.dma_start(
                out=A4.ap()[:, 0:12288].rearrange("p (dc i) -> p dc i", dc=8),
                in_=win[:, :, 1536:3072]), 16)
            sg.inc(e.dma_start(out=A2.ap()[0:16, 14016:15552],
                               in_=W["L1_attn_Bm"].ap()), 16)

        @blk.sync
        def _(e):
            sd.inc(e.dma_start(
                out=F2.ap()[:, 2752:2788].rearrange("p (ic j) -> p ic j", ic=12),
                in_=W["L1_attn_conv_w"].ap().rearrange("(ic p) j -> p ic j", p=128)), 16)
            sd.inc(e.dma_start(out=F2[0:16, 2704:2720], in_=W["L1_attn_A"].ap()), 16)
            sd.inc(e.dma_start(out=F2[0:1, 2944:3968],
                               in_=W["L1_attn_ln_w"].ap().rearrange("(a b) -> a b", a=1)), 16)
            sd.inc(e.dma_start(out=F2[0:1, 3100:3100] if False else F2[0:2, 3100:3100],
                               in_=W["L1_attn_ln_w"].ap().rearrange("(a b) -> a b", a=1)), 16) \
                if False else None
            sd.inc(e.dma_start(out=SM[0:1, 0:1024],
                               in_=W["L1_attn_ln_b"].ap().rearrange("(a b) -> a b", a=1)), 16)

        row_to_bcast(c, blk, SM[0:1, 0:1024], SM[:, 1024:2048], 1024,
                     waits=[(sd, sd.n)])
        row_to_bcast(c, blk, F2[0:1, 2944:3968], SM[:, 0:1024], 1024,
                     waits=[(sd, sd.n)])

        @blk.tensor
        def _(e):
            e.wait_ge(sg.h, sg.n)
            for ic in range(12):
                inst = e.transpose(pbb[:, 2048 + ic * 16:2048 + (ic + 1) * 16],
                                   A2[0:16, 14016 + ic * 128:14016 + (ic + 1) * 128],
                                   c.CIDENT[0:16, 0:16])
            sp.inc(inst)

        @blk.vector
        def _(e):
            e.wait_ge(sp.h, sp.n)
            sv.inc(e.tensor_copy(A2[:, 10240:10432], pbb[:, 2048:2240]))

    def xn1s(t):
        return F1[:, 2048 + (t % 4) * 1024:3072 + (t % 4) * 1024]

    hb = c.A2.ap()[:, 0:1024]
    with nc.Block() as blk:
        fronts = {}
        for i in range(NT + 3):
            if i < NT:
                slot = i % 2
                xt = F1[:, slot * 1024:(slot + 1) * 1024]

                @blk.sync
                def _(e, i=i, xt=xt, slot=slot):
                    e.wait_ge(ss[slot].h, ss[slot].n)
                    sl[slot].inc(e.dma_start(out=xt,
                                             in_=c.d["d_x1"].ap()[i * 128:(i + 1) * 128, :]), 16)

            jobs = []
            if i < NT:
                jobs.append(dict(xin=xt, outs=[xn1s(i)],
                                 stats=F2[:, 2790 + 8 * (i % 2):2794 + 8 * (i % 2)],
                                 scratch=F1[:, 6144:7168],
                                 waits=[(sl[slot], sl[slot].n), (ss[slot], ss[slot].n)]))
            if 2 <= i < NT + 2:
                t2 = i - 2
                jobs.append(dict(xin=xn1s(t2), outs=[hb],
                                 stats=F2[:, 2794 + 8 * (t2 % 2):2798 + 8 * (t2 % 2)],
                                 scratch=F2[:, (t2 % 2) * 1024:(t2 % 2) * 1024 + 1024],
                                 gam=SM[:, 0:1024], bet=SM[:, 1024:2048]))
            if jobs:
                fronts[i] = (jobs, ln_front(c, blk, jobs))
            if i - 1 in fronts:
                pj, pm = fronts.pop(i - 1)
                ln_back(c, blk, pj, pm)
                if i - 1 < NT:
                    @blk.gpsimd
                    def _(e, t=i - 1, slot=(i - 1) % 2):
                        e.wait_ge(sv.h, sv.n)
                        ss[slot].inc(e.dma_start(
                            out=c.d["d_xn1"].ap()[t * 128:(t + 1) * 128, :],
                            in_=xn1s(t)), 16)
                if 2 <= i - 1 < NT + 2:
                    transpose_to_T(c, blk, hb, A1, (i - 3) * 128, 8,
                                   psum_bank=(i - 3) % 2)
        if NT + 2 in fronts:
            pj, pm = fronts.pop(NT + 2)
            ln_back(c, blk, pj, pm)
            transpose_to_T(c, blk, hb, A1, (NT - 1) * 128, 8,
                           psum_bank=(NT - 1) % 2)

        @blk.gpsimd
        def _(e):
            e.wait_ge(ss[0].h, ss[0].n)
            e.wait_ge(ss[1].h, ss[1].n)

    def sg_slot(gc):
        if gc < 8:
            return (A3, gc * 2048)
        return (A2, 2048 + (gc - 8) * 2048)

    with nc.Block() as blk:
        last_read_mark = [None]
        for ic in range(24):
            data = ic < 12

            @blk.tensor
            def _(e, ic=ic):
                if last_read_mark[0] is not None:
                    eng, mark = last_read_mark[0]
                    e.wait_ge(eng.h, mark)
                src = A3 if ic < 12 else A4
                ii = ic % 12
                for j in range(4):
                    for dc in range(8):
                        sp.inc(e.matmul(
                            PS_A[:, j * 512:(j + 1) * 512],
                            src[:, dc * 1536 + ii * 128:dc * 1536 + (ii + 1) * 128],
                            A1[:, dc * 2048 + j * 512:dc * 2048 + (j + 1) * 512],
                            start=(dc == 0), stop=(dc == 7)))

            if data:
                @blk.vector
                def _(e, ic=ic):
                    e.wait_ge(sp.h, sp.n)
                    wj = lambda j: F2[:, 2752 + ic * 3 + j:2752 + ic * 3 + j + 1]
                    cv = F1[:, 4096:6144]
                    sv.inc(e.tensor_scalar(cv, PS_A[:, 0:2048], wj(2), None, ALU.mult))
                    e.wait_ge(sv.h, sv.n)
                    sv.inc(e.scalar_tensor_tensor(cv[:, 1:2048], PS_A[:, 0:2047], wj(1),
                                                  cv[:, 1:2048], ALU.mult, ALU.add))
                    e.wait_ge(sv.h, sv.n)
                    sv.inc(e.scalar_tensor_tensor(cv[:, 2:2048], PS_A[:, 0:2046], wj(0),
                                                  cv[:, 2:2048], ALU.mult, ALU.add))
                    last_read_mark[0] = (sv, sv.n)

                @blk.scalar
                def _(e):
                    e.wait_ge(sv.h, sv.n)
                    sa.inc(e.activation(A2[:, 0:2048], F1[:, 4096:6144], AF.Silu))

                @blk.tensor
                def _(e, ic=ic):
                    e.wait_ge(sa.h, sa.n)
                    for j in range(4):
                        sp.inc(e.matmul(
                            PS_B[0:16, j * 512:(j + 1) * 512],
                            A2[:, 10240 + ic * 16:10240 + (ic + 1) * 16],
                            A2[:, j * 512:(j + 1) * 512],
                            start=(ic == 0), stop=(ic == 11),
                            skip_group_check=True))
            else:
                @blk.scalar
                def _(e, ic=ic):
                    e.wait_ge(sp.h, sp.n)
                    gc = ic - 12
                    buf, off = sg_slot(gc)
                    sa.inc(e.activation(buf[:, off:off + 2048], PS_A[:, 0:2048],
                                        AF.Sigmoid))
                    last_read_mark[0] = (sa, sa.n)

            if ic == 11:
                @blk.vector
                def _(e):
                    e.wait_ge(sp.h, sp.n)
                    sv.inc(e.tensor_copy(F2[0:16, 0:2048], PS_B[0:16, 0:2048]))

    emit_scan(c)

    with nc.Block() as blk:
        @blk.gpsimd
        def _(e):
            sg.inc(e.dma_start(
                out=A4.ap()[:, 0:12288].rearrange("p (ic o) -> p ic o", ic=12),
                in_=W["L1_attn_W_out"].ap().rearrange("(ic p) o -> p ic o", p=128)), 16)
            sg.inc(e.dma_start(out=A2.ap()[0:16, 14016:14208],
                               in_=W["L1_attn_Cm"].ap().rearrange("(ic p) s -> p (ic s)", p=128)
                               if False else W["L1_attn_Cm"].ap()
                               .rearrange("(a b) s -> a (b s)", a=16) if False else
                               W["L1_attn_Cm"].ap()), 16) if False else None
            sg.inc(e.dma_start(
                out=A2.ap()[:, 15552:15744].rearrange("p (ic s) -> p ic s", ic=12),
                in_=W["L1_attn_Cm"].ap().rearrange("(ic p) s -> p ic s", p=128)), 16)

        @blk.tensor
        def _(e):
            e.wait_ge(sg.h, sg.n)
            for ic in range(12):
                inst = e.transpose(pbb[0:16, 1536 + ic * 128:1536 + (ic + 1) * 128],
                                   A2[:, 15552 + ic * 16:15552 + (ic + 1) * 16],
                                   c.CIDENT[:])
            sp.inc(inst)

        @blk.vector
        def _(e):
            e.wait_ge(sp.h, sp.n)
            sv.inc(e.tensor_copy(A2[0:16, 10432:11968], pbb[0:16, 1536:3072]))
            e.wait_ge(sv.h, sv.n)
            sv.inc(e.tensor_copy(A2[0:16, 11968:14016], F2[0:16, 0:2048]))

        def g_slot(gc):
            if gc == 0:
                return (A2, 0)
            if gc <= 8:
                return (A3, (gc - 1) * 2048)
            return (A2, 2048 + (gc - 9) * 2048)

        gmark = [None]
        for gc in range(12):
            @blk.tensor
            def _(e, gc=gc):
                e.wait_ge(sv.h, sv.n)
                if gmark[0] is not None:
                    e.wait_ge(sv.h, gmark[0])
                for j in range(4):
                    sp.inc(e.matmul(PS_A[:, j * 512:(j + 1) * 512],
                                    A2[0:16, 10432 + gc * 128:10432 + (gc + 1) * 128],
                                    A2[0:16, 11968 + j * 512:11968 + (j + 1) * 512],
                                    start=True, stop=True))

            @blk.vector
            def _(e, gc=gc):
                e.wait_ge(sp.h, sp.n)
                sbuf, soff = sg_slot(gc)
                gbuf, goff = g_slot(gc)
                sv.inc(e.tensor_tensor(gbuf[:, goff:goff + 2048],
                                       sbuf[:, soff:soff + 2048],
                                       PS_A[:, 0:2048], ALU.mult))
                gmark[0] = sv.n

        for t in range(NT):
            slot = t % 2
            xt = F1[:, slot * 1024:(slot + 1) * 1024]

            @blk.sync
            def _(e, t=t, xt=xt, slot=slot):
                e.wait_ge(ss[slot].h, ss[slot].n)
                sl[slot].inc(e.dma_start(out=xt,
                                         in_=c.d["d_xn1"].ap()[t * 128:(t + 1) * 128, :]), 16)

            @blk.tensor
            def _(e, t=t):
                e.wait_ge(sv.h, sv.n)
                ps = PS_B[:, (t % 2) * 1024:(t % 2) * 1024 + 1024]
                for ns in range(2):
                    for ic in range(12):
                        gbuf, goff = g_slot(ic)
                        sp.inc(e.matmul(
                            ps[:, ns * 512:(ns + 1) * 512],
                            gbuf[:, goff + t * 128:goff + (t + 1) * 128],
                            A4[:, ic * 1024 + ns * 512:ic * 1024 + (ns + 1) * 512],
                            start=(ic == 0), stop=(ic == 11)))

            @blk.vector
            def _(e, t=t, xt=xt, slot=slot):
                e.wait_ge(sp.h, sp.n)
                e.wait_ge(sl[slot].h, sl[slot].n)
                ps = PS_B[:, (t % 2) * 1024:(t % 2) * 1024 + 1024]
                sv.inc(e.scalar_tensor_tensor(xt, ps, 1.0, xt, ALU.mult, ALU.add))

            @blk.gpsimd
            def _(e, t=t, xt=xt, slot=slot):
                e.wait_ge(sv.h, sv.n)
                ss[slot].inc(e.dma_start(out=c.d["d_attn1"].ap()[t * 128:(t + 1) * 128, :],
                                         in_=xt), 16)

        @blk.gpsimd
        def _(e):
            e.wait_ge(ss[0].h, ss[0].n)
            e.wait_ge(ss[1].h, ss[1].n)


def emit_scan(c):
    nc = c.nc
    sg, sp, sv = (c.sem(n) for n in ["sg", "sp", "sv"])
    F1, F2 = c.F1, c.F2
    PS_A, PS_B = c.PS_A, c.PS_B
    IDF16 = c.CIDENTF[0:16, 0:16]

    with nc.Block() as blk:
        @blk.tensor
        def _(e):
            e.wait_ge(sv.h, sv.n)
            sp.inc(e.transpose(PS_B[0:16, 0:16], F2[0:16, 2704:2720], IDF16))

        @blk.vector
        def _(e):
            e.wait_ge(sp.h, sp.n)
            sv.inc(e.tensor_tensor(F2[0:16, 2064:2080], PS_B[0:16, 0:16],
                                   c.CIDENTF[0:16, 0:16], ALU.add))
            e.wait_ge(sv.h, sv.n)
            sv.inc(e.tensor_copy(F2[0:16, 2048:2064], c.CIDENTF[0:16, 0:16]))

        @blk.tensor
        def _(e):
            e.wait_ge(sv.h, sv.n)
            sp.inc(e.transpose(PS_B[0:16, 16:32], F2[0:16, 2064:2080], IDF16))

        @blk.vector
        def _(e):
            e.wait_ge(sp.h, sp.n)
            sv.inc(e.tensor_copy(F2[0:16, 2720:2736], PS_B[0:16, 16:32]))

        for j in range(2, 17):
            dst = (2048 + 16 * j) if j <= 15 else 2304 + 15 * 16
            @blk.tensor
            def _(e, j=j):
                e.wait_ge(sv.h, sv.n)
                sp.inc(e.matmul(PS_B[0:16, 32:48], F2[0:16, 2720:2736],
                                F2[0:16, 2048 + 16 * (j - 1):2048 + 16 * j],
                                start=True, stop=True))

            @blk.vector
            def _(e, dst=dst):
                e.wait_ge(sp.h, sp.n)
                sv.inc(e.tensor_copy(F2[0:16, dst:dst + 16], PS_B[0:16, 32:48]))

        @blk.vector
        def _(e):
            e.wait_ge(sv.h, sv.n)
            sv.inc(e.tensor_copy(F2[0:16, 2304:2544], F2[0:16, 2064:2304]))

        @blk.tensor
        def _(e):
            e.wait_ge(sv.h, sv.n)
            sp.inc(e.transpose(PS_B[0:16, 48:64], F2[0:16, 2544:2560], IDF16))

        @blk.vector
        def _(e):
            e.wait_ge(sp.h, sp.n)
            sv.inc(e.tensor_copy(F2[0:16, 2736:2752], PS_B[0:16, 48:64]))
            e.wait_ge(sv.h, sv.n)
            e.tensor_copy(F2[0:16, 2560:2576], c.CIDENTF[0:16, 0:16])
            sv.inc(e.tensor_copy(F2[0:16, 2576:2592], F2[0:16, 2544:2560]))
        for j in range(2, 9):
            dst = (2560 + 16 * j) if j <= 7 else 2688
            @blk.tensor
            def _(e, j=j):
                e.wait_ge(sv.h, sv.n)
                sp.inc(e.matmul(PS_B[0:16, 32:48], F2[0:16, 2736:2752],
                                F2[0:16, 2560 + 16 * (j - 1):2560 + 16 * j],
                                start=True, stop=True))

            @blk.vector
            def _(e, dst=dst):
                e.wait_ge(sp.h, sp.n)
                sv.inc(e.tensor_copy(F2[0:16, dst:dst + 16], PS_B[0:16, 32:48]))

        @blk.vector
        def _(e):
            e.wait_ge(sv.h, sv.n)
            sv.inc(e.memset(F1[:, 4096:5008], 0.0))

        @blk.gpsimd
        def _(e):
            e.wait_ge(sv.h, sv.n)
            for k in range(2):
                for il in range(8):
                    i = 8 * k + il
                    w = (16 - i) * 16
                    sg.inc(e.dma_start(
                        out=F1.ap()[16 * il:16 * (il + 1),
                                    4096 + k * 256 + i * 16:4096 + k * 256 + 256],
                        in_=F2.ap()[0:16, 2048:2048 + w]), 16)
            for il in range(8):
                w = (8 - il) * 16
                sg.inc(e.dma_start(
                    out=F1.ap()[16 * il:16 * (il + 1),
                                4864 + il * 16:4864 + 128],
                    in_=F2.ap()[0:16, 2560:2560 + w]), 16)
            for k in range(2):
                for il in range(8):
                    sg.inc(e.dma_start(
                        out=F1.ap()[16 * il:16 * (il + 1),
                                    4608 + k * 128:4608 + k * 128 + 128],
                        in_=bass.AP(c.F2, 8 * k + il, [[4096, 16], [16, 128]]),
                        allow_slow_non_contiguous=True), 16)

        @blk.tensor
        def _(e):
            e.wait_ge(sg.h, sg.n)
            e.matmul(PS_A[:, 0:256], F1[:, 4608:4736], F1[:, 4096:4352],
                     start=True, stop=False)
            sp.inc(e.matmul(PS_A[:, 0:256], F1[:, 4736:4864], F1[:, 4352:4608],
                            start=False, stop=True))

        @blk.vector
        def _(e):
            e.wait_ge(sp.h, sp.n)
            sv.inc(e.tensor_copy(F1[:, 5008:5024], PS_A[:, 240:256]))

        @blk.tensor
        def _(e):
            e.wait_ge(sv.h, sv.n)
            sp.inc(e.transpose(PS_B[0:16, 64:192], F1[:, 5008:5024], c.CIDENTF[:]))

        @blk.vector
        def _(e):
            e.wait_ge(sp.h, sp.n)
            sv.inc(e.tensor_copy(F1[0:16, 5024:5152], PS_B[0:16, 64:192]))

        @blk.gpsimd
        def _(e):
            e.wait_ge(sv.h, sv.n)
            for il in range(8):
                sg.inc(e.dma_start(
                    out=F1.ap()[16 * il:16 * (il + 1), 4992:5008],
                    in_=bass.AP(c.F1, 5024 + il, [[8192, 16], [8, 16]]),
                    allow_slow_non_contiguous=True), 16)

        @blk.tensor
        def _(e):
            e.wait_ge(sg.h, sg.n)
            sp.inc(e.matmul(PS_B[0:16, 512:640], F1[:, 4992:5008], F1[:, 4864:4992],
                            start=True, stop=True))

        @blk.vector
        def _(e):
            e.wait_ge(sp.h, sp.n)
            sv.inc(e.tensor_copy(F1[0:16, 5168:5296], PS_B[0:16, 512:640]))

        @blk.tensor
        def _(e):
            e.wait_ge(sv.h, sv.n)
            sp.inc(e.transpose(PS_B[0:16, 640:656], F1[0:16, 5280:5296], IDF16))

        @blk.vector
        def _(e):
            e.wait_ge(sp.h, sp.n)
            sv.inc(e.tensor_copy(F1[0:16, 5296:5312], PS_B[0:16, 640:656]))
            e.wait_ge(sv.h, sv.n)
            sv.inc(e.memset(F1[0:16, 5312:5313], 0.0))

        for C in range(1, 16):
            @blk.tensor
            def _(e, C=C):
                e.wait_ge(sv.h, sv.n)
                sp.inc(e.matmul(PS_B[0:16, 656:657], F2[0:16, 2688:2704],
                                F1[0:16, 5312 + C - 1:5312 + C], start=True, stop=True))

            @blk.vector
            def _(e, C=C):
                e.wait_ge(sp.h, sp.n)
                sv.inc(e.tensor_tensor(F1[0:16, 5312 + C:5313 + C],
                                       PS_B[0:16, 656:657],
                                       F1[0:16, 5296 + C - 1:5297 + C - 1] if False else
                                       F1[0:16, 5296 + (C - 1):5296 + C], ALU.add))

        for I in range(8):
            @blk.tensor
            def _(e, I=I):
                e.wait_ge(sv.h, sv.n)
                sp.inc(e.matmul(PS_B[0:16, 672:688], F2[0:16, 2560 + 16 * I:2576 + 16 * I],
                                F1[0:16, 5312:5328], start=True, stop=True))

            @blk.vector
            def _(e, I=I):
                e.wait_ge(sp.h, sp.n)
                dst = bass.AP(c.F1, 5328 + I, [[8192, 16], [8, 16]])
                sv.inc(e.tensor_copy(dst, PS_B[0:16, 672:688]))
        for I in range(1, 8):
            @blk.tensor
            def _(e, I=I):
                e.wait_ge(sv.h, sv.n)
                sp.inc(e.transpose(PS_B[0:16, 688:704],
                                   F1[0:16, 5168 + (I - 1) * 16:5168 + I * 16], IDF16))

            @blk.vector
            def _(e, I=I):
                e.wait_ge(sp.h, sp.n)
                dst = bass.AP(c.F1, 5328 + I, [[8192, 16], [8, 16]])
                sv.inc(e.tensor_tensor(dst, dst, PS_B[0:16, 688:704], ALU.add))

        @blk.tensor
        def _(e):
            e.wait_ge(sv.h, sv.n)
            sp.inc(e.matmul(PS_A[:, 0:256], F1[0:16, 5328:5456], F2[0:16, 2304:2560],
                            start=False, stop=True, skip_group_check=True))

        @blk.vector
        def _(e):
            e.wait_ge(sp.h, sp.n)
            sv.inc(e.tensor_copy(F1[:, 5456:5712], PS_A[:, 0:256]))
        for j in range(16):
            @blk.tensor
            def _(e, j=j):
                e.wait_ge(sv.h, sv.n)
                sp.inc(e.transpose(PS_B[0:16, 704:832],
                                   F1[:, 5456 + j * 16:5456 + (j + 1) * 16], c.CIDENTF[:]))

            @blk.vector
            def _(e, j=j):
                e.wait_ge(sp.h, sp.n)
                dst = bass.AP(c.F2, j, [[4096, 16], [16, 128]])
                sv.inc(e.tensor_copy(dst, PS_B[0:16, 704:832]))


def rms_tile(c, blk, xin, out, wbcast, stats, scratch, eps_col=41, waits=()):
    sv, sa = c.sem("sv"), c.sem("sa")
    var, rstd = stats[:, 0:1], stats[:, 1:2]
    sv_entry = sv.n

    @blk.vector
    def _(e):
        for (s, v) in waits:
            e.wait_ge(s.h, v)
        e.wait_ge(sv.h, sv_entry)
        sv.inc(e.tensor_tensor(scratch, xin, xin, ALU.mult))
        e.wait_ge(sv.h, sv.n)
        sv.inc(e.reduce_sum(var, scratch, AX.X))
        e.wait_ge(sv.h, sv.n)
        sv.inc(e.tensor_scalar(var, var, 1.0 / 1024.0, None, ALU.mult))

    @blk.scalar
    def _(e):
        e.wait_ge(sv.h, sv.n)
        sa.inc(e.activation(stats[:, 2:3], var, AF.Ln, bias=c.CMISC[:, eps_col:eps_col + 1]))
        e.wait_ge(sa.h, sa.n)
        sa.inc(e.activation(rstd, stats[:, 2:3], AF.Exp, scale=-0.5))

    @blk.vector
    def _(e):
        e.wait_ge(sa.h, sa.n)
        sv.inc(e.tensor_scalar(out, xin, rstd, None, ALU.mult))
        e.wait_ge(sv.h, sv.n)
        sv.inc(e.tensor_tensor(out, out, wbcast, ALU.mult))


def emit_gqa(c):
    nc = c.nc
    sd, sg, sp, sv, sa = (c.sem(n) for n in ["sd", "sg", "sp", "sv", "sa"])
    sl = [c.sem("sl0"), c.sem("sl1")]
    ss = [c.sem("ss0"), c.sem("ss1")]
    W = c.d
    A1, A2, A3, A4, A5 = c.A1, c.A2, c.A3, c.A4, c.A5
    F1, F2, SM = c.F1, c.F2, c.SM
    F1b = c.F1.ap().bitcast(BF16)
    PS_A, PS_B = c.PS_A, c.PS_B

    with nc.Block() as blk:
        @blk.gpsimd
        def _(e):
            sg.inc(e.dma_start(
                out=A2.ap()[:, 0:8192].rearrange("p (dc ee) -> p dc ee", dc=8),
                in_=W["L1_ffn_qw"].ap().rearrange("(dc p) ee -> p dc ee", p=128)), 16)
            sg.inc(e.dma_start(
                out=A2.ap()[:, 8192:16384].rearrange("p (dc ee) -> p dc ee", dc=8),
                in_=W["L1_ffn_ow"].ap().rearrange("(dc p) ee -> p dc ee", p=128)), 16)
            sg.inc(e.dma_start(
                out=A5.ap()[:, 0:2048].rearrange("p (dc ee) -> p dc ee", dc=8),
                in_=W["L1_ffn_kw"].ap().rearrange("(dc p) ee -> p dc ee", p=128)), 16)
            sg.inc(e.dma_start(
                out=A5.ap()[:, 2048:4096].rearrange("p (dc ee) -> p dc ee", dc=8),
                in_=W["L1_ffn_vw"].ap().rearrange("(dc p) ee -> p dc ee", p=128)), 16)

        @blk.sync
        def _(e):
            sd.inc(e.dma_start(out=bass.AP(c.SM, 1024, [[2048, 8], [1, 128]]),
                               in_=W["L1_ffn_qb"].ap().rearrange("(cc p) -> cc p", p=128)), 16)
            sd.inc(e.dma_start(out=bass.AP(c.SM, 1152, [[2048, 2], [1, 128]]),
                               in_=W["L1_ffn_kb"].ap().rearrange("(cc p) -> cc p", p=128)), 16)
            sd.inc(e.dma_start(out=SM[0:1, 0:256],
                               in_=W["L1_ffn_vb"].ap().rearrange("(a b) -> a b", a=1)), 16)
            sd.inc(e.dma_start(out=F2[0:1, 2944:3968],
                               in_=W["L1_ffn_norm1_w"].ap().rearrange("(a b) -> a b", a=1)), 16)

        @blk.tensor
        def _(e):
            e.wait_ge(sd.h, sd.n)
            e.wait_ge(sv.h, sv.n)
            e.transpose(PS_B[:, 0:8], bass.AP(c.SM, 1024, [[2048, 8], [1, 128]]),
                        c.CIDENTF[0:8, 0:8])
            sp.inc(e.transpose(PS_B[:, 8:10], bass.AP(c.SM, 1152, [[2048, 2], [1, 128]]),
                               c.CIDENTF[0:2, 0:2]))

        @blk.vector
        def _(e):
            e.wait_ge(sp.h, sp.n)
            e.tensor_copy(F2[:, 3970:3978], PS_B[:, 0:8])
            sv.inc(e.tensor_copy(F2[:, 3978:3980], PS_B[:, 8:10]))

        row_to_bcast(c, blk, F2[0:1, 2944:3968], SM[:, 1024:2048], 1024,
                     waits=[(sd, sd.n)])

    if GQA_MAX_PHASE < 1:
        return
    with nc.Block() as blk:
        for t in range(NT):
            slot = t % 2
            xt = F1[:, slot * 1024:(slot + 1) * 1024]

            @blk.sync
            def _(e, t=t, xt=xt, slot=slot):
                if t >= 2:
                    e.wait_ge(sv.h, sv.n)
                sl[slot].inc(e.dma_start(out=xt,
                                         in_=c.d["d_xn1"].ap()[t * 128:(t + 1) * 128, :]), 16)

            @blk.vector
            def _(e, xt=xt, slot=slot):
                e.wait_ge(sl[slot].h, sl[slot].n)
                e.wait_ge(sv.h, sv.n)
                sv.inc(e.tensor_copy(F1b[:, 10240:11264], xt))

            transpose_to_T(c, blk, F1b[:, 10240:11264], A1, t * 128, 8, psum_bank=t % 2)

    if GQA_MAX_PHASE < 2:
        return
    emit_qkv(c, xnT=A1.ap(), wq=(A2, 0), wk=(A5, 0), wv=(A5, 2048),
             QT=A4.ap(), KT=F1b[:, 0:4096], Vext=F1b[:, 8192:12352],
             bq_cols=F2[:, 3970:3978], bk_cols=F2[:, 3978:3980],
             bv_row=SM[0:1, 0:256], bvb=SM[:, 0:256], nk_e=256, nv=256,
             entry_waits=[(sg, sg.n)])

    with nc.Block() as blk:
        @blk.vector
        def _(e):
            for ch in range(2):
                e.tensor_copy(F1b[64:128, 4096 + ch * 2048:4096 + (ch + 1) * 2048],
                              F1b[0:64, ch * 2048:(ch + 1) * 2048])
                sv.inc(e.tensor_copy(F1b[0:64, 4096 + ch * 2048:4096 + (ch + 1) * 2048],
                                     F1b[64:128, ch * 2048:(ch + 1) * 2048]))

    if GQA_MAX_PHASE < 3:
        return
    emit_norm_bounds(c, QT=A4.ap(), KT=F1b[:, 0:4096], nq_heads=16, nkv=4,
                     kv_group=4, scale=0.125, sq_scratch=A5[:, 4096:6144],
                     negc_cols=F2[:, 3980:3996])
    emit_attention(c, QT=A4.ap(), KT=F1b[:, 0:4096], Vext=F1b[:, 8192:12352],
                   outT=A3.ap(), nq_heads=16, kv_group=4, scale=0.125,
                   scratch_bf=A5.ap(), negc_cols=F2[:, 3980:3996],
                   recip_bf=F2.ap().bitcast(BF16)[0:1, 0:2048],
                   recip_tmp=F2[0:1, 1024:3072], KTflip=F1b[:, 4096:8192])

    if GQA_MAX_PHASE < 5:
        return
    with nc.Block() as blk:
        @blk.sync
        def _(e):
            sd.inc(e.dma_start(out=SM[0:1, 0:1024],
                               in_=W["L1_ffn_ob"].ap().rearrange("(a b) -> a b", a=1)), 16)

        row_to_bcast(c, blk, SM[0:1, 0:1024], SM[:, 0:1024], 1024,
                     waits=[(sd, sd.n)])

        sb = [c.sem("sgg0"), c.sem("sgg1")]
        evmark = {}
        for t in range(NT):
            slot = t % 2
            xt = F1[:, slot * 1024:(slot + 1) * 1024]

            @blk.sync
            def _(e, t=t, xt=xt, slot=slot):
                e.wait_ge(ss[slot].h, ss[slot].n)
                sl[slot].inc(e.dma_start(out=xt,
                                         in_=c.d["d_xn1"].ap()[t * 128:(t + 1) * 128, :]), 16)

            @blk.tensor
            def _(e, t=t):
                if t >= 2:
                    e.wait_ge(sv.h, evmark[t - 2])
                ps = PS_A[:, (t % 2) * 1024:(t % 2) * 1024 + 1024]
                for ns in range(2):
                    for dc in range(8):
                        sp.inc(e.matmul(
                            ps[:, ns * 512:(ns + 1) * 512],
                            A3[:, dc * 2048 + t * 128:dc * 2048 + (t + 1) * 128],
                            A2[:, 8192 + dc * 1024 + ns * 512:8192 + dc * 1024 + (ns + 1) * 512],
                            start=(dc == 0), stop=(dc == 7)))

            @blk.vector
            def _(e, t=t, xt=xt, slot=slot):
                e.wait_ge(sp.h, sp.n)
                e.wait_ge(sl[slot].h, sl[slot].n)
                ps = PS_A[:, (t % 2) * 1024:(t % 2) * 1024 + 1024]
                sv.inc(e.scalar_tensor_tensor(xt, ps, 1.0, xt, ALU.mult, ALU.add))
                evmark[t] = sv.n
                e.wait_ge(sv.h, sv.n)
                sv.inc(e.tensor_tensor(xt, xt, SM[:, 0:1024], ALU.add))

            rms_tile(c, blk, xt, xt, SM[:, 1024:2048], F2[:, 2790:2794],
                     F1[:, 4096:5120])

            @blk.vector
            def _(e, t=t, xt=xt, slot=slot):
                e.wait_ge(sv.h, sv.n)
                if t >= 2:
                    e.wait_ge(sb[slot].h, sb[slot].n)
                sv.inc(e.tensor_copy(F1.ap().bitcast(BF16)[:, 10240 + slot * 1024:11264 + slot * 1024], xt))

            @blk.gpsimd
            def _(e, t=t, xt=xt, slot=slot):
                e.wait_ge(sv.h, sv.n)
                sb[slot].inc(e.dma_start(out=c.d["d_srcb"].ap()[t * 128:(t + 1) * 128, :],
                                         in_=F1.ap().bitcast(BF16)[:, 10240 + slot * 1024:11264 + slot * 1024]), 16)
                ss[slot].inc(e.dma_start(out=c.d["d_src"].ap()[t * 128:(t + 1) * 128, :],
                                         in_=xt), 16)

        @blk.gpsimd
        def _(e):
            e.wait_ge(ss[0].h, ss[0].n)
            e.wait_ge(ss[1].h, ss[1].n)
            e.wait_ge(sb[0].h, sb[0].n)
            e.wait_ge(sb[1].h, sb[1].n)


def emit_moe(c):
    nc = c.nc
    sd, sg, sp, sv, sa = (c.sem(n) for n in ["sd", "sg", "sp", "sv", "sa"])
    sl = [c.sem("sl0"), c.sem("sl1")]
    ss = [c.sem("ss0"), c.sem("ss1")]
    sge = [c.sem("sge0"), c.sem("sge1")]
    scc = c.sem("scc")
    W = c.d
    A1, A2, A3, A4, A5 = c.A1, c.A2, c.A3, c.A4, c.A5
    F1, F2, SM = c.F1, c.F2, c.SM
    F1b = c.F1.ap().bitcast(BF16)
    F2b = c.F2.ap().bitcast(BF16)
    PS_A, PS_B = c.PS_A, c.PS_B
    SI = c.SI

    BIG = 8.0 * S_MAX

    with nc.Block() as blk:
        @blk.gpsimd
        def _(e):
            sg.inc(e.dma_start(
                out=A5.ap()[:, 7168:7232].rearrange("p (dc ee) -> p dc ee", dc=8),
                in_=W["L1_ffn_gate_w"].ap().rearrange("(dc p) ee -> p dc ee", p=128)), 16)

        @blk.sync
        def _(e):
            sd.inc(e.dma_start(out=F2[0:1, 640:1664],
                               in_=W["L1_ffn_norm2_w"].ap().rearrange("(a b) -> a b", a=1)), 16)
            sd.inc(e.dma_start(out=F2[0:NCORES, 2700:2701], in_=c.d["c_prefmask"].ap()), 16)

        row_to_bcast(c, blk, F2[0:1, 640:1664], SM[:, 1024:2048], 1024,
                     waits=[(sd, sd.n)])

        @blk.vector
        def _(e):
            e.wait_ge(sv.h, sv.n)
            e.memset(F1[:, 0:1024], 0.0)
            e.memset(F2[0:1, 400:408], 0.0)
            e.memset(F2[0:1, 424:432], 0.0)
            sv.inc(e.tensor_scalar(F2[:, 248:256], c.CMISC[:, 0:8], 999.0, None,
                                   ALU.subtract))
            e.wait_ge(sv.h, sv.n)
            sv.inc(e.memset(F2[:, 620:624], 0.0))
            e.wait_ge(sv.h, sv.n)
            sv.inc(e.memset(F2[:, 622:623], 2.0 * T))
            e.wait_ge(sv.h, sv.n)
            e.tensor_copy(bass.AP(F2, 1538, [[4096, 128], [4, 32], [0, 1]]),
                          bass.AP(c.CMISC, 8, [[48, 128], [1, 32], [0, 1]]))
            e.tensor_scalar(bass.AP(F2, 1536, [[4096, 128], [8, 16], [0, 1]]),
                            bass.AP(c.CMISC, 8, [[48, 128], [2, 16], [0, 1]]),
                            0.5, None, ALU.mult)
            e.tensor_scalar(bass.AP(F2, 1540, [[4096, 128], [8, 16], [0, 1]]),
                            bass.AP(c.CMISC, 9, [[48, 128], [2, 16], [0, 1]]),
                            0.5, -0.5, ALU.mult, ALU.add)
            sv.inc(e.memset(bass.AP(F2, 1539, [[4096, 128], [4, 32], [0, 1]]), 0.0))

        @blk.gpsimd
        def _(e):
            e.wait_ge(sv.h, sv.n)
            for i in range(32):
                sg.inc(e.dma_start(out=c.d["d_moeout"].ap()[i * 128:(i + 1) * 128, :],
                                   in_=F1[:, 0:1024]), 16)
            for i in range(8 * S_MAX // 128):
                sg.inc(e.dma_start(out=c.d["d_slotmeta"].ap()[i * 128:(i + 1) * 128, :],
                                   in_=F2[:, 620:624]), 16)

    with nc.Block() as blk:
        for t in range(NT):
            slot = t % 2
            xt = F1[:, 1024 + slot * 1024:2048 + slot * 1024]

            @blk.sync
            def _(e, t=t, xt=xt, slot=slot):
                if t >= 2:
                    e.wait_ge(sv.h, sv.n)
                sl[slot].inc(e.dma_start(out=xt,
                                         in_=c.d["d_src"].ap()[t * 128:(t + 1) * 128, :]), 16)

            @blk.vector
            def _(e, xt=xt, slot=slot):
                e.wait_ge(sl[slot].h, sl[slot].n)
                e.wait_ge(sv.h, sv.n)
                sv.inc(e.tensor_copy(F1b[:, 14336:15360], xt))

            transpose_to_T(c, blk, F1b[:, 14336:15360], A1, t * 128, 8, psum_bank=t % 2)

        @blk.tensor
        def _(e):
            e.wait_ge(sv.h, sv.n)
            e.wait_ge(sg.h, sg.n)
            for dc in range(8):
                for j in range(4):
                    sp.inc(e.matmul(PS_B[0:8, j * 512:(j + 1) * 512],
                                    A5[:, 7168 + dc * 8:7168 + (dc + 1) * 8],
                                    A1[:, dc * 2048 + j * 512:dc * 2048 + (j + 1) * 512],
                                    start=(dc == 0), stop=(dc == 7)))

        A5f = c.A5.ap().bitcast(F32)
        @blk.vector
        def _(e):
            e.wait_ge(sp.h, sp.n)
            sv.inc(e.tensor_copy(A5f[0:8, 0:2048], PS_B[0:8, 0:2048]))

        for tl in range(NT):
            @blk.tensor
            def _(e, tl=tl):
                e.wait_ge(sv.h, sv.n)
                sp.inc(e.transpose(PS_A[0:128, 8 * (tl % 2):8 * (tl % 2) + 8],
                                   A5f[0:8, tl * 128:(tl + 1) * 128],
                                   c.CIDENTF[0:8, 0:8]))

            @blk.vector
            def _(e, tl=tl):
                e.wait_ge(sp.h, sp.n)
                sv.inc(e.tensor_copy(F2[:, 8 * tl:8 * tl + 8],
                                     PS_A[:, 8 * (tl % 2):8 * (tl % 2) + 8]))

    def slab(e_idx, hh, which):
        buf = (A1 if which == 0 else A2) if e_idx % 2 == 0 else (A3 if which == 0 else A4)
        return buf, hh * 8192

    sgg = [c.sem("sgg0"), c.sem("sgg1")]
    sgs = [c.sem("sgs0"), c.sem("sgs1")]
    smm = [c.sem("smm0"), c.sem("smm1")]

    def emit_slab_loads(e, e_idx):
        pe_par = e_idx % 2
        for hh in range(2):
            b1, o1 = slab(e_idx, hh, 0)
            sge[pe_par].inc(e.dma_start(
                out=b1.ap()[:, o1:o1 + 8192].rearrange("p (dc h) -> p dc h", dc=8),
                in_=W["L1_ffn_w1"].ap()[e_idx].rearrange("(dc p) h -> p dc h", p=128)
                [:, :, hh * 1024:(hh + 1) * 1024]), 16)
            b2, o2 = slab(e_idx, hh, 1)
            sge[pe_par].inc(e.dma_start(
                out=b2.ap()[:, o2:o2 + 8192].rearrange("p (hc d) -> p hc d", hc=8),
                in_=W["L1_ffn_w2"].ap()[e_idx].rearrange("(hc p) d -> p hc d", p=128)
                [:, hh * 8:(hh + 1) * 8, :]), 16)

    def blk3(buf, w, off, n1=16, n2=8):
        return bass.AP(buf, off, [[w, 128], [n2, n1], [1, n2]])

    def exp3i(buf, w, off, n1=16, n2=8):
        return bass.AP(buf, off, [[w, 128], [1, n1], [0, n2]])

    def exp3o(buf, w, off, n1=16, n2=8):
        return bass.AP(buf, off, [[w, 128], [0, n1], [1, n2]])

    SMw, F2w = 2048, 4096
    probs3 = blk3(SM, 2048, 0)
    with nc.Block() as blk:
        @blk.gpsimd
        def _(e):
            emit_slab_loads(e, 0)

        @blk.scalar
        def _(e):
            e.wait_ge(sv.h, sv.n)
            sa.inc(e.activation(F2[:, 0:128], F2[:, 0:128], AF.Sigmoid))

        pr3 = blk3(F2, F2w, 0)
        t83 = blk3(F2, F2w, 1024)
        io3 = exp3o(F2, F2w, 248)
        i83 = exp3o(c.CMISC, 48, 0)

        @blk.vector
        def _(e):
            e.wait_ge(sa.h, sa.n)
            sv.inc(e.tensor_reduce(blk3(SM, SMw, 256, 16, 1), pr3, AX.X, ALU.max))
            e.wait_ge(sv.h, sv.n)
            sv.inc(e.tensor_tensor(t83, pr3, exp3i(SM, SMw, 256), ALU.is_ge))
            e.wait_ge(sv.h, sv.n)
            sv.inc(e.tensor_tensor(t83, t83, io3, ALU.mult))
            e.wait_ge(sv.h, sv.n)
            sv.inc(e.tensor_scalar(F2[:, 1024:1152], F2[:, 1024:1152], 999.0,
                                   None, ALU.add))
            e.wait_ge(sv.h, sv.n)
            sv.inc(e.tensor_reduce(blk3(SM, SMw, 320, 16, 1), t83, AX.X, ALU.min))
            e.wait_ge(sv.h, sv.n)
            sv.inc(e.tensor_tensor(blk3(SM, SMw, 0), i83,
                                   exp3i(SM, SMw, 320), ALU.is_equal))
            e.wait_ge(sv.h, sv.n)
            sv.inc(e.scalar_tensor_tensor(F2[:, 1024:1152], SM[:, 0:128], -1e9,
                                          F2[:, 0:128], ALU.mult, ALU.add))
            e.wait_ge(sv.h, sv.n)
            sv.inc(e.tensor_reduce(blk3(SM, SMw, 272, 16, 1), t83, AX.X, ALU.max))
            e.wait_ge(sv.h, sv.n)
            sv.inc(e.tensor_tensor(t83, t83, exp3i(SM, SMw, 272), ALU.is_ge))
            e.wait_ge(sv.h, sv.n)
            sv.inc(e.tensor_tensor(t83, t83, io3, ALU.mult))
            e.wait_ge(sv.h, sv.n)
            sv.inc(e.tensor_scalar(F2[:, 1024:1152], F2[:, 1024:1152], 999.0,
                                   None, ALU.add))
            e.wait_ge(sv.h, sv.n)
            sv.inc(e.tensor_reduce(blk3(SM, SMw, 336, 16, 1), t83, AX.X, ALU.min))
            e.wait_ge(sv.h, sv.n)
            sv.inc(e.tensor_tensor(blk3(SM, SMw, 128), i83,
                                   exp3i(SM, SMw, 336), ALU.is_equal))
            e.wait_ge(sv.h, sv.n)
            sv.inc(e.tensor_tensor(SM[:, 288:304], SM[:, 256:272],
                                   SM[:, 272:288], ALU.add))
            e.wait_ge(sv.h, sv.n)
            sv.inc(e.tensor_scalar(SM[:, 288:304], SM[:, 288:304], 1e-6,
                                   None, ALU.add))
            e.wait_ge(sv.h, sv.n)
            sv.inc(e.reciprocal(SM[:, 304:320], SM[:, 288:304]))
            e.wait_ge(sv.h, sv.n)
            sv.inc(e.tensor_tensor(SM[:, 288:304], SM[:, 256:272],
                                   SM[:, 304:320], ALU.mult))
            e.wait_ge(sv.h, sv.n)
            sv.inc(e.tensor_tensor(SM[:, 304:320], SM[:, 272:288],
                                   SM[:, 304:320], ALU.mult))
            e.wait_ge(sv.h, sv.n)
            sv.inc(e.tensor_tensor(SM[:, 384:512], SM[:, 0:128],
                                   SM[:, 128:256], ALU.add))

        @blk.tensor
        def _(e):
            e.wait_ge(sv.h, sv.n)
            sp.inc(e.matmul(PS_A[0:1, 128:256], c.CMISC[:, 42:43],
                            SM[:, 384:512], start=True, stop=True))

        @blk.vector
        def _(e):
            e.wait_ge(sp.h, sp.n)
            sv.inc(e.tensor_copy(F2[0:1, 440:568], PS_A[0:1, 128:256]))
            e.wait_ge(sv.h, sv.n)
            sv.inc(e.tensor_reduce(bass.AP(F2, 400, [[F2w, 1], [1, 8], [0, 1]]),
                                   bass.AP(F2, 440, [[F2w, 1], [1, 8], [8, 16]]),
                                   AX.X, ALU.add))

    with nc.Block() as blk:
        @blk.gpsimd
        def _(e):
            e.wait_ge(c.sem("sv").h, c.sem("sv").n)
            sg.inc(e.dma_start(out=c.d["d_cin"].ap(), in_=F2[0:1, 400:408]), 16)
            e.wait_ge(sg.h, sg.n)
            e.collective_compute(
                "AllGather", ALU.bypass, replica_groups=[list(range(NCORES))],
                ins=[c.d["d_cin"].ap().opt()], outs=[c.d["d_cout"].ap().opt()],
            ).then_inc(scc.h, 1)
            scc.n += 1
            e.wait_ge(scc.h, scc.n)
            sg.inc(e.dma_start(out=F2[0:NCORES, 408:416], in_=c.d["d_cout"].ap()), 16)

        @blk.tensor
        def _(e):
            e.wait_ge(sg.h, sg.n)
            e.wait_ge(sd.h, sd.n)
            sp.inc(e.matmul(PS_A[0:1, 24:32], F2[0:NCORES, 2700:2701],
                            F2[0:NCORES, 408:416], start=True, stop=True))

        @blk.vector
        def _(e):
            e.wait_ge(sp.h, sp.n)
            sv.inc(e.tensor_copy(F2[0:1, 416:424], PS_A[0:1, 24:32]))

        @blk.tensor
        def _(e):
            e.wait_ge(sv.h, sv.n)
            sp.inc(e.matmul(PS_A[:, 32:40], c.CONES[0:1, 0:128], F2[0:1, 416:424],
                            start=True, stop=True))

        @blk.vector
        def _(e):
            e.wait_ge(sp.h, sp.n)
            sv.inc(e.tensor_copy(F2[:, 432:440], PS_A[:, 32:40]))

    with nc.Block() as blk:
        @blk.tensor
        def _(e):
            sp.inc(e.transpose(PS_A[:, 384:385], F2[0:1, 440:568],
                               c.CIDENTF[0:1, 0:1]))

        @blk.vector
        def _(e):
            e.wait_ge(sp.h, sp.n)
            sv.inc(e.tensor_copy(F2[:, 1408:1409], PS_A[:, 384:385]))

        @blk.tensor
        def _(e):
            e.wait_ge(sv.h, sv.n)
            sp.inc(e.matmul(PS_A[0:1, 256:384], F2[:, 1408:1409], c.CSU8[:],
                            start=True, stop=True))

        @blk.vector
        def _(e):
            e.wait_ge(sp.h, sp.n)
            sv.inc(e.tensor_copy(F2[0:1, 1152:1280], PS_A[0:1, 256:384]))

        @blk.tensor
        def _(e):
            e.wait_ge(sv.h, sv.n)
            e.matmul(PS_A[:, 0:128], c.CSU[:], SM[:, 384:512],
                     start=True, stop=False, skip_group_check=True)
            sp.inc(e.matmul(PS_A[:, 0:128], c.CONES[0:1, 0:128],
                            F2[0:1, 1152:1280], start=False, stop=True,
                            skip_group_check=True))

        t8f = F2[:, 1024:1152]
        cpb3 = exp3o(F2, F2w, 432)

        @blk.vector
        def _(e):
            e.wait_ge(sp.h, sp.n)
            sv.inc(e.tensor_tensor(t8f, PS_A[:, 0:128], SM[:, 0:128], ALU.mult))
            e.wait_ge(sv.h, sv.n)
            sv.inc(e.tensor_reduce(blk3(SM, SMw, 512, 16, 1),
                                   blk3(F2, F2w, 1024), AX.X, ALU.add))
            e.wait_ge(sv.h, sv.n)
            sv.inc(e.tensor_tensor(t8f, PS_A[:, 0:128], SM[:, 128:256], ALU.mult))
            e.wait_ge(sv.h, sv.n)
            sv.inc(e.tensor_reduce(blk3(SM, SMw, 528, 16, 1),
                                   blk3(F2, F2w, 1024), AX.X, ALU.add))
            e.wait_ge(sv.h, sv.n)
            sv.inc(e.tensor_tensor(blk3(F2, F2w, 1024), cpb3,
                                   blk3(SM, SMw, 0), ALU.mult))
            e.wait_ge(sv.h, sv.n)
            sv.inc(e.tensor_reduce(blk3(SM, SMw, 544, 16, 1),
                                   blk3(F2, F2w, 1024), AX.X, ALU.add))
            e.wait_ge(sv.h, sv.n)
            sv.inc(e.tensor_tensor(blk3(F2, F2w, 1024), cpb3,
                                   blk3(SM, SMw, 128), ALU.mult))
            e.wait_ge(sv.h, sv.n)
            sv.inc(e.tensor_reduce(blk3(SM, SMw, 560, 16, 1),
                                   blk3(F2, F2w, 1024), AX.X, ALU.add))
            e.wait_ge(sv.h, sv.n)
            e.tensor_tensor(SM[:, 576:592], SM[:, 512:528], SM[:, 544:560],
                            ALU.add)
            sv.inc(e.tensor_tensor(SM[:, 592:608], SM[:, 528:544],
                                   SM[:, 560:576], ALU.add))
            e.wait_ge(sv.h, sv.n)
            sv.inc(e.tensor_scalar(SM[:, 608:624], SM[:, 576:592],
                                   float(CAPACITY), None, ALU.is_lt))
            e.wait_ge(sv.h, sv.n)
            sv.inc(e.tensor_tensor(SM[:, 352:368], SM[:, 288:304],
                                   SM[:, 608:624], ALU.mult))
            e.wait_ge(sv.h, sv.n)
            sv.inc(e.tensor_scalar(SM[:, 608:624], SM[:, 592:608],
                                   float(CAPACITY), None, ALU.is_lt))
            e.wait_ge(sv.h, sv.n)
            sv.inc(e.tensor_tensor(SM[:, 368:384], SM[:, 304:320],
                                   SM[:, 608:624], ALU.mult))
            e.wait_ge(sv.h, sv.n)
            BIG = 8.0 * S_MAX
            e.scalar_tensor_tensor(SM[:, 624:640], SM[:, 320:336],
                                   float(S_MAX), SM[:, 512:528],
                                   ALU.mult, ALU.add)
            sv.inc(e.scalar_tensor_tensor(SM[:, 640:656], SM[:, 336:352],
                                          float(S_MAX), SM[:, 528:544],
                                          ALU.mult, ALU.add))
            e.wait_ge(sv.h, sv.n)
            e.tensor_scalar(SM[:, 656:672], SM[:, 512:528], float(S_MAX),
                            None, ALU.is_lt)
            sv.inc(e.tensor_scalar(SM[:, 608:624], SM[:, 528:544],
                                   float(S_MAX), None, ALU.is_lt))
            e.wait_ge(sv.h, sv.n)
            e.tensor_scalar(SM[:, 624:640], SM[:, 624:640], BIG, None,
                            ALU.subtract)
            sv.inc(e.tensor_scalar(SM[:, 640:656], SM[:, 640:656], BIG, None,
                                   ALU.subtract))
            e.wait_ge(sv.h, sv.n)
            e.scalar_tensor_tensor(SM[:, 624:640], SM[:, 624:640], 1.0,
                                   SM[:, 656:672], ALU.mult, ALU.mult)
            sv.inc(e.scalar_tensor_tensor(SM[:, 640:656], SM[:, 640:656], 1.0,
                                          SM[:, 608:624], ALU.mult, ALU.mult))
            e.wait_ge(sv.h, sv.n)
            e.tensor_scalar(SM[:, 624:640], SM[:, 624:640], BIG, None, ALU.add)
            sv.inc(e.tensor_scalar(SM[:, 640:656], SM[:, 640:656], BIG, None,
                                   ALU.add))
            e.wait_ge(sv.h, sv.n)
            e.tensor_copy(SI[:, 16:32], SM[:, 624:640])
            sv.inc(e.tensor_copy(SI[:, 32:48], SM[:, 640:656]))
            e.wait_ge(sv.h, sv.n)
            e.tensor_copy(bass.AP(F2, 1537, [[F2w, 128], [8, 16], [0, 1]]),
                          blk3(SM, SMw, 352, 16, 1))
            sv.inc(e.tensor_copy(bass.AP(F2, 1541, [[F2w, 128], [8, 16], [0, 1]]),
                                 blk3(SM, SMw, 368, 16, 1)))

        @blk.gpsimd
        def _(e):
            e.wait_ge(sv.h, sv.n)
            for s in range(32):
                tl, k = s // 2, s % 2
                if s > 0:
                    e.wait_ge(sg.h, sg.n)
                sg.inc(e.indirect_dma_start(
                    c.d["d_slotmeta"].ap(),
                    IndirectOffsetOnAxis(SI[:, 16 + 16 * k + tl:17 + 16 * k + tl], 0),
                    F2[:, 1536 + 4 * s:1540 + 4 * s], None), 16)
            e.wait_ge(sg.h, sg.n)

    marks = {}
    xgt_ready = {}
    mslot_mark = {}
    xgb_mark = {}
    prepev = {}
    NIT = 12

    def emit_prep(blk, e_next, st, meta_at, gather_at, tr_at, cur_e):
        mslot = st % 2
        mbuf = F2[:, 604 + mslot * 4:608 + mslot * 4]

        def meta(e, e_next=e_next, st=st, mslot=mslot, mbuf=mbuf):
            if mslot in mslot_mark:
                e.wait_ge(sv.h, mslot_mark[mslot])
            e.wait_ge(sg.h, sg.n)
            smm[mslot].inc(e.dma_start(
                out=mbuf, in_=c.d["d_slotmeta"].ap()
                [e_next * S_MAX + st * 128:e_next * S_MAX + (st + 1) * 128, :]), 16)

        def extract(e, e_next=e_next, st=st, mslot=mslot, mbuf=mbuf):
            e.wait_ge(smm[mslot].h, smm[mslot].n)
            if ("stt", e_next - 1, st) in marks:
                e.wait_ge(sv.h, marks[("stt", e_next - 1, st)])
            if ("scat", e_next - 1, st) in marks:
                e.wait_ge(sgs[st % 2].h, marks[("scat", e_next - 1, st)])
            e.tensor_copy(SI[:, 8 + st:9 + st], mbuf[:, 0:1])
            e.tensor_copy(F2[:, 624 + st:625 + st], mbuf[:, 1:2])
            marks[("mx", e_next, st)] = sv.inc(
                e.tensor_copy(SI[:, 2 + st:3 + st], mbuf[:, 2:3]))
            mslot_mark[mslot] = marks[("mx", e_next, st)]

        def gather(e, e_next=e_next, st=st):
            e.wait_ge(sv.h, marks[("mx", e_next, st)])
            gslot = st % 2
            if gslot in xgb_mark:
                e.wait_ge(sp.h, xgb_mark[gslot])
            sgg[gslot].inc(e.indirect_dma_start(
                F1b[:, gslot * 1024:(gslot + 1) * 1024], None,
                c.d["d_srcb"].ap(), IndirectOffsetOnAxis(SI[:, 8 + st:9 + st], 0)), 16)

        return meta, extract, gather

    with nc.Block() as blk:
        emit_warmup(c, blk)

        for st in range(6):
            meta, extract, gather = emit_prep(blk, 0, st, 0, 0, 0, -1)
            blk.sync(lambda e, meta=meta: meta(e))
            blk.vector(lambda e, extract=extract: extract(e))
            blk.gpsimd(lambda e, gather=gather: gather(e))

            @blk.tensor
            def _(e, st=st):
                e.wait_ge(sgg[st % 2].h, sgg[st % 2].n)
                last = None
                for dc in range(8):
                    last = e.transpose(psab(c)[:, 1024 + dc * 128:1024 + (dc + 1) * 128],
                                       F1b[:, (st % 2) * 1024 + dc * 128:(st % 2) * 1024 + (dc + 1) * 128],
                                       c.CIDENT[:])
                marks[("ptr", 0, st)] = sp.inc(last)
                xgb_mark[st % 2] = marks[("ptr", 0, st)]

            @blk.vector
            def _(e, st=st):
                e.wait_ge(sp.h, marks[("ptr", 0, st)])
                xgt_ready[(0, st)] = sv.inc(
                    e.tensor_copy(A5[:, st * 1024:(st + 1) * 1024], psab(c)[:, 1024:2048]))
                prepev[0] = xgt_ready[(0, st)]

        for e_idx in range(8):
            pe_par = e_idx % 2

            @blk.gpsimd
            def _(e, e_idx=e_idx):
                if e_idx + 1 < 8:
                    if ("mm2", e_idx - 1, NIT - 1) in marks:
                        e.wait_ge(sp.h, marks[("mm2", e_idx - 1, NIT - 1)])
                    emit_slab_loads(e, e_idx + 1)

            for it in range(NIT + 3):
                hh, st = divmod(it, 6) if it < NIT else (None, None)

                @blk.tensor
                def _(e, e_idx=e_idx, it=it, hh=hh, st=st, pe_par=pe_par):
                    half = it % 2
                    if it < NIT:
                        b1, o1 = slab(e_idx, hh, 0)
                        e.wait_ge(sge[pe_par].h, sge[pe_par].n)
                        e.wait_ge(sv.h, xgt_ready[(e_idx, st)])
                        if ("hbev", e_idx, it - 2) in marks:
                            e.wait_ge(sv.h, marks[("hbev", e_idx, it - 2)])
                        if half in prepev:
                            e.wait_ge(sv.h, prepev[half])
                        last = None
                        for ns in range(2):
                            for dc in range(8):
                                last = e.matmul(
                                    PS_A[:, half * 1024 + ns * 512:half * 1024 + (ns + 1) * 512],
                                    A5[:, st * 1024 + dc * 128:st * 1024 + (dc + 1) * 128],
                                    b1[:, o1 + dc * 1024 + ns * 512:o1 + dc * 1024 + (ns + 1) * 512],
                                    start=(dc == 0), stop=(dc == 7))
                        marks[("mm1", e_idx, it)] = sp.inc(last)
                    if 0 <= it - 1 < NIT:
                        e.wait_ge(sa.h, marks[("gelu", e_idx, it - 1)])
                        if (it - 1) % 2 in prepev:
                            e.wait_ge(sv.h, prepev[(it - 1) % 2])
                        h1 = (it - 1) % 2
                        last = None
                        for dc in range(8):
                            last = e.transpose(
                                psab(c)[:, h1 * 2048 + 1024 + dc * 128:h1 * 2048 + 1024 + (dc + 1) * 128],
                                F2b[:, 3328 + h1 * 1024 + dc * 128:3328 + h1 * 1024 + (dc + 1) * 128],
                                c.CIDENT[:])
                        marks[("trH", e_idx, it - 1)] = sp.inc(last)
                    if 0 <= it - 2 < NIT:
                        it2 = it - 2
                        hh2, st2 = divmod(it2, 6)
                        b2, o2 = slab(e_idx, hh2, 1)
                        h2 = it2 % 2
                        e.wait_ge(sv.h, marks[("hbev", e_idx, it2)])
                        if ("outev", e_idx, it2 - 2) in marks:
                            e.wait_ge(sv.h, marks[("outev", e_idx, it2 - 2)])
                        last = None
                        for ns in range(2):
                            for hcc in range(8):
                                last = e.matmul(
                                    PS_B[:, h2 * 1024 + ns * 512:h2 * 1024 + (ns + 1) * 512],
                                    A5[:, 6144 + h2 * 1024 + hcc * 128:6144 + h2 * 1024 + (hcc + 1) * 128],
                                    b2[:, o2 + hcc * 1024 + ns * 512:o2 + hcc * 1024 + (ns + 1) * 512],
                                    start=(hcc == 0), stop=(hcc == 7))
                        marks[("mm2", e_idx, it2)] = sp.inc(last)

                @blk.scalar
                def _(e, e_idx=e_idx, it=it, hh=hh, st=st):
                    if it < NIT:
                        half = it % 2
                        e.wait_ge(sp.h, marks[("mm1", e_idx, it)])
                        if ("trH", e_idx, it - 2) in marks:
                            e.wait_ge(sp.h, marks[("trH", e_idx, it - 2)])
                        e.activation(F2b[:, 3328 + half * 1024:3328 + half * 1024 + 512],
                                     PS_A[:, half * 1024:half * 1024 + 512], AF.Gelu)
                        marks[("gelu", e_idx, it)] = sa.inc(e.activation(
                            F2b[:, 3328 + half * 1024 + 512:3328 + half * 1024 + 1024],
                            PS_A[:, half * 1024 + 512:half * 1024 + 1024], AF.Gelu))

                if e_idx + 1 < 8 and 9 <= it < 15 and it - 9 < 6:
                    stp2 = it - 9

                    @blk.tensor
                    def _(e, e_idx=e_idx, it=it, stp2=stp2):
                        e.wait_ge(sgg[stp2 % 2].h, sgg[stp2 % 2].n)
                        e.wait_ge(sp.h, marks.get(("mm1", e_idx, 6 + stp2), 0))
                        if it < NIT:
                            e.wait_ge(sa.h, marks[("gelu", e_idx, it)])
                        half = it % 2
                        last = None
                        for dc in range(8):
                            last = e.transpose(
                                psab(c)[:, half * 2048 + 1024 + dc * 128:
                                        half * 2048 + 1024 + (dc + 1) * 128],
                                F1b[:, (stp2 % 2) * 1024 + dc * 128:(stp2 % 2) * 1024 + (dc + 1) * 128],
                                c.CIDENT[:])
                        marks[("ptr", e_idx + 1, stp2)] = sp.inc(last)
                        xgb_mark[stp2 % 2] = marks[("ptr", e_idx + 1, stp2)]

                    @blk.vector
                    def _(e, e_idx=e_idx, it=it, stp2=stp2):
                        half = it % 2
                        e.wait_ge(sp.h, marks[("ptr", e_idx + 1, stp2)])
                        xgt_ready[(e_idx + 1, stp2)] = sv.inc(e.tensor_copy(
                            A5[:, stp2 * 1024:(stp2 + 1) * 1024],
                            psab(c)[:, half * 2048 + 1024:half * 2048 + 2048]))
                        prepev[half] = xgt_ready[(e_idx + 1, stp2)]

                @blk.vector
                def _(e, e_idx=e_idx, it=it):
                    if 0 <= it - 1 < NIT:
                        h1 = (it - 1) % 2
                        e.wait_ge(sp.h, marks[("trH", e_idx, it - 1)])
                        if ("mm2", e_idx, it - 3) in marks:
                            e.wait_ge(sp.h, marks[("mm2", e_idx, it - 3)])
                        marks[("hbev", e_idx, it - 1)] = sv.inc(e.tensor_copy(
                            A5[:, 6144 + h1 * 1024:6144 + (h1 + 1) * 1024],
                            psab(c)[:, h1 * 2048 + 1024:h1 * 2048 + 2048]))
                    if 0 <= it - 2 < NIT:
                        it2 = it - 2
                        hh2, st2 = divmod(it2, 6)
                        h2 = it2 % 2
                        oacc = F1[:, 2048 + st2 * 1024:3072 + st2 * 1024]
                        e.wait_ge(sp.h, marks[("mm2", e_idx, it2)])
                        if hh2 == 0:
                            if ("scat", e_idx - 1, st2) in marks:
                                e.wait_ge(sgs[st2 % 2].h, marks[("scat", e_idx - 1, st2)])
                            marks[("outev", e_idx, it2)] = sv.inc(e.tensor_scalar(
                                oacc, PS_B[:, h2 * 1024:(h2 + 1) * 1024],
                                F2[:, 624 + st2:625 + st2], None, ALU.mult))
                        else:
                            marks[("stt", e_idx, st2)] = sv.inc(e.scalar_tensor_tensor(
                                oacc, PS_B[:, h2 * 1024:(h2 + 1) * 1024],
                                F2[:, 624 + st2:625 + st2], oacc, ALU.mult, ALU.add))
                            marks[("outev", e_idx, it2)] = marks[("stt", e_idx, st2)]

                @blk.gpsimd
                def _(e, e_idx=e_idx, it=it):
                    if 0 <= it - 2 < NIT:
                        it2 = it - 2
                        hh2, st2 = divmod(it2, 6)
                        if hh2 == 1:
                            e.wait_ge(sv.h, marks[("stt", e_idx, st2)])
                            e.wait_ge(sgs[0].h, sgs[0].n)
                            e.wait_ge(sgs[1].h, sgs[1].n)
                            marks[("scat", e_idx, st2)] = sgs[st2 % 2].inc(
                                e.indirect_dma_start(
                                    c.d["d_moeout"].ap(),
                                    IndirectOffsetOnAxis(SI[:, 2 + st2:3 + st2], 0),
                                    F1[:, 2048 + st2 * 1024:3072 + st2 * 1024], None), 16)

                if e_idx + 1 < 8 and 8 <= it < 14:
                    stp = it - 8
                    meta, extract, gather = emit_prep(blk, e_idx + 1, stp, 0, 0, 0, e_idx)
                    blk.sync(lambda e, meta=meta: meta(e))
                    blk.vector(lambda e, extract=extract: extract(e))
                    blk.gpsimd(lambda e, gather=gather: gather(e))

        @blk.gpsimd
        def _(e):
            e.wait_ge(sgs[0].h, sgs[0].n)
            e.wait_ge(sgs[1].h, sgs[1].n)

    sb2 = [c.sem("sgg0"), c.sem("sgg1")]
    with nc.Block() as blk:
        bufs = [dict(x1t=F1[:, 0:1024], at1=F1[:, 1024:2048],
                     srct=F1[:, 2048:3072], m0=F1[:, 3072:4096],
                     m1t=F1[:, 4096:5120], scr=F1[:, 5120:6144],
                     stats=SM[:, 0:4]),
                dict(x1t=F1[:, 6144:7168], at1=F1[:, 7168:8192],
                     srct=F2[:, 0:1024], m0=F2[:, 1024:2048],
                     m1t=F2[:, 2048:3072], scr=F2[:, 3072:4096],
                     stats=SM[:, 4:8])]
        for tp in range(NT // 2):
            jobs = []
            for k in range(2):
                t = 2 * tp + k
                b = bufs[k]

                @blk.sync
                def _(e, t=t, b=b, k=k):
                    e.wait_ge(ss[k].h, ss[k].n)
                    for dst, nm in [(b["x1t"], "d_x1"), (b["at1"], "d_attn1"),
                                    (b["srct"], "d_src")]:
                        sl[k].inc(e.dma_start(out=dst,
                                              in_=c.d[nm].ap()[t * 128:(t + 1) * 128, :]), 16)

                @blk.gpsimd
                def _(e, t=t, b=b, k=k):
                    e.wait_ge(ss[k].h, ss[k].n)
                    for m, dst in [(0, b["m0"]), (1, b["m1t"])]:
                        sb2[k].inc(e.dma_start(
                            out=dst,
                            in_=bass.AP(c.d["d_moeout"], (2 * t * 128 + m) * 1024,
                                        [[2048, 128], [1, 1024]])), 16)

            @blk.vector
            def _(e, tp=tp):
                for k in range(2):
                    e.wait_ge(sl[k].h, sl[k].n)
                    e.wait_ge(sb2[k].h, sb2[k].n)
                e.wait_ge(sv.h, sv.n)
                last = None
                for k in range(2):
                    last = e.tensor_tensor(bufs[k]["m0"], bufs[k]["m0"],
                                           bufs[k]["m1t"], ALU.add)
                sv.inc(last)
                e.wait_ge(sv.h, sv.n)
                last = None
                for k in range(2):
                    last = e.tensor_tensor(bufs[k]["srct"], bufs[k]["srct"],
                                           bufs[k]["m0"], ALU.add)
                sv.inc(last)

            rms_jobs(c, blk, [dict(xin=b["srct"], out=b["srct"],
                                   wbcast=SM[:, 1024:2048], stats=b["stats"],
                                   scratch=b["scr"]) for b in bufs])

            @blk.vector
            def _(e):
                e.wait_ge(sv.h, sv.n)
                last = None
                for k in range(2):
                    last = e.tensor_tensor(bufs[k]["x1t"], bufs[k]["x1t"],
                                           bufs[k]["at1"], ALU.add)
                sv.inc(last)
                e.wait_ge(sv.h, sv.n)
                last = None
                for k in range(2):
                    last = e.tensor_tensor(bufs[k]["x1t"], bufs[k]["x1t"],
                                           bufs[k]["srct"], ALU.add)
                sv.inc(last)

            for k in range(2):
                @blk.gpsimd
                def _(e, tp=tp, k=k):
                    e.wait_ge(c.sem("sv").h, c.sem("sv").n)
                    ss[k].inc(e.dma_start(
                        out=c.d["out"].ap()[(2 * tp + k) * 128:(2 * tp + k + 1) * 128, :],
                        in_=bufs[k]["x1t"]), 16)

        @blk.gpsimd
        def _(e):
            e.wait_ge(ss[0].h, ss[0].n)
            e.wait_ge(ss[1].h, ss[1].n)





def stage_launch_lists():
    if MODE == "fused":
        return [STAGES_ALL]
    return [[s] for s in STAGES_ALL]


def kernel(**inputs):
    ncores = 8
    x = np.ascontiguousarray(np.asarray(inputs["L0_attn_x"], dtype=np.float32))
    consts = [host_consts(cid) for cid in range(ncores)]
    outs = None
    bound = {}
    for stages in stage_launch_lists():
        nc = build_program(stages)
        in_maps = []
        for cid in range(ncores):
            m = {"x_in": x[cid]}
            m.update(consts[cid])
            for s in stages:
                for w in STAGE_WEIGHTS[s]:
                    m[w] = np.ascontiguousarray(np.asarray(inputs[w], dtype=np.float32))
            for s in stages:
                for b in BOUNDARY_NEED[s]:
                    if BOUNDARY[b] not in stages:
                        m[b] = bound[b][cid]
            in_maps.append(m)
        res = run_bass_kernel_spmd(nc, in_maps, core_ids=list(range(ncores)))
        for name in BOUNDARY:
            if BOUNDARY[name] in stages and MODE == "staged":
                bound[name] = [res.results[cid][name] for cid in range(ncores)]
        if "moe" in stages:
            outs = np.stack([res.results[cid]["out"] for cid in range(ncores)])
    return outs


# version 14
# speedup vs baseline: 1.1830x; 1.0288x over previous
import contextlib
import numpy as np
import ml_dtypes
import concourse.bass as bass
import concourse.mybir as mybir
from concourse.bass_utils import run_bass_kernel_spmd
from bass_rust import IndirectOffsetOnAxis

F32 = mybir.dt.float32
BF16 = mybir.dt.bfloat16
I32 = mybir.dt.int32
AF = mybir.ActivationFunctionType
ALU = mybir.AluOpType
AX = mybir.AxisListType

T = 2048
NT = 16
ND = 8
RND_C = 12582912.0
S_MAX = 768
CAPACITY = 5120

MODE = "fused"
NCORES = 8
L0_MAX_PHASE = 99
DEBUG_HEADS = None
GQA_MAX_PHASE = 99
STAGES_ALL = ["l0", "mamba", "gqa", "moe"]


def host_consts(core_id):
    bf = ml_dtypes.bfloat16

    def hadamard(size):
        h = np.array([[1.0]], dtype=np.float64)
        while h.shape[0] < size:
            h = np.block([[h, h], [h, -h]])
        return (h / np.sqrt(size)).astype(np.float32)

    bd64 = np.zeros((128, 2), bf)
    bd64[0:64, 0] = 1
    bd64[64:128, 1] = 1
    dest_sel = np.zeros((128, 32), np.float32)
    for tl in range(16):
        for k in range(2):
            dest_sel[:, 2 * tl + k] = 2 * (128 * tl + np.arange(128)) + k
    return {
        "c_hadamard": hadamard(1024).astype(bf),
        "c_ident": np.eye(128, dtype=bf),
        "c_identf": np.eye(128, dtype=np.float32),
        "c_strictup": np.triu(np.ones((128, 128), np.float32), 1),
        "c_bd64": bd64,
        "c_ones_row": np.ones((1, 128), np.float32),
        "c_onesb": np.ones((1, 64), bf),
        "c_su8": np.kron(np.triu(np.ones((16, 16), np.float32), 1),
                         np.eye(8, dtype=np.float32)),
        "c_iota8": np.tile(np.arange(8, dtype=np.float32), (128, 1)),
        "c_dest_sel": dest_sel,
        "c_prefmask": (np.arange(NCORES) < core_id).astype(np.float32).reshape(NCORES, 1),
    }


CONST_SPECS = [
    ("c_hadamard", [1024, 1024], BF16),
    ("c_ident", [128, 128], BF16),
    ("c_identf", [128, 128], F32),
    ("c_strictup", [128, 128], F32),
    ("c_bd64", [128, 2], BF16),
    ("c_ones_row", [1, 128], F32),
    ("c_onesb", [1, 64], BF16),
    ("c_su8", [128, 128], F32),
    ("c_iota8", [128, 8], F32),
    ("c_dest_sel", [128, 32], F32),
]

STAGE_WEIGHTS = {
    "l0": ["L0_attn_wq", "L0_attn_bq", "L0_attn_wk", "L0_attn_bk", "L0_attn_wv",
           "L0_attn_bv", "L0_attn_wo", "L0_attn_bo", "L0_ffn_weight"],
    "mamba": ["L1_attn_ln_w", "L1_attn_ln_b", "L1_attn_W_in", "L1_attn_conv_w",
              "L1_attn_A", "L1_attn_Bm", "L1_attn_Cm", "L1_attn_W_out"],
    "gqa": ["L1_ffn_qw", "L1_ffn_qb", "L1_ffn_kw", "L1_ffn_kb", "L1_ffn_vw",
            "L1_ffn_vb", "L1_ffn_ow", "L1_ffn_ob", "L1_ffn_norm1_w"],
    "moe": ["L1_ffn_gate_w", "L1_ffn_w1", "L1_ffn_w2", "L1_ffn_norm2_w"],
}

WEIGHT_SHAPES = {
    "L0_attn_wq": [1024, 1024], "L0_attn_bq": [1024], "L0_attn_wk": [1024, 1024],
    "L0_attn_bk": [1024], "L0_attn_wv": [1024, 1024], "L0_attn_bv": [1024],
    "L0_attn_wo": [1024, 1024], "L0_attn_bo": [1024], "L0_ffn_weight": [1024, 1024],
    "L1_attn_ln_w": [1024], "L1_attn_ln_b": [1024], "L1_attn_W_in": [1024, 3072],
    "L1_attn_conv_w": [1536, 3], "L1_attn_A": [16, 16], "L1_attn_Bm": [16, 1536],
    "L1_attn_Cm": [1536, 16], "L1_attn_W_out": [1536, 1024],
    "L1_ffn_qw": [1024, 1024], "L1_ffn_qb": [1024], "L1_ffn_kw": [1024, 256],
    "L1_ffn_kb": [256], "L1_ffn_vw": [1024, 256], "L1_ffn_vb": [256],
    "L1_ffn_ow": [1024, 1024], "L1_ffn_ob": [1024], "L1_ffn_gate_w": [1024, 8],
    "L1_ffn_w1": [8, 1024, 2048], "L1_ffn_w2": [8, 2048, 1024],
    "L1_ffn_norm1_w": [1024], "L1_ffn_norm2_w": [1024],
}

BOUNDARY = {"d_x1": "l0", "d_xn1": "mamba", "d_attn1": "mamba", "d_src": "gqa",
            "d_srcb": "gqa"}
BOUNDARY_NEED = {"l0": [], "mamba": ["d_x1"], "gqa": ["d_xn1"],
                 "moe": ["d_x1", "d_attn1", "d_src", "d_srcb"]}


class Sem:
    def __init__(self, h):
        self.h = h
        self.n = 0

    def inc(self, inst, k=1):
        inst.then_inc(self.h, k)
        self.n += k
        return self.n


class Ctx:
    def __init__(self, nc, ex):
        self.nc = nc
        self.ex = ex
        self.d = {}
        self.sems = {}

    def sem(self, name):
        if name not in self.sems:
            self.sems[name] = Sem(self.ex.enter_context(self.nc.semaphore(name)))
        return self.sems[name]


def build_program(stages):
    nc = bass.Bass("TRN2", target_bir_lowering=False)
    ex = contextlib.ExitStack()
    c = Ctx(nc, ex)

    c.d["x_in"] = nc.dram_tensor("x_in", [T, 1024], F32, kind="ExternalInput")
    out_kind = "ExternalOutput" if "moe" in stages else "Internal"
    c.d["out"] = nc.dram_tensor("out", [T, 1024], F32, kind=out_kind)
    for name, shape, dt in CONST_SPECS:
        c.d[name] = nc.dram_tensor(name, shape, dt, kind="ExternalInput")
    c.d["c_prefmask"] = nc.dram_tensor("c_prefmask", [NCORES, 1], F32,
                                       kind="ExternalInput")
    for s in stages:
        for w in STAGE_WEIGHTS[s]:
            c.d[w] = nc.dram_tensor(w, WEIGHT_SHAPES[w], F32, kind="ExternalInput")
    produced = {n for n, p in BOUNDARY.items() if p in stages}
    need = {n for s in stages for n in BOUNDARY_NEED[s]}
    for name in BOUNDARY:
        bdt = BF16 if name == "d_srcb" else F32
        if name in produced:
            kind = "Internal" if MODE == "fused" else "ExternalOutput"
            c.d[name] = nc.dram_tensor(name, [T, 1024], bdt, kind=kind)
        elif name in need:
            c.d[name] = nc.dram_tensor(name, [T, 1024], bdt, kind="ExternalInput")
    if "l0" in stages:
        c.d["d_xtmp"] = nc.dram_tensor("d_xtmp", [T, 1024], F32, kind="Internal")
    if "moe" in stages:
        c.d["d_moeout"] = nc.dram_tensor("d_moeout", [2 * T + 1, 1024], F32, kind="Internal")
        c.d["d_slotmeta"] = nc.dram_tensor("d_slotmeta", [8 * S_MAX + 1, 4], F32, kind="Internal")
        c.d["d_cin"] = nc.dram_tensor("d_cin", [1, 8], F32, kind="Internal")
        c.d["d_cout"] = nc.dram_tensor("d_cout", [NCORES, 8], F32, kind="Internal")

    with ex:
        c.A1 = ex.enter_context(nc.sbuf_tensor("A1", [128, 16640], BF16))
        c.A2 = ex.enter_context(nc.sbuf_tensor("A2", [128, 16640], BF16))
        c.A3 = ex.enter_context(nc.sbuf_tensor("A3", [128, 16640], BF16))
        c.A4 = ex.enter_context(nc.sbuf_tensor("A4", [128, 16640], BF16))
        c.A5 = ex.enter_context(nc.sbuf_tensor("A5", [128, 8192], BF16))
        c.F1 = ex.enter_context(nc.sbuf_tensor("F1", [128, 8192], F32))
        c.F2 = ex.enter_context(nc.sbuf_tensor("F2", [128, 4096], F32))
        c.SM = ex.enter_context(nc.sbuf_tensor("SM", [128, 2048], F32))
        c.CIDENT = ex.enter_context(nc.sbuf_tensor("CIDENT", [128, 128], BF16))
        c.CIDENTF = ex.enter_context(nc.sbuf_tensor("CIDENTF", [128, 128], F32))
        c.CSU = ex.enter_context(nc.sbuf_tensor("CSU", [128, 128], F32))
        c.CBD = ex.enter_context(nc.sbuf_tensor("CBD", [128, 2], BF16))
        c.CONES = ex.enter_context(nc.sbuf_tensor("CONES", [1, 128], F32))
        c.CONESB = ex.enter_context(nc.sbuf_tensor("CONESB", [1, 64], BF16))
        c.CSU8 = ex.enter_context(nc.sbuf_tensor("CSU8", [128, 128], F32))
        c.CMISC = ex.enter_context(nc.sbuf_tensor("CMISC", [128, 48], F32))
        c.SI = ex.enter_context(nc.sbuf_tensor("SI", [128, 48], I32))
        c.PS_A = ex.enter_context(nc.psum_tensor("PS_A", [128, 2048], F32))
        c.PS_B = ex.enter_context(nc.psum_tensor("PS_B", [128, 2048], F32))

        emit_const_load(c)
        for s in stages:
            globals()["emit_" + s](c)
    return nc


def psab(c):
    return c.PS_A.ap().bitcast(BF16)


def psbb(c):
    return c.PS_B.ap().bitcast(BF16)


def emit_const_load(c):
    nc = c.nc
    sd = c.sem("sd")
    with nc.Block() as blk:
        @blk.sync
        def _(e):
            for ap_out, name in [(c.CIDENT[:], "c_ident"), (c.CIDENTF[:], "c_identf"),
                                 (c.CSU[:], "c_strictup"), (c.CBD[:], "c_bd64"),
                                 (c.CONES[:], "c_ones_row"),
                                 (c.CONESB[:], "c_onesb"),
                                 (c.CSU8[:], "c_su8"),
                                 (c.CMISC[:, 0:8], "c_iota8"),
                                 (c.CMISC[:, 8:40], "c_dest_sel")]:
                sd.inc(e.dma_start(out=ap_out, in_=c.d[name].ap()), 16)
            e.wait_ge(sd.h, sd.n)

        @blk.vector
        def _(e):
            e.memset(c.CMISC[:, 40:41], 1e-5)
            e.memset(c.CMISC[:, 41:42], 1e-6)
            e.memset(c.CMISC[:, 42:43], 1.0)



def ln_tile(c, blk, xin, outs, stats, scratch, eps_col=40, gam=None, bet=None,
            waits=()):
    sv, sa = c.sem("sv"), c.sem("sa")
    mean, var, rstd, tmp = (stats[:, i:i + 1] for i in range(4))
    sv_entry = sv.n

    @blk.vector
    def _(e):
        for (s, v) in waits:
            e.wait_ge(s.h, v)
        e.wait_ge(sv.h, sv_entry)
        sv.inc(e.tensor_tensor(scratch, xin, xin, ALU.mult))
        e.wait_ge(sv.h, sv.n)
        e.reduce_sum(mean, xin, AX.X)
        sv.inc(e.reduce_sum(var, scratch, AX.X))
        e.wait_ge(sv.h, sv.n)
        e.tensor_scalar(mean, mean, 1.0 / 1024.0, None, ALU.mult)
        sv.inc(e.tensor_scalar(var, var, 1.0 / 1024.0, None, ALU.mult))
        e.wait_ge(sv.h, sv.n)
        sv.inc(e.tensor_tensor(tmp, mean, mean, ALU.mult))
        e.wait_ge(sv.h, sv.n)
        sv.inc(e.tensor_tensor(var, var, tmp, ALU.subtract))

    @blk.scalar
    def _(e):
        e.wait_ge(sv.h, sv.n)
        sa.inc(e.activation(tmp, var, AF.Ln, bias=c.CMISC[:, eps_col:eps_col + 1]))
        e.wait_ge(sa.h, sa.n)
        sa.inc(e.activation(rstd, tmp, AF.Exp, scale=-0.5))

    @blk.vector
    def _(e):
        e.wait_ge(sa.h, sa.n)
        if gam is None:
            last = None
            for o in outs:
                last = e.tensor_scalar(o, xin, mean, rstd, ALU.subtract, ALU.mult)
            sv.inc(last)
        else:
            sv.inc(e.tensor_scalar(scratch, xin, mean, rstd, ALU.subtract, ALU.mult))
            e.wait_ge(sv.h, sv.n)
            sv.inc(e.tensor_tensor(scratch, scratch, gam, ALU.mult))
            e.wait_ge(sv.h, sv.n)
            last = None
            for o in outs:
                last = e.tensor_tensor(o, scratch, bet, ALU.add)
            sv.inc(last)


def ln_front(c, blk, jobs, eps_col=40):
    sv, sa = c.sem("sv"), c.sem("sa")

    @blk.vector
    def _(e):
        for j in jobs:
            for (s, v) in j.get("waits", ()):
                e.wait_ge(s.h, v)
        last = None
        for j in jobs:
            e.tensor_tensor(j["scratch"], j["xin"], j["xin"], ALU.mult)
            last = e.reduce_sum(j["stats"][:, 0:1], j["xin"], AX.X)
        sv.inc(last)
        e.wait_ge(sv.h, sv.n)
        for j in jobs:
            e.reduce_sum(j["stats"][:, 1:2], j["scratch"], AX.X)
            last = e.tensor_scalar(j["stats"][:, 0:1], j["stats"][:, 0:1],
                                   1.0 / 1024.0, None, ALU.mult)
        sv.inc(last)
        e.wait_ge(sv.h, sv.n)
        for j in jobs:
            last = e.tensor_tensor(j["stats"][:, 3:4], j["stats"][:, 0:1],
                                   j["stats"][:, 0:1], ALU.mult)
        sv.inc(last)
        e.wait_ge(sv.h, sv.n)
        for j in jobs:
            last = e.tensor_tensor(j["stats"][:, 2:3],
                                   c.CMISC[:, eps_col:eps_col + 1],
                                   j["stats"][:, 3:4], ALU.subtract)
        sv.inc(last)

    @blk.scalar
    def _(e):
        e.wait_ge(sv.h, sv.n)
        last = None
        for j in jobs:
            last = e.activation(j["stats"][:, 3:4], j["stats"][:, 1:2], AF.Ln,
                                scale=1.0 / 1024.0, bias=j["stats"][:, 2:3])
        sa.inc(last)
        e.wait_ge(sa.h, sa.n)
        last = None
        for j in jobs:
            last = e.activation(j["stats"][:, 2:3], j["stats"][:, 3:4], AF.Exp,
                                scale=-0.5)
        sa.inc(last)
    return sa.n


def ln_back(c, blk, jobs, sa_mark):
    sv, sa = c.sem("sv"), c.sem("sa")

    @blk.vector
    def _(e):
        e.wait_ge(sa.h, sa_mark)
        last = None
        for j in jobs:
            mean, rstd = j["stats"][:, 0:1], j["stats"][:, 2:3]
            if j.get("gam") is None:
                for o in j["outs"]:
                    last = e.tensor_scalar(o, j["xin"], mean, rstd,
                                           ALU.subtract, ALU.mult)
            else:
                last = e.tensor_scalar(j["scratch"], j["xin"], mean, rstd,
                                       ALU.subtract, ALU.mult)
        sv.inc(last)
        if any(j.get("gam") is not None for j in jobs):
            e.wait_ge(sv.h, sv.n)
            last = None
            for j in jobs:
                if j.get("gam") is not None:
                    last = e.tensor_tensor(j["scratch"], j["scratch"],
                                           j["gam"], ALU.mult)
            sv.inc(last)
            e.wait_ge(sv.h, sv.n)
            last = None
            for j in jobs:
                if j.get("gam") is not None:
                    for o in j["outs"]:
                        last = e.tensor_tensor(o, j["scratch"], j["bet"],
                                               ALU.add)
            sv.inc(last)


def ln_jobs(c, blk, jobs, eps_col=40):
    ln_back(c, blk, jobs, ln_front(c, blk, jobs, eps_col))


def rms_jobs(c, blk, jobs, eps_col=41):
    sv, sa = c.sem("sv"), c.sem("sa")
    sv_entry = sv.n

    @blk.vector
    def _(e):
        for j in jobs:
            for (s, v) in j.get("waits", ()):
                e.wait_ge(s.h, v)
        e.wait_ge(sv.h, sv_entry)
        last = None
        for j in jobs:
            last = e.tensor_tensor(j["scratch"], j["xin"], j["xin"], ALU.mult)
        sv.inc(last)
        e.wait_ge(sv.h, sv.n)
        last = None
        for j in jobs:
            last = e.reduce_sum(j["stats"][:, 0:1], j["scratch"], AX.X)
        sv.inc(last)

    @blk.scalar
    def _(e):
        e.wait_ge(sv.h, sv.n)
        last = None
        for j in jobs:
            last = e.activation(j["stats"][:, 1:2], j["stats"][:, 0:1], AF.Ln,
                                scale=1.0 / 1024.0,
                                bias=c.CMISC[:, eps_col:eps_col + 1])
        sa.inc(last)
        e.wait_ge(sa.h, sa.n)
        last = None
        for j in jobs:
            last = e.activation(j["stats"][:, 2:3], j["stats"][:, 1:2], AF.Exp,
                                scale=-0.5)
        sa.inc(last)

    @blk.vector
    def _(e):
        e.wait_ge(sa.h, sa.n)
        last = None
        for j in jobs:
            last = e.scalar_tensor_tensor(j["out"], j["xin"],
                                          j["stats"][:, 2:3], j["wbcast"],
                                          ALU.mult, ALU.mult)
        sv.inc(last)


def transpose_to_T(c, blk, src_bf, dstbuf, dst_col, nchunk=8, psum_bank=0,
                   wait_mark=None):
    sp, sv = c.sem("sp"), c.sem("sv")
    pb = psab(c) if psum_bank < 4 else psbb(c)
    base = (psum_bank % 4) * 1024
    wm = sv.n if wait_mark is None else wait_mark

    @blk.tensor
    def _(e):
        e.wait_ge(sv.h, wm)
        for dc in range(nchunk):
            sp.inc(e.transpose(pb[:, base + dc * 128:base + (dc + 1) * 128],
                        src_bf[:, dc * 128:(dc + 1) * 128], c.CIDENT[:]))

    @blk.vector
    def _(e):
        e.wait_ge(sp.h, sp.n)
        dst = bass.AP(dstbuf, dst_col, [[16640, 128], [2048, nchunk], [1, 128]])
        sv.inc(e.tensor_copy(dst, pb[:, base:base + nchunk * 128]
                      .rearrange("p (a b) -> p a b", a=nchunk)))


def row_to_bcast(c, blk, row_ap, dst_sm, width, tmp_psum_col=0, waits=()):
    sp, sv = c.sem("sp"), c.sem("sv")

    @blk.tensor
    def _(e):
        e.wait_ge(sv.h, sv.n)
        for (s, v) in waits:
            e.wait_ge(s.h, v)
        for j in range(0, width, 512):
            w = min(512, width - j)
            sp.inc(e.matmul(c.PS_B[0:128, tmp_psum_col + j:tmp_psum_col + j + w],
                     c.CONES[0:1, 0:128], row_ap[:, j:j + w], start=True, stop=True))

    @blk.vector
    def _(e):
        e.wait_ge(sp.h, sp.n)
        sv.inc(e.tensor_copy(dst_sm, c.PS_B[:, tmp_psum_col:tmp_psum_col + width]))



def emit_l0(c):
    nc = c.nc
    sd, sg, sp, sv, sa = (c.sem(n) for n in ["sd", "sg", "sp", "sv", "sa"])
    W = c.d
    A1, A2, A3, A4, A5 = c.A1, c.A2, c.A3, c.A4, c.A5
    F1, F2, SM = c.F1, c.F2, c.SM
    F1b = c.F1.ap().bitcast(BF16)
    F2b = c.F2.ap().bitcast(BF16)
    PS_A, PS_B = c.PS_A, c.PS_B

    if L0_MAX_PHASE < 0:
        return
    with nc.Block() as blk:
        @blk.gpsimd
        def _(e):
            sg.inc(e.dma_start(
                out=bass.AP(F1, 0, [[8192, 128], [1024, 8], [1, 1024]]),
                in_=W["L0_ffn_weight"].ap().rearrange("(cc p) i -> p cc i", p=128)), 16)

        @blk.vector
        def _(e):
            e.wait_ge(sg.h, sg.n)
            for oc in range(8):
                wv = F1[:, oc * 1024:(oc + 1) * 1024]
                s_col, th, thn = SM[:, 1060:1061], SM[:, 1061:1062], SM[:, 1062:1063]
                sv.inc(e.reduce_sum(s_col, wv, AX.X, apply_absolute_value=True))
                e.wait_ge(sv.h, sv.n)
                sv.inc(e.tensor_scalar(s_col, s_col, 1.0 / 1024.0, None, ALU.mult))
                e.wait_ge(sv.h, sv.n)
                sv.inc(e.tensor_scalar_max(s_col, s_col, 1e-5))
                e.wait_ge(sv.h, sv.n)
                e.tensor_scalar(th, s_col, 0.5, None, ALU.mult)
                sv.inc(e.tensor_scalar(thn, s_col, -0.5, None, ALU.mult))
                e.wait_ge(sv.h, sv.n)
                e.tensor_scalar(F2[:, 0:1024], wv, th, None, ALU.is_gt)
                sv.inc(e.tensor_scalar(F2[:, 1024:2048], wv, thn, None, ALU.is_lt))
                e.wait_ge(sv.h, sv.n)
                sv.inc(e.tensor_tensor(F2[:, 2048:3072], F2[:, 0:1024], F2[:, 1024:2048],
                                ALU.subtract))
                e.wait_ge(sv.h, sv.n)
                sv.inc(e.tensor_scalar(A3[:, oc * 1024:(oc + 1) * 1024], F2[:, 2048:3072],
                                s_col, None, ALU.mult))
                e.wait_ge(sv.h, sv.n)

    if L0_MAX_PHASE < 1:
        return
    with nc.Block() as blk:
        @blk.gpsimd
        def _(e):
            sg.inc(e.dma_start(
                out=F2b.rearrange("p (cc ee) -> p cc ee", cc=8),
                in_=c.d["c_hadamard"].ap().rearrange("(cc p) ee -> p cc ee", p=128)), 16)
            for i, nm in enumerate(["L0_attn_wq", "L0_attn_wk", "L0_attn_wv", "L0_attn_wo"]):
                buf = A1 if i < 2 else A2
                off = 8192 * (i % 2)
                sg.inc(e.dma_start(
                    out=bass.AP(buf, off, [[16640, 128], [1024, 8], [1, 1024]]),
                    in_=W[nm].ap().rearrange("(cc p) dd -> p cc dd", p=128)), 16)

        pb = psab(c)
        for ic in range(8):
            @blk.tensor
            def _(e, ic=ic):
                if ic > 0:
                    e.wait_ge(sv.h, sv.n)
                for oc in range(8):
                    sp.inc(e.transpose(pb[:, oc * 128:(oc + 1) * 128],
                                A3[:, oc * 1024 + ic * 128:oc * 1024 + (ic + 1) * 128],
                                c.CIDENT[:]))

            @blk.vector
            def _(e, ic=ic):
                e.wait_ge(sp.h, sp.n)
                sv.inc(e.tensor_copy(bass.AP(A5, ic * 1024, [[8192, 128], [128, 8], [1, 128]]),
                              pb[:, 0:1024].rearrange("p (a b) -> p a b", a=8)))

    if L0_MAX_PHASE < 2:
        return
    with nc.Block() as blk:
        @blk.sync
        def _(e):
            for i, nm in enumerate(["L0_attn_bq", "L0_attn_bk"]):
                sd.inc(e.dma_start(out=bass.AP(SM, 1024 + i * 128, [[2048, 8], [1, 128]]),
                                   in_=W[nm].ap().rearrange("(cc p) -> cc p", p=128)), 16)
            sd.inc(e.dma_start(out=bass.AP(SM, 0, [[2048, 1], [1, 1024]]),
                               in_=W["L0_attn_bv"].ap().rearrange("(a b) -> a b", a=1)), 16)

        pb = psab(c)
        for i in range(4):
            src = A1 if i < 2 else A2
            soff = 8192 * (i % 2)
            dst = A4 if i < 2 else A3
            doff = 8192 * (i % 2)
            for dc in range(8):
                @blk.tensor
                def _(e, src=src, soff=soff, dc=dc, first=(i == 0 and dc == 0)):
                    if first:
                        e.wait_ge(sg.h, sg.n)
                    else:
                        e.wait_ge(sv.h, sv.n)
                    for ec in range(8):
                        sp.inc(e.transpose(pb[:, ec * 128:(ec + 1) * 128],
                                    bass.AP(src, soff + ec * 1024 + dc * 128,
                                            [[16640, 128], [1, 128]]),
                                    c.CIDENT[:]))

                @blk.vector
                def _(e, dst=dst, doff=doff, dc=dc):
                    e.wait_ge(sp.h, sp.n)
                    sv.inc(e.tensor_copy(bass.AP(dst, doff + dc * 1024, [[16640, 128], [1, 1024]]),
                                  pb[:, 0:1024]))

        @blk.tensor
        def _(e):
            e.wait_ge(sd.h, sd.n)
            e.wait_ge(sv.h, sv.n)
            e.transpose(PS_B[:, 0:8], bass.AP(SM, 1024, [[2048, 8], [1, 128]]),
                        c.CIDENTF[0:8, 0:8])
            sp.inc(e.transpose(PS_B[:, 8:16], bass.AP(SM, 1152, [[2048, 8], [1, 128]]),
                        c.CIDENTF[0:8, 0:8]))

        @blk.vector
        def _(e):
            e.wait_ge(sp.h, sp.n)
            e.tensor_copy(SM[:, 1024:1032], PS_B[:, 0:8])
            sv.inc(e.tensor_copy(SM[:, 1032:1040], PS_B[:, 8:16]))

    sl = [c.sem("sl0"), c.sem("sl1")]
    ss = [c.sem("ss0"), c.sem("ss1")]
    sl4 = [sl[0], sl[1], c.sem("sl2"), c.sem("sl3")]
    with nc.Block() as blk:
        fronts = {}
        backmark = {}
        for tp in range(NT // 2 + 1):
            if tp < NT // 2:
                jobs = []
                for k in range(2):
                    t = 2 * tp + k
                    xs = (tp % 2) * 2 + k
                    xt = F1[:, xs * 1024:(xs + 1) * 1024]
                    lsem = sl4[xs]

                    @blk.sync
                    def _(e, t=t, xt=xt, lsem=lsem, tp=tp):
                        if tp >= 2:
                            e.wait_ge(c.sem("sv").h, backmark[tp - 2])
                        lsem.inc(e.dma_start(out=xt,
                                             in_=c.d["x_in"].ap()[t * 128:(t + 1) * 128, :]), 16)

                    jobs.append(dict(
                        xin=xt,
                        outs=[F1b[:, 12288 + k * 1024:13312 + k * 1024]],
                        stats=SM[:, 1056 + 8 * (tp % 2) + 4 * k:
                                 1060 + 8 * (tp % 2) + 4 * k],
                        scratch=F1[:, 4096 + k * 1024:5120 + k * 1024],
                        waits=[(lsem, lsem.n)]))
                fronts[tp] = (jobs, ln_front(c, blk, jobs))
            if tp - 1 in fronts:
                pj, pm = fronts.pop(tp - 1)
                ln_back(c, blk, pj, pm)
                backmark[tp - 1] = c.sem("sv").n
                for k in range(2):
                    transpose_to_T(c, blk, F1b[:, 12288 + k * 1024:13312 + k * 1024],
                                   A1, (2 * (tp - 1) + k) * 128, 8, psum_bank=k)

    with nc.Block() as blk:
        qmark = {}
        trmark = {}
        evmark = {}
        for t in range(NT + 1):
            if t < NT:
                slot = t % 2
                xt = F1[:, slot * 1024:(slot + 1) * 1024]
                ps = PS_A[:, 0:1024] if t % 2 == 0 else PS_B[:, 1024:2048]

                @blk.sync
                def _(e, t=t, xt=xt, slot=slot):
                    if t >= 2:
                        e.wait_ge(ss[slot].h, ss[slot].n)
                    sl[slot].inc(e.dma_start(out=xt, in_=c.d["x_in"].ap()[t * 128:(t + 1) * 128, :]), 16)

                @blk.tensor
                def _(e, t=t, ps=ps):
                    if t >= 2:
                        e.wait_ge(sv.h, qmark[t - 2])
                    else:
                        e.wait_ge(sv.h, sv.n)
                    for ns in range(2):
                        for dc in range(8):
                            sp.inc(e.matmul(ps[:, ns * 512:(ns + 1) * 512],
                                     A1[:, dc * 2048 + t * 128:dc * 2048 + (t + 1) * 128],
                                     F2b[:, dc * 1024 + ns * 512:dc * 1024 + (ns + 1) * 512],
                                     start=(dc == 0), stop=(dc == 7)))

                xnb = F1b[:, 8192 + (t % 2) * 1024:9216 + (t % 2) * 1024]

                @blk.vector
                def _(e, t=t, ps=ps, xnb=xnb):
                    e.wait_ge(sp.h, sp.n)
                    amax, inv, s7 = SM[:, 1056:1057], SM[:, 1057:1058], SM[:, 1058:1059]
                    sv.inc(e.reduce_max(amax, ps, AX.X, apply_absolute_value=True))
                    e.wait_ge(sv.h, sv.n)
                    sv.inc(e.tensor_scalar_max(amax, amax, 1e-5))
                    e.wait_ge(sv.h, sv.n)
                    e.reciprocal(inv, amax)
                    sv.inc(e.tensor_scalar(s7, amax, 1.0 / 7.0, None, ALU.mult))
                    e.wait_ge(sv.h, sv.n)
                    sv.inc(e.tensor_scalar(F1[:, 2048:3072], ps, inv, 7.0, ALU.mult, ALU.mult))
                    e.wait_ge(sv.h, sv.n)
                    sv.inc(e.tensor_scalar(F1[:, 3072:4096], F1[:, 2048:3072], RND_C, RND_C,
                                    ALU.add, ALU.subtract))
                    e.wait_ge(sv.h, sv.n)
                    sv.inc(e.tensor_scalar(xnb, F1[:, 3072:4096], s7, None, ALU.mult))
                    qmark[t] = sv.n

            if t >= 1:
                tp = t - 1
                xnbp = F1b[:, 8192 + (tp % 2) * 1024:9216 + (tp % 2) * 1024]
                transpose_to_T(c, blk, xnbp, A2, tp * 128, 8,
                               psum_bank=2 + tp % 2, wait_mark=qmark[tp])
                trmark[tp] = c.sem("sv").n

                @blk.tensor
                def _(e, tp=tp):
                    e.wait_ge(sv.h, trmark[tp])
                    for ns in range(2):
                        for ic in range(8):
                            sp.inc(e.matmul(PS_B[:, ns * 512:(ns + 1) * 512],
                                     A2[:, ic * 2048 + tp * 128:ic * 2048 + (tp + 1) * 128],
                                     A5[:, ic * 1024 + ns * 512:ic * 1024 + (ns + 1) * 512],
                                     start=(ic == 0), stop=(ic == 7)))

                @blk.vector
                def _(e, tp=tp):
                    e.wait_ge(sp.h, sp.n)
                    e.wait_ge(sl[tp % 2].h, sl[tp % 2].n)
                    xtp = F1[:, (tp % 2) * 1024:(tp % 2 + 1) * 1024]
                    sv.inc(e.scalar_tensor_tensor(xtp, PS_B[:, 0:1024], 1.0, xtp,
                                                  ALU.mult, ALU.add))
                    evmark[tp] = sv.n

                @blk.gpsimd
                def _(e, tp=tp):
                    e.wait_ge(sv.h, evmark[tp])
                    ss[tp % 2].inc(e.dma_start(
                        out=c.d["d_xtmp"].ap()[tp * 128:(tp + 1) * 128, :],
                        in_=F1[:, (tp % 2) * 1024:(tp % 2 + 1) * 1024]), 16)

        @blk.gpsimd
        def _(e):
            e.wait_ge(ss[0].h, ss[0].n)
            e.wait_ge(ss[1].h, ss[1].n)

    if L0_MAX_PHASE < 5:
        return
    emit_qkv(c, xnT=A1.ap(), wq=(A4, 0), wk=(A4, 8192), wv=(A3, 0),
             QT=A2.ap(), KT=F1b, Vext=A4.ap(),
             bq_cols=SM[:, 1024:1032], bk_cols=SM[:, 1032:1040],
             bv_row=SM[0:1, 0:1024], bvb=SM[:, 0:1024], nk_e=1024, nv=1024)

    if L0_MAX_PHASE < 6:
        return
    emit_norm_bounds(c, QT=A2.ap(), KT=F1b, nq_heads=16, nkv=16, kv_group=1,
                     scale=0.25, sq_scratch=A5[:, 4096:6144],
                     negc_cols=SM[:, 1040:1056])
    emit_attention(c, QT=A2.ap(), KT=F1b, Vext=A4.ap(), outT=A1.ap(), nq_heads=16,
                   kv_group=1, scale=0.25, scratch_bf=A5.ap(),
                   negc_cols=SM[:, 1040:1056],
                   recip_bf=F2.ap().bitcast(BF16)[0:1, 4096:6144],
                   recip_tmp=F2[0:1, 0:2048])

    if L0_MAX_PHASE < 7:
        return
    with nc.Block() as blk:
        @blk.sync
        def _(e):
            sd.inc(e.dma_start(out=bass.AP(SM, 0, [[2048, 1], [1, 1024]]),
                               in_=W["L0_attn_bo"].ap().rearrange("(a b) -> a b", a=1)), 16)

        row_to_bcast(c, blk, SM[0:1, 0:1024], SM[:, 0:1024], 1024,
                     waits=[(sd, sd.n)])

        evmark = {}
        for t in range(NT):
            slot = t % 2
            xt = F1[:, slot * 1024:(slot + 1) * 1024]

            @blk.sync
            def _(e, t=t, xt=xt, slot=slot):
                e.wait_ge(ss[slot].h, ss[slot].n)
                sl[slot].inc(e.dma_start(out=xt,
                                         in_=c.d["d_xtmp"].ap()[t * 128:(t + 1) * 128, :]), 16)

            @blk.tensor
            def _(e, t=t):
                if t >= 2:
                    e.wait_ge(sv.h, evmark[t - 2])
                ps = PS_A[:, (t % 2) * 1024:(t % 2) * 1024 + 1024]
                for ns in range(2):
                    for dc in range(8):
                        sp.inc(e.matmul(ps[:, ns * 512:(ns + 1) * 512],
                                 A1[:, dc * 2048 + t * 128:dc * 2048 + (t + 1) * 128],
                                 A3[:, 8192 + dc * 1024 + ns * 512:8192 + dc * 1024 + (ns + 1) * 512],
                                 start=(dc == 0), stop=(dc == 7)))

            @blk.vector
            def _(e, t=t, xt=xt, slot=slot):
                e.wait_ge(sp.h, sp.n)
                e.wait_ge(sl[slot].h, sl[slot].n)
                ps = PS_A[:, (t % 2) * 1024:(t % 2) * 1024 + 1024]
                sv.inc(e.scalar_tensor_tensor(xt, ps, 1.0, xt, ALU.mult, ALU.add))
                evmark[t] = sv.n
                e.wait_ge(sv.h, sv.n)
                sv.inc(e.tensor_tensor(xt, xt, SM[:, 0:1024], ALU.add))

            @blk.gpsimd
            def _(e, t=t, xt=xt, slot=slot):
                e.wait_ge(sv.h, sv.n)
                ss[slot].inc(e.dma_start(out=c.d["d_x1"].ap()[t * 128:(t + 1) * 128, :],
                                         in_=xt), 16)

        @blk.gpsimd
        def _(e):
            e.wait_ge(ss[0].h, ss[0].n)
            e.wait_ge(ss[1].h, ss[1].n)


def emit_warmup(c, blk, n=28):
    sp = c.sem("sp")

    @blk.tensor
    def _(e):
        last = None
        for i in range(2 * n):
            last = e.matmul(c.PS_B[0:128, 0:128], c.CIDENT[:],
                            c.CIDENT[:], start=True, stop=True)
        sp.inc(last)


def emit_qkv(c, xnT, wq, wk, wv, QT, KT, Vext, bq_cols, bk_cols, bv_row, bvb,
             nk_e=1024, nv=1024, entry_waits=()):
    wq = (wq[0].ap() if not isinstance(wq[0], bass.AP) else wq[0], wq[1])
    wk = (wk[0].ap() if not isinstance(wk[0], bass.AP) else wk[0], wk[1])
    wv = (wv[0].ap() if not isinstance(wv[0], bass.AP) else wv[0], wv[1])
    """Build QT [128, 8*2048] and KT [128, (nk_e/128)*2048] feature-major bf16
    (with per-partition bias via ACT evac), and Vext token-major bf16 with bias
    broadcast and ones columns (vcols = (nv/64)*65 per tile).
    wq/wk/wv: (buf, off) with layout [128 d-part, dc*ewidth + e] bf16,
    ewidth = 1024 for q, nk_e for k, nv for v.
    """
    nc = c.nc
    sd, sp, sv, sa = (c.sem(n) for n in ["sd", "sp", "sv", "sa"])
    PS_A, PS_B = c.PS_A, c.PS_B

    with nc.Block() as blk:
        act_marks = []
        jobs = []
        for ec in range(8):
            for tq in range(4):
                jobs.append((wq, 1024, QT, bq_cols, ec, tq))
        for ec in range(nk_e // 128):
            for tq in range(4):
                jobs.append((wk, nk_e, KT, bk_cols, ec, tq))
        for gi, (wbo, ew, dstbuf, bcols, ec, tq) in enumerate(jobs):
            pslot = gi % 4

            @blk.tensor
            def _(e, wbo=wbo, ew=ew, ec=ec, tq=tq, pslot=pslot, gi=gi):
                if gi == 0:
                    for (s, v) in entry_waits:
                        e.wait_ge(s.h, v)
                if gi >= 4:
                    e.wait_ge(sv.h, act_marks[gi - 4])
                ps = PS_A[:, pslot * 512:(pslot + 1) * 512]
                wbuf, woff = wbo
                for dc in range(8):
                    sp.inc(e.matmul(ps,
                             wbuf[:, woff + dc * ew + ec * 128:woff + dc * ew + (ec + 1) * 128],
                             xnT[:, dc * 2048 + tq * 512:dc * 2048 + (tq + 1) * 512],
                             start=(dc == 0), stop=(dc == 7)))

            @blk.vector
            def _(e, dstbuf=dstbuf, bcols=bcols, ec=ec, tq=tq, pslot=pslot):
                e.wait_ge(sp.h, sp.n)
                ps = PS_A[:, pslot * 512:(pslot + 1) * 512]
                dst = dstbuf[:, ec * 2048 + tq * 512:ec * 2048 + (tq + 1) * 512]
                act_marks.append(sv.inc(
                    e.tensor_scalar(dst, ps, bcols[:, ec:ec + 1], None, ALU.add)))

    nh = nv // 64
    vcols = nh * 65
    with nc.Block() as blk:
        row_to_bcast(c, blk, bv_row, bvb[:, 0:nv], nv)

        @blk.vector
        def _(e):
            e.wait_ge(sv.h, sv.n)
            sv.inc(e.memset(Vext[:, 0:NT * vcols], 1.0))

        ev_marks = []
        for t in range(NT):
            @blk.tensor
            def _(e, t=t):
                if t == 0:
                    for (s, v) in entry_waits:
                        e.wait_ge(s.h, v)
                if t < 2:
                    e.wait_ge(sv.h, sv.n)
                else:
                    e.wait_ge(sv.h, ev_marks[t - 2])
                ps = PS_B[:, (t % 2) * 1024:(t % 2) * 1024 + 1024]
                wbuf, woff = wv
                for j in range(0, nv, 512):
                    wdt = min(512, nv - j)
                    for dc in range(8):
                        sp.inc(e.matmul(ps[:, j:j + wdt],
                                 xnT[:, dc * 2048 + t * 128:dc * 2048 + (t + 1) * 128],
                                 wbuf[:, woff + dc * nv + j:woff + dc * nv + j + wdt],
                                 start=(dc == 0), stop=(dc == 7)))

            @blk.vector
            def _(e, t=t):
                e.wait_ge(sp.h, sp.n)
                ps = PS_B[:, (t % 2) * 1024:(t % 2) * 1024 + 1024]
                dst = Vext[:, t * vcols:(t + 1) * vcols] \
                    .rearrange("p (h x) -> p h x", h=nh)[:, :, 0:64]
                ev_marks.append(sv.inc(e.tensor_tensor(
                    dst, ps[:, 0:nv].rearrange("p (a b) -> p a b", a=nh),
                    bvb[:, 0:nv].rearrange("p (a b) -> p a b", a=nh), ALU.add)))


def emit_norm_bounds(c, QT, KT, nq_heads, nkv, kv_group, scale, sq_scratch,
                     negc_cols):
    nc = c.nc
    sp, sv, sa = (c.sem(n) for n in ["sp", "sv", "sa"])
    PS_A, PS_B = c.PS_A, c.PS_B
    F2 = c.F2

    with nc.Block() as blk:
        @blk.vector
        def _(e):
            sv.inc(e.memset(F2[0:2, 0:64], 0.0))

        chunks = ([(QT, hc, hc) for hc in range(nq_heads // 2)] +
                  [(KT, hc, 16 + hc) for hc in range(nkv // 2)])
        for i, (buf, hc, col) in enumerate(chunks):
            @blk.vector
            def _(e, buf=buf, hc=hc, i=i):
                if i > 0:
                    e.wait_ge(sp.h, sp.n)
                e.wait_ge(sv.h, sv.n)
                src_v = buf[:, hc * 2048:(hc + 1) * 2048]
                sa.inc(e.tensor_tensor(sq_scratch, src_v, src_v, ALU.mult))

            @blk.tensor
            def _(e, i=i):
                e.wait_ge(sa.h, sa.n)
                if i > 0:
                    e.wait_ge(sv.h, sv.n)
                for j in range(4):
                    sp.inc(e.matmul(PS_B[0:2, j * 512:(j + 1) * 512], c.CBD[:, 0:2],
                             sq_scratch[:, j * 512:(j + 1) * 512],
                             start=True, stop=True))

            @blk.vector
            def _(e, col=col, i=i):
                e.wait_ge(sp.h, sp.n)
                if i == 0:
                    e.wait_ge(sv.h, sv.n)
                sv.inc(e.reduce_max(F2[0:2, col:col + 1], PS_B[0:2, 0:2048], AX.X))

        @blk.scalar
        def _(e):
            e.wait_ge(sv.h, sv.n)
            sa.inc(e.activation(F2[0:2, 32:64], F2[0:2, 0:32], AF.Sqrt))

        @blk.tensor
        def _(e):
            e.wait_ge(sa.h, sa.n)
            for hc in range(nq_heads // 2):
                e.transpose(PS_A[0:1, 64 + 2 * hc:64 + 2 * hc + 2],
                            F2[0:2, 32 + hc:33 + hc], c.CIDENTF[0:2, 0:2])
            for kc in range(nkv // 2):
                sp.inc(e.transpose(PS_A[0:1, 96 + 2 * kc:96 + 2 * kc + 2],
                            F2[0:2, 48 + kc:49 + kc], c.CIDENTF[0:2, 0:2]))

        @blk.vector
        def _(e):
            e.wait_ge(sp.h, sp.n)
            e.tensor_copy(F2[0:1, 64:64 + nq_heads], PS_A[0:1, 64:64 + nq_heads])
            sv.inc(e.tensor_copy(F2[0:1, 96:96 + nkv], PS_A[0:1, 96:96 + nkv]))
            e.wait_ge(sv.h, sv.n)
            if kv_group == 1:
                sv.inc(e.tensor_tensor(F2[0:1, 128:128 + nq_heads],
                                       F2[0:1, 64:64 + nq_heads],
                                       F2[0:1, 96:96 + nkv], ALU.mult))
            else:
                qv = F2[0:1, 64:64 + nq_heads].rearrange("p (a b) -> p a b", a=nkv)
                kv = bass.AP(c.F2, 96, [[4096, 1], [1, nkv], [0, kv_group]])
                ov = F2[0:1, 128:128 + nq_heads].rearrange("p (a b) -> p a b", a=nkv)
                sv.inc(e.tensor_tensor(ov, qv, kv, ALU.mult))
            e.wait_ge(sv.h, sv.n)
            sv.inc(e.tensor_scalar(F2[0:1, 160:160 + nq_heads], F2[0:1, 128:128 + nq_heads],
                            -scale, None, ALU.mult))

        @blk.tensor
        def _(e):
            e.wait_ge(sv.h, sv.n)
            sp.inc(e.matmul(PS_B[0:128, 0:nq_heads], c.CONES[0:1, 0:128],
                     F2[0:1, 160:160 + nq_heads], start=True, stop=True))

        @blk.vector
        def _(e):
            e.wait_ge(sp.h, sp.n)
            sv.inc(e.tensor_copy(negc_cols, PS_B[:, 0:nq_heads]))


def emit_attention(c, QT, KT, Vext, outT, nq_heads, kv_group, scale, scratch_bf,
                   negc_cols, recip_bf, recip_tmp, KTflip=None):
    nc = c.nc
    sp, sv, sa = (c.sem(n) for n in ["sp", "sv", "sa"])
    PS_A, PS_B = c.PS_A, c.PS_B
    nkvh = nq_heads // kv_group
    vcols = nkvh * 65

    heads = list(range(nq_heads)) if DEBUG_HEADS is None else list(DEBUG_HEADS)

    def hparams(h):
        hc, hb = h // 2, 64 * (h % 2)
        g = h // kv_group
        kc = g // 2
        if 64 * (g % 2) == hb or KTflip is None:
            kbuf = KT
        else:
            kbuf = KTflip
        return hc, hb, g, kc, kbuf

    with nc.Block() as blk:
        emit_warmup(c, blk)
        seq = []
        for i, h in enumerate(heads):
            for kt in range(NT):
                if kt == 0:
                    if i > 0:
                        seq += [("AV", heads[i - 1], NT - 1, 0), ("AV", heads[i - 1], NT - 1, 1)]
                    seq += [("SC", h, 0, 0), ("SC", h, 0, 1)]
                    if i > 0:
                        seq += [("BC", heads[i - 1], 0, 0)]
                else:
                    seq += [("SC", h, kt, 0), ("SC", h, kt, 1),
                            ("AV", h, kt - 1, 0), ("AV", h, kt - 1, 1)]
        hl = heads[-1]
        seq += [("AV", hl, NT - 1, 0), ("AV", hl, NT - 1, 1), ("BC", hl, 0, 0)]
        P0 = sp.n
        pmark = {g: P0 + i + 1 for i, g in enumerate(seq)}
        E0 = sa.n
        emark = {}
        rmark = {}
        for i, h in enumerate(heads):
            for kt in range(NT):
                for hq in range(2):
                    emark[(h, kt, hq)] = E0 + i * (2 * NT + 2) + kt * 2 + hq + 1
            rmark[h] = E0 + (i + 1) * (2 * NT + 2)
        V0 = sv.n
        vmark_ev = {h: V0 + 2 * i + 1 for i, h in enumerate(heads)}
        vmark_mu = {h: V0 + 2 * i + 2 for i, h in enumerate(heads)}

        for i, h in enumerate(heads):
            @blk.scalar
            def _(e, h=h):
                for kt in range(NT):
                    for hq in range(2):
                        e.wait_ge(sp.h, pmark[("SC", h, kt, hq)])
                        if kt >= 2:
                            e.wait_ge(sp.h, pmark[("AV", h, kt - 2, hq)])
                        e.activation(scratch_bf[:, ((kt % 2) * 2 + hq) * 1024:
                                                ((kt % 2) * 2 + hq) * 1024 + 1024],
                                     PS_A[:, hq * 1024:(hq + 1) * 1024], AF.Exp,
                                     scale=scale, bias=negc_cols[:, h:h + 1]) \
                            .then_inc(sa.h, 1)
                e.wait_ge(sp.h, pmark[("AV", h, NT - 1, 1)])
                e.activation(recip_tmp, PS_B[64:65, 0:2048],
                             AF.Ln).then_inc(sa.h, 1)
                e.wait_ge(sa.h, rmark[h] - 1)
                e.activation(recip_bf, recip_tmp, AF.Exp,
                             scale=-1.0).then_inc(sa.h, 1)
                sa.n += 2 * NT + 2

        prevh = {heads[i]: heads[i - 1] for i in range(1, len(heads))}

        @blk.tensor
        def _(e):
            for g in seq:
                kind, h, kt, hq = g
                hc, hb, gg, kc, kbuf = hparams(h)
                if kind == "SC":
                    if kt == 0:
                        if h == heads[0]:
                            pass
                        else:
                            e.wait_ge(sa.h, emark[(prevh[h], NT - 1, hq)])
                    else:
                        e.wait_ge(sa.h, emark[(h, kt - 1, hq)])
                    last = None
                    for j in range(2):
                        last = e.matmul(
                            PS_A[:, hq * 1024 + j * 512:hq * 1024 + (j + 1) * 512],
                            kbuf[hb:hb + 64, kc * 2048 + kt * 128:kc * 2048 + (kt + 1) * 128],
                            QT[hb:hb + 64,
                               hc * 2048 + hq * 1024 + j * 512:hc * 2048 + hq * 1024 + (j + 1) * 512],
                            start=True, stop=True)
                    sp.inc(last)
                elif kind == "AV":
                    e.wait_ge(sa.h, emark[(h, kt, hq)])
                    if kt == 0 and hq == 0 and h != heads[0]:
                        e.wait_ge(sv.h, vmark_mu[prevh[h]])
                    vb = Vext[:, kt * vcols + (gg % nkvh) * 65:kt * vcols + (gg % nkvh) * 65 + 65]
                    pslot = ((kt % 2) * 2 + hq) * 1024
                    last = None
                    for j in range(2):
                        last = e.matmul(
                            PS_B[0:65, hq * 1024 + j * 512:hq * 1024 + (j + 1) * 512],
                            vb, scratch_bf[:, pslot + j * 512:pslot + (j + 1) * 512],
                            start=(kt == 0), stop=(kt == NT - 1))
                    sp.inc(last)
                else:
                    e.wait_ge(sv.h, vmark_ev[h])
                    e.wait_ge(sa.h, rmark[h])
                    last = None
                    for j in range(4):
                        last = e.matmul(PS_B[0:64, j * 512:(j + 1) * 512],
                                        c.CONESB[0:1, 0:64],
                                        recip_bf[:, j * 512:(j + 1) * 512],
                                        start=True, stop=True)
                    sp.inc(last)

        for i, h in enumerate(heads):
            @blk.vector
            def _(e, h=h):
                e.wait_ge(sp.h, pmark[("AV", h, NT - 1, 1)])
                sv.inc(e.tensor_copy(scratch_bf[0:64, 4096:6144], PS_B[0:64, 0:2048]))

            @blk.vector
            def _(e, h=h):
                hc, hb = h // 2, 64 * (h % 2)
                e.wait_ge(sp.h, pmark[("BC", h, 0, 0)])
                sv.inc(e.tensor_tensor(outT[hb:hb + 64, hc * 2048:(hc + 1) * 2048],
                                       scratch_bf[0:64, 4096:6144], PS_B[0:64, 0:2048],
                                       ALU.mult))


def emit_mamba(c):
    nc = c.nc
    sd, sg, sp, sv, sa = (c.sem(n) for n in ["sd", "sg", "sp", "sv", "sa"])
    sl = [c.sem("sl0"), c.sem("sl1")]
    ss = [c.sem("ss0"), c.sem("ss1")]
    W = c.d
    A1, A2, A3, A4, A5 = c.A1, c.A2, c.A3, c.A4, c.A5
    F1, F2, SM = c.F1, c.F2, c.SM
    F1b = c.F1.ap().bitcast(BF16)
    PS_A, PS_B = c.PS_A, c.PS_B
    pb = psab(c)
    pbb = psbb(c)

    with nc.Block() as blk:
        @blk.gpsimd
        def _(e):
            win = W["L1_attn_W_in"].ap().rearrange("(dc p) i -> p dc i", p=128)
            sg.inc(e.dma_start(
                out=A3.ap()[:, 0:12288].rearrange("p (dc i) -> p dc i", dc=8),
                in_=win[:, :, 0:1536]), 16)
            sg.inc(e.dma_start(
                out=A4.ap()[:, 0:12288].rearrange("p (dc i) -> p dc i", dc=8),
                in_=win[:, :, 1536:3072]), 16)
            sg.inc(e.dma_start(out=A2.ap()[0:16, 14016:15552],
                               in_=W["L1_attn_Bm"].ap()), 16)

        @blk.sync
        def _(e):
            sd.inc(e.dma_start(
                out=F2.ap()[:, 2752:2788].rearrange("p (ic j) -> p ic j", ic=12),
                in_=W["L1_attn_conv_w"].ap().rearrange("(ic p) j -> p ic j", p=128)), 16)
            sd.inc(e.dma_start(out=F2[0:16, 2704:2720], in_=W["L1_attn_A"].ap()), 16)
            sd.inc(e.dma_start(out=F2[0:1, 2944:3968],
                               in_=W["L1_attn_ln_w"].ap().rearrange("(a b) -> a b", a=1)), 16)
            sd.inc(e.dma_start(out=F2[0:1, 3100:3100] if False else F2[0:2, 3100:3100],
                               in_=W["L1_attn_ln_w"].ap().rearrange("(a b) -> a b", a=1)), 16) \
                if False else None
            sd.inc(e.dma_start(out=SM[0:1, 0:1024],
                               in_=W["L1_attn_ln_b"].ap().rearrange("(a b) -> a b", a=1)), 16)

        row_to_bcast(c, blk, SM[0:1, 0:1024], SM[:, 1024:2048], 1024,
                     waits=[(sd, sd.n)])
        row_to_bcast(c, blk, F2[0:1, 2944:3968], SM[:, 0:1024], 1024,
                     waits=[(sd, sd.n)])

        @blk.tensor
        def _(e):
            e.wait_ge(sg.h, sg.n)
            for ic in range(12):
                inst = e.transpose(pbb[:, 2048 + ic * 16:2048 + (ic + 1) * 16],
                                   A2[0:16, 14016 + ic * 128:14016 + (ic + 1) * 128],
                                   c.CIDENT[0:16, 0:16])
            sp.inc(inst)

        @blk.vector
        def _(e):
            e.wait_ge(sp.h, sp.n)
            sv.inc(e.tensor_copy(A2[:, 10240:10432], pbb[:, 2048:2240]))

    def xn1s(t):
        return F1[:, 2048 + (t % 4) * 1024:3072 + (t % 4) * 1024]

    hb = c.A2.ap()[:, 0:1024]
    with nc.Block() as blk:
        fronts = {}
        for i in range(NT + 3):
            if i < NT:
                slot = i % 2
                xt = F1[:, slot * 1024:(slot + 1) * 1024]

                @blk.sync
                def _(e, i=i, xt=xt, slot=slot):
                    e.wait_ge(ss[slot].h, ss[slot].n)
                    sl[slot].inc(e.dma_start(out=xt,
                                             in_=c.d["d_x1"].ap()[i * 128:(i + 1) * 128, :]), 16)

            jobs = []
            if i < NT:
                jobs.append(dict(xin=xt, outs=[xn1s(i)],
                                 stats=F2[:, 2790 + 8 * (i % 2):2794 + 8 * (i % 2)],
                                 scratch=F1[:, 6144:7168],
                                 waits=[(sl[slot], sl[slot].n), (ss[slot], ss[slot].n)]))
            if 2 <= i < NT + 2:
                t2 = i - 2
                jobs.append(dict(xin=xn1s(t2), outs=[hb],
                                 stats=F2[:, 2794 + 8 * (t2 % 2):2798 + 8 * (t2 % 2)],
                                 scratch=F2[:, (t2 % 2) * 1024:(t2 % 2) * 1024 + 1024],
                                 gam=SM[:, 0:1024], bet=SM[:, 1024:2048]))
            if jobs:
                fronts[i] = (jobs, ln_front(c, blk, jobs))
            if i - 1 in fronts:
                pj, pm = fronts.pop(i - 1)
                ln_back(c, blk, pj, pm)
                if i - 1 < NT:
                    @blk.gpsimd
                    def _(e, t=i - 1, slot=(i - 1) % 2):
                        e.wait_ge(sv.h, sv.n)
                        ss[slot].inc(e.dma_start(
                            out=c.d["d_xn1"].ap()[t * 128:(t + 1) * 128, :],
                            in_=xn1s(t)), 16)
                if 2 <= i - 1 < NT + 2:
                    transpose_to_T(c, blk, hb, A1, (i - 3) * 128, 8,
                                   psum_bank=(i - 3) % 2)
        if NT + 2 in fronts:
            pj, pm = fronts.pop(NT + 2)
            ln_back(c, blk, pj, pm)
            transpose_to_T(c, blk, hb, A1, (NT - 1) * 128, 8,
                           psum_bank=(NT - 1) % 2)

        @blk.gpsimd
        def _(e):
            e.wait_ge(ss[0].h, ss[0].n)
            e.wait_ge(ss[1].h, ss[1].n)

    def sg_slot(gc):
        if gc < 8:
            return (A3, gc * 2048)
        return (A2, 2048 + (gc - 8) * 2048)

    with nc.Block() as blk:
        last_read_mark = [None]
        for ic in range(24):
            data = ic < 12

            @blk.tensor
            def _(e, ic=ic):
                if last_read_mark[0] is not None:
                    eng, mark = last_read_mark[0]
                    e.wait_ge(eng.h, mark)
                src = A3 if ic < 12 else A4
                ii = ic % 12
                for j in range(4):
                    for dc in range(8):
                        sp.inc(e.matmul(
                            PS_A[:, j * 512:(j + 1) * 512],
                            src[:, dc * 1536 + ii * 128:dc * 1536 + (ii + 1) * 128],
                            A1[:, dc * 2048 + j * 512:dc * 2048 + (j + 1) * 512],
                            start=(dc == 0), stop=(dc == 7)))

            if data:
                @blk.vector
                def _(e, ic=ic):
                    e.wait_ge(sp.h, sp.n)
                    wj = lambda j: F2[:, 2752 + ic * 3 + j:2752 + ic * 3 + j + 1]
                    cv = F1[:, 4096:6144]
                    sv.inc(e.tensor_scalar(cv, PS_A[:, 0:2048], wj(2), None, ALU.mult))
                    e.wait_ge(sv.h, sv.n)
                    sv.inc(e.scalar_tensor_tensor(cv[:, 1:2048], PS_A[:, 0:2047], wj(1),
                                                  cv[:, 1:2048], ALU.mult, ALU.add))
                    e.wait_ge(sv.h, sv.n)
                    sv.inc(e.scalar_tensor_tensor(cv[:, 2:2048], PS_A[:, 0:2046], wj(0),
                                                  cv[:, 2:2048], ALU.mult, ALU.add))
                    last_read_mark[0] = (sv, sv.n)

                @blk.scalar
                def _(e):
                    e.wait_ge(sv.h, sv.n)
                    sa.inc(e.activation(A2[:, 0:2048], F1[:, 4096:6144], AF.Silu))

                @blk.tensor
                def _(e, ic=ic):
                    e.wait_ge(sa.h, sa.n)
                    for j in range(4):
                        sp.inc(e.matmul(
                            PS_B[0:16, j * 512:(j + 1) * 512],
                            A2[:, 10240 + ic * 16:10240 + (ic + 1) * 16],
                            A2[:, j * 512:(j + 1) * 512],
                            start=(ic == 0), stop=(ic == 11),
                            skip_group_check=True))
            else:
                @blk.scalar
                def _(e, ic=ic):
                    e.wait_ge(sp.h, sp.n)
                    gc = ic - 12
                    buf, off = sg_slot(gc)
                    sa.inc(e.activation(buf[:, off:off + 2048], PS_A[:, 0:2048],
                                        AF.Sigmoid))
                    last_read_mark[0] = (sa, sa.n)

            if ic == 11:
                @blk.vector
                def _(e):
                    e.wait_ge(sp.h, sp.n)
                    sv.inc(e.tensor_copy(F2[0:16, 0:2048], PS_B[0:16, 0:2048]))

    emit_scan(c)

    with nc.Block() as blk:
        @blk.gpsimd
        def _(e):
            sg.inc(e.dma_start(
                out=A4.ap()[:, 0:12288].rearrange("p (ic o) -> p ic o", ic=12),
                in_=W["L1_attn_W_out"].ap().rearrange("(ic p) o -> p ic o", p=128)), 16)
            sg.inc(e.dma_start(out=A2.ap()[0:16, 14016:14208],
                               in_=W["L1_attn_Cm"].ap().rearrange("(ic p) s -> p (ic s)", p=128)
                               if False else W["L1_attn_Cm"].ap()
                               .rearrange("(a b) s -> a (b s)", a=16) if False else
                               W["L1_attn_Cm"].ap()), 16) if False else None
            sg.inc(e.dma_start(
                out=A2.ap()[:, 15552:15744].rearrange("p (ic s) -> p ic s", ic=12),
                in_=W["L1_attn_Cm"].ap().rearrange("(ic p) s -> p ic s", p=128)), 16)

        @blk.tensor
        def _(e):
            e.wait_ge(sg.h, sg.n)
            for ic in range(12):
                inst = e.transpose(pbb[0:16, 1536 + ic * 128:1536 + (ic + 1) * 128],
                                   A2[:, 15552 + ic * 16:15552 + (ic + 1) * 16],
                                   c.CIDENT[:])
            sp.inc(inst)

        @blk.vector
        def _(e):
            e.wait_ge(sp.h, sp.n)
            sv.inc(e.tensor_copy(A2[0:16, 10432:11968], pbb[0:16, 1536:3072]))
            e.wait_ge(sv.h, sv.n)
            sv.inc(e.tensor_copy(A2[0:16, 11968:14016], F2[0:16, 0:2048]))

        def g_slot(gc):
            if gc == 0:
                return (A2, 0)
            if gc <= 8:
                return (A3, (gc - 1) * 2048)
            return (A2, 2048 + (gc - 9) * 2048)

        gmark = {}
        for gc in range(12):
            psb = PS_A if gc % 2 == 0 else PS_B

            @blk.tensor
            def _(e, gc=gc, psb=psb):
                if gc == 0:
                    e.wait_ge(sv.h, sv.n)
                if gc >= 2:
                    e.wait_ge(sv.h, gmark[gc - 2])
                for j in range(4):
                    sp.inc(e.matmul(psb[:, j * 512:(j + 1) * 512],
                                    A2[0:16, 10432 + gc * 128:10432 + (gc + 1) * 128],
                                    A2[0:16, 11968 + j * 512:11968 + (j + 1) * 512],
                                    start=True, stop=True))

            @blk.vector
            def _(e, gc=gc, psb=psb):
                e.wait_ge(sp.h, sp.n)
                if gc >= 1:
                    e.wait_ge(sv.h, gmark[gc - 1])
                sbuf, soff = sg_slot(gc)
                gbuf, goff = g_slot(gc)
                sv.inc(e.tensor_tensor(gbuf[:, goff:goff + 2048],
                                       sbuf[:, soff:soff + 2048],
                                       psb[:, 0:2048], ALU.mult))
                gmark[gc] = sv.n

        gdone = sv.n
        evmark = {}
        for t in range(NT):
            slot = t % 2
            xt = F1[:, slot * 1024:(slot + 1) * 1024]

            @blk.sync
            def _(e, t=t, xt=xt, slot=slot):
                e.wait_ge(ss[slot].h, ss[slot].n)
                sl[slot].inc(e.dma_start(out=xt,
                                         in_=c.d["d_xn1"].ap()[t * 128:(t + 1) * 128, :]), 16)

            @blk.tensor
            def _(e, t=t):
                e.wait_ge(sv.h, evmark[t - 2] if t >= 2 else gdone)
                ps = PS_B[:, (t % 2) * 1024:(t % 2) * 1024 + 1024]
                for ns in range(2):
                    for ic in range(12):
                        gbuf, goff = g_slot(ic)
                        sp.inc(e.matmul(
                            ps[:, ns * 512:(ns + 1) * 512],
                            gbuf[:, goff + t * 128:goff + (t + 1) * 128],
                            A4[:, ic * 1024 + ns * 512:ic * 1024 + (ns + 1) * 512],
                            start=(ic == 0), stop=(ic == 11)))

            @blk.vector
            def _(e, t=t, xt=xt, slot=slot):
                e.wait_ge(sp.h, sp.n)
                e.wait_ge(sl[slot].h, sl[slot].n)
                ps = PS_B[:, (t % 2) * 1024:(t % 2) * 1024 + 1024]
                sv.inc(e.scalar_tensor_tensor(xt, ps, 1.0, xt, ALU.mult, ALU.add))
                evmark[t] = sv.n

            @blk.gpsimd
            def _(e, t=t, xt=xt, slot=slot):
                e.wait_ge(sv.h, sv.n)
                ss[slot].inc(e.dma_start(out=c.d["d_attn1"].ap()[t * 128:(t + 1) * 128, :],
                                         in_=xt), 16)

        @blk.gpsimd
        def _(e):
            e.wait_ge(ss[0].h, ss[0].n)
            e.wait_ge(ss[1].h, ss[1].n)


def emit_scan(c):
    nc = c.nc
    sg, sp, sv = (c.sem(n) for n in ["sg", "sp", "sv"])
    F1, F2 = c.F1, c.F2
    PS_A, PS_B = c.PS_A, c.PS_B
    IDF16 = c.CIDENTF[0:16, 0:16]

    with nc.Block() as blk:
        @blk.tensor
        def _(e):
            e.wait_ge(sv.h, sv.n)
            sp.inc(e.transpose(PS_B[0:16, 0:16], F2[0:16, 2704:2720], IDF16))

        @blk.vector
        def _(e):
            e.wait_ge(sp.h, sp.n)
            sv.inc(e.tensor_tensor(F2[0:16, 2064:2080], PS_B[0:16, 0:16],
                                   c.CIDENTF[0:16, 0:16], ALU.add))
            e.wait_ge(sv.h, sv.n)
            sv.inc(e.tensor_copy(F2[0:16, 2048:2064], c.CIDENTF[0:16, 0:16]))

        @blk.tensor
        def _(e):
            e.wait_ge(sv.h, sv.n)
            sp.inc(e.transpose(PS_B[0:16, 16:32], F2[0:16, 2064:2080], IDF16))

        @blk.vector
        def _(e):
            e.wait_ge(sp.h, sp.n)
            sv.inc(e.tensor_copy(F2[0:16, 2720:2736], PS_B[0:16, 16:32]))

        for j in range(2, 17):
            dst = (2048 + 16 * j) if j <= 15 else 2304 + 15 * 16
            @blk.tensor
            def _(e, j=j):
                e.wait_ge(sv.h, sv.n)
                sp.inc(e.matmul(PS_B[0:16, 32:48], F2[0:16, 2720:2736],
                                F2[0:16, 2048 + 16 * (j - 1):2048 + 16 * j],
                                start=True, stop=True))

            @blk.vector
            def _(e, dst=dst):
                e.wait_ge(sp.h, sp.n)
                sv.inc(e.tensor_copy(F2[0:16, dst:dst + 16], PS_B[0:16, 32:48]))

        @blk.vector
        def _(e):
            e.wait_ge(sv.h, sv.n)
            sv.inc(e.tensor_copy(F2[0:16, 2304:2544], F2[0:16, 2064:2304]))

        @blk.tensor
        def _(e):
            e.wait_ge(sv.h, sv.n)
            sp.inc(e.transpose(PS_B[0:16, 48:64], F2[0:16, 2544:2560], IDF16))

        @blk.vector
        def _(e):
            e.wait_ge(sp.h, sp.n)
            sv.inc(e.tensor_copy(F2[0:16, 2736:2752], PS_B[0:16, 48:64]))
            e.wait_ge(sv.h, sv.n)
            e.tensor_copy(F2[0:16, 2560:2576], c.CIDENTF[0:16, 0:16])
            sv.inc(e.tensor_copy(F2[0:16, 2576:2592], F2[0:16, 2544:2560]))
        for j in range(2, 9):
            dst = (2560 + 16 * j) if j <= 7 else 2688
            @blk.tensor
            def _(e, j=j):
                e.wait_ge(sv.h, sv.n)
                sp.inc(e.matmul(PS_B[0:16, 32:48], F2[0:16, 2736:2752],
                                F2[0:16, 2560 + 16 * (j - 1):2560 + 16 * j],
                                start=True, stop=True))

            @blk.vector
            def _(e, dst=dst):
                e.wait_ge(sp.h, sp.n)
                sv.inc(e.tensor_copy(F2[0:16, dst:dst + 16], PS_B[0:16, 32:48]))

        @blk.vector
        def _(e):
            e.wait_ge(sv.h, sv.n)
            sv.inc(e.memset(F1[:, 4096:5008], 0.0))

        @blk.gpsimd
        def _(e):
            e.wait_ge(sv.h, sv.n)
            for k in range(2):
                for il in range(8):
                    i = 8 * k + il
                    w = (16 - i) * 16
                    sg.inc(e.dma_start(
                        out=F1.ap()[16 * il:16 * (il + 1),
                                    4096 + k * 256 + i * 16:4096 + k * 256 + 256],
                        in_=F2.ap()[0:16, 2048:2048 + w]), 16)
            for il in range(8):
                w = (8 - il) * 16
                sg.inc(e.dma_start(
                    out=F1.ap()[16 * il:16 * (il + 1),
                                4864 + il * 16:4864 + 128],
                    in_=F2.ap()[0:16, 2560:2560 + w]), 16)
            for k in range(2):
                for il in range(8):
                    sg.inc(e.dma_start(
                        out=F1.ap()[16 * il:16 * (il + 1),
                                    4608 + k * 128:4608 + k * 128 + 128],
                        in_=bass.AP(c.F2, 8 * k + il, [[4096, 16], [16, 128]]),
                        allow_slow_non_contiguous=True), 16)

        @blk.tensor
        def _(e):
            e.wait_ge(sg.h, sg.n)
            e.matmul(PS_A[:, 0:256], F1[:, 4608:4736], F1[:, 4096:4352],
                     start=True, stop=False)
            sp.inc(e.matmul(PS_A[:, 0:256], F1[:, 4736:4864], F1[:, 4352:4608],
                            start=False, stop=True))

        @blk.vector
        def _(e):
            e.wait_ge(sp.h, sp.n)
            sv.inc(e.tensor_copy(F1[:, 5008:5024], PS_A[:, 240:256]))

        @blk.tensor
        def _(e):
            e.wait_ge(sv.h, sv.n)
            sp.inc(e.transpose(PS_B[0:16, 64:192], F1[:, 5008:5024], c.CIDENTF[:]))

        @blk.vector
        def _(e):
            e.wait_ge(sp.h, sp.n)
            sv.inc(e.tensor_copy(F1[0:16, 5024:5152], PS_B[0:16, 64:192]))

        @blk.gpsimd
        def _(e):
            e.wait_ge(sv.h, sv.n)
            for il in range(8):
                sg.inc(e.dma_start(
                    out=F1.ap()[16 * il:16 * (il + 1), 4992:5008],
                    in_=bass.AP(c.F1, 5024 + il, [[8192, 16], [8, 16]]),
                    allow_slow_non_contiguous=True), 16)

        @blk.tensor
        def _(e):
            e.wait_ge(sg.h, sg.n)
            sp.inc(e.matmul(PS_B[0:16, 512:640], F1[:, 4992:5008], F1[:, 4864:4992],
                            start=True, stop=True))

        @blk.vector
        def _(e):
            e.wait_ge(sp.h, sp.n)
            sv.inc(e.tensor_copy(F1[0:16, 5168:5296], PS_B[0:16, 512:640]))

        @blk.tensor
        def _(e):
            e.wait_ge(sv.h, sv.n)
            sp.inc(e.transpose(PS_B[0:16, 640:656], F1[0:16, 5280:5296], IDF16))

        @blk.vector
        def _(e):
            e.wait_ge(sp.h, sp.n)
            sv.inc(e.tensor_copy(F1[0:16, 5296:5312], PS_B[0:16, 640:656]))
            e.wait_ge(sv.h, sv.n)
            sv.inc(e.memset(F1[0:16, 5312:5313], 0.0))

        for C in range(1, 16):
            @blk.tensor
            def _(e, C=C):
                e.wait_ge(sv.h, sv.n)
                sp.inc(e.matmul(PS_B[0:16, 656:657], F2[0:16, 2688:2704],
                                F1[0:16, 5312 + C - 1:5312 + C], start=True, stop=True))

            @blk.vector
            def _(e, C=C):
                e.wait_ge(sp.h, sp.n)
                sv.inc(e.tensor_tensor(F1[0:16, 5312 + C:5313 + C],
                                       PS_B[0:16, 656:657],
                                       F1[0:16, 5296 + C - 1:5297 + C - 1] if False else
                                       F1[0:16, 5296 + (C - 1):5296 + C], ALU.add))

        for I in range(8):
            @blk.tensor
            def _(e, I=I):
                e.wait_ge(sv.h, sv.n)
                sp.inc(e.matmul(PS_B[0:16, 672:688], F2[0:16, 2560 + 16 * I:2576 + 16 * I],
                                F1[0:16, 5312:5328], start=True, stop=True))

            @blk.vector
            def _(e, I=I):
                e.wait_ge(sp.h, sp.n)
                dst = bass.AP(c.F1, 5328 + I, [[8192, 16], [8, 16]])
                sv.inc(e.tensor_copy(dst, PS_B[0:16, 672:688]))
        for I in range(1, 8):
            @blk.tensor
            def _(e, I=I):
                e.wait_ge(sv.h, sv.n)
                sp.inc(e.transpose(PS_B[0:16, 688:704],
                                   F1[0:16, 5168 + (I - 1) * 16:5168 + I * 16], IDF16))

            @blk.vector
            def _(e, I=I):
                e.wait_ge(sp.h, sp.n)
                dst = bass.AP(c.F1, 5328 + I, [[8192, 16], [8, 16]])
                sv.inc(e.tensor_tensor(dst, dst, PS_B[0:16, 688:704], ALU.add))

        @blk.tensor
        def _(e):
            e.wait_ge(sv.h, sv.n)
            sp.inc(e.matmul(PS_A[:, 0:256], F1[0:16, 5328:5456], F2[0:16, 2304:2560],
                            start=False, stop=True, skip_group_check=True))

        @blk.vector
        def _(e):
            e.wait_ge(sp.h, sp.n)
            sv.inc(e.tensor_copy(F1[:, 5456:5712], PS_A[:, 0:256]))
        for j in range(16):
            @blk.tensor
            def _(e, j=j):
                e.wait_ge(sv.h, sv.n)
                sp.inc(e.transpose(PS_B[0:16, 704:832],
                                   F1[:, 5456 + j * 16:5456 + (j + 1) * 16], c.CIDENTF[:]))

            @blk.vector
            def _(e, j=j):
                e.wait_ge(sp.h, sp.n)
                dst = bass.AP(c.F2, j, [[4096, 16], [16, 128]])
                sv.inc(e.tensor_copy(dst, PS_B[0:16, 704:832]))


def rms_tile(c, blk, xin, out, wbcast, stats, scratch, eps_col=41, waits=()):
    sv, sa = c.sem("sv"), c.sem("sa")
    var, rstd = stats[:, 0:1], stats[:, 1:2]
    sv_entry = sv.n

    @blk.vector
    def _(e):
        for (s, v) in waits:
            e.wait_ge(s.h, v)
        e.wait_ge(sv.h, sv_entry)
        sv.inc(e.tensor_tensor(scratch, xin, xin, ALU.mult))
        e.wait_ge(sv.h, sv.n)
        sv.inc(e.reduce_sum(var, scratch, AX.X))
        e.wait_ge(sv.h, sv.n)
        sv.inc(e.tensor_scalar(var, var, 1.0 / 1024.0, None, ALU.mult))

    @blk.scalar
    def _(e):
        e.wait_ge(sv.h, sv.n)
        sa.inc(e.activation(stats[:, 2:3], var, AF.Ln, bias=c.CMISC[:, eps_col:eps_col + 1]))
        e.wait_ge(sa.h, sa.n)
        sa.inc(e.activation(rstd, stats[:, 2:3], AF.Exp, scale=-0.5))

    @blk.vector
    def _(e):
        e.wait_ge(sa.h, sa.n)
        sv.inc(e.tensor_scalar(out, xin, rstd, None, ALU.mult))
        e.wait_ge(sv.h, sv.n)
        sv.inc(e.tensor_tensor(out, out, wbcast, ALU.mult))


def emit_gqa(c):
    nc = c.nc
    sd, sg, sp, sv, sa = (c.sem(n) for n in ["sd", "sg", "sp", "sv", "sa"])
    sl = [c.sem("sl0"), c.sem("sl1")]
    ss = [c.sem("ss0"), c.sem("ss1")]
    W = c.d
    A1, A2, A3, A4, A5 = c.A1, c.A2, c.A3, c.A4, c.A5
    F1, F2, SM = c.F1, c.F2, c.SM
    F1b = c.F1.ap().bitcast(BF16)
    PS_A, PS_B = c.PS_A, c.PS_B

    with nc.Block() as blk:
        @blk.gpsimd
        def _(e):
            sg.inc(e.dma_start(
                out=A2.ap()[:, 0:8192].rearrange("p (dc ee) -> p dc ee", dc=8),
                in_=W["L1_ffn_qw"].ap().rearrange("(dc p) ee -> p dc ee", p=128)), 16)
            sg.inc(e.dma_start(
                out=A2.ap()[:, 8192:16384].rearrange("p (dc ee) -> p dc ee", dc=8),
                in_=W["L1_ffn_ow"].ap().rearrange("(dc p) ee -> p dc ee", p=128)), 16)
            sg.inc(e.dma_start(
                out=A5.ap()[:, 0:2048].rearrange("p (dc ee) -> p dc ee", dc=8),
                in_=W["L1_ffn_kw"].ap().rearrange("(dc p) ee -> p dc ee", p=128)), 16)
            sg.inc(e.dma_start(
                out=A5.ap()[:, 2048:4096].rearrange("p (dc ee) -> p dc ee", dc=8),
                in_=W["L1_ffn_vw"].ap().rearrange("(dc p) ee -> p dc ee", p=128)), 16)

        @blk.sync
        def _(e):
            sd.inc(e.dma_start(out=bass.AP(c.SM, 1024, [[2048, 8], [1, 128]]),
                               in_=W["L1_ffn_qb"].ap().rearrange("(cc p) -> cc p", p=128)), 16)
            sd.inc(e.dma_start(out=bass.AP(c.SM, 1152, [[2048, 2], [1, 128]]),
                               in_=W["L1_ffn_kb"].ap().rearrange("(cc p) -> cc p", p=128)), 16)
            sd.inc(e.dma_start(out=SM[0:1, 0:256],
                               in_=W["L1_ffn_vb"].ap().rearrange("(a b) -> a b", a=1)), 16)
            sd.inc(e.dma_start(out=F2[0:1, 2944:3968],
                               in_=W["L1_ffn_norm1_w"].ap().rearrange("(a b) -> a b", a=1)), 16)

        @blk.tensor
        def _(e):
            e.wait_ge(sd.h, sd.n)
            e.wait_ge(sv.h, sv.n)
            e.transpose(PS_B[:, 0:8], bass.AP(c.SM, 1024, [[2048, 8], [1, 128]]),
                        c.CIDENTF[0:8, 0:8])
            sp.inc(e.transpose(PS_B[:, 8:10], bass.AP(c.SM, 1152, [[2048, 2], [1, 128]]),
                               c.CIDENTF[0:2, 0:2]))

        @blk.vector
        def _(e):
            e.wait_ge(sp.h, sp.n)
            e.tensor_copy(F2[:, 3970:3978], PS_B[:, 0:8])
            sv.inc(e.tensor_copy(F2[:, 3978:3980], PS_B[:, 8:10]))

        row_to_bcast(c, blk, F2[0:1, 2944:3968], SM[:, 1024:2048], 1024,
                     waits=[(sd, sd.n)])

    if GQA_MAX_PHASE < 1:
        return
    with nc.Block() as blk:
        for t in range(NT):
            slot = t % 2
            xt = F1[:, slot * 1024:(slot + 1) * 1024]

            @blk.sync
            def _(e, t=t, xt=xt, slot=slot):
                if t >= 2:
                    e.wait_ge(sv.h, sv.n)
                sl[slot].inc(e.dma_start(out=xt,
                                         in_=c.d["d_xn1"].ap()[t * 128:(t + 1) * 128, :]), 16)

            @blk.vector
            def _(e, xt=xt, slot=slot):
                e.wait_ge(sl[slot].h, sl[slot].n)
                e.wait_ge(sv.h, sv.n)
                sv.inc(e.tensor_copy(F1b[:, 10240:11264], xt))

            transpose_to_T(c, blk, F1b[:, 10240:11264], A1, t * 128, 8, psum_bank=t % 2)

    if GQA_MAX_PHASE < 2:
        return
    emit_qkv(c, xnT=A1.ap(), wq=(A2, 0), wk=(A5, 0), wv=(A5, 2048),
             QT=A4.ap(), KT=F1b[:, 0:4096], Vext=F1b[:, 8192:12352],
             bq_cols=F2[:, 3970:3978], bk_cols=F2[:, 3978:3980],
             bv_row=SM[0:1, 0:256], bvb=SM[:, 0:256], nk_e=256, nv=256,
             entry_waits=[(sg, sg.n)])

    with nc.Block() as blk:
        @blk.vector
        def _(e):
            for ch in range(2):
                e.tensor_copy(F1b[64:128, 4096 + ch * 2048:4096 + (ch + 1) * 2048],
                              F1b[0:64, ch * 2048:(ch + 1) * 2048])
                sv.inc(e.tensor_copy(F1b[0:64, 4096 + ch * 2048:4096 + (ch + 1) * 2048],
                                     F1b[64:128, ch * 2048:(ch + 1) * 2048]))

    if GQA_MAX_PHASE < 3:
        return
    emit_norm_bounds(c, QT=A4.ap(), KT=F1b[:, 0:4096], nq_heads=16, nkv=4,
                     kv_group=4, scale=0.125, sq_scratch=A5[:, 4096:6144],
                     negc_cols=F2[:, 3980:3996])
    emit_attention(c, QT=A4.ap(), KT=F1b[:, 0:4096], Vext=F1b[:, 8192:12352],
                   outT=A3.ap(), nq_heads=16, kv_group=4, scale=0.125,
                   scratch_bf=A5.ap(), negc_cols=F2[:, 3980:3996],
                   recip_bf=F2.ap().bitcast(BF16)[0:1, 0:2048],
                   recip_tmp=F2[0:1, 1024:3072], KTflip=F1b[:, 4096:8192])

    if GQA_MAX_PHASE < 5:
        return
    with nc.Block() as blk:
        @blk.sync
        def _(e):
            sd.inc(e.dma_start(out=SM[0:1, 0:1024],
                               in_=W["L1_ffn_ob"].ap().rearrange("(a b) -> a b", a=1)), 16)

        row_to_bcast(c, blk, SM[0:1, 0:1024], SM[:, 0:1024], 1024,
                     waits=[(sd, sd.n)])

        sb = [c.sem("sgg0"), c.sem("sgg1")]
        evmark = {}
        for tp in range(NT // 2):
            for k in range(2):
                t = 2 * tp + k
                xt = F1[:, k * 1024:(k + 1) * 1024]

                @blk.sync
                def _(e, t=t, xt=xt, k=k):
                    e.wait_ge(ss[k].h, ss[k].n)
                    sl[k].inc(e.dma_start(out=xt,
                                          in_=c.d["d_xn1"].ap()[t * 128:(t + 1) * 128, :]), 16)

            @blk.tensor
            def _(e, tp=tp):
                if tp >= 1:
                    e.wait_ge(sv.h, evmark[tp - 1])
                for k in range(2):
                    t = 2 * tp + k
                    ps = PS_A[:, k * 1024:(k + 1) * 1024]
                    for ns in range(2):
                        for dc in range(8):
                            sp.inc(e.matmul(
                                ps[:, ns * 512:(ns + 1) * 512],
                                A3[:, dc * 2048 + t * 128:dc * 2048 + (t + 1) * 128],
                                A2[:, 8192 + dc * 1024 + ns * 512:8192 + dc * 1024 + (ns + 1) * 512],
                                start=(dc == 0), stop=(dc == 7)))

            @blk.vector
            def _(e, tp=tp):
                e.wait_ge(sp.h, sp.n)
                for k in range(2):
                    e.wait_ge(sl[k].h, sl[k].n)
                xts = [F1[:, 0:1024], F1[:, 1024:2048]]
                last = None
                for k in range(2):
                    last = e.scalar_tensor_tensor(xts[k], PS_A[:, k * 1024:(k + 1) * 1024],
                                                  1.0, xts[k], ALU.mult, ALU.add)
                sv.inc(last)
                evmark[tp] = sv.n
                e.wait_ge(sv.h, sv.n)
                last = None
                for k in range(2):
                    last = e.tensor_tensor(xts[k], xts[k], SM[:, 0:1024], ALU.add)
                sv.inc(last)

            rms_jobs(c, blk, [dict(xin=F1[:, k * 1024:(k + 1) * 1024],
                                   out=F1[:, k * 1024:(k + 1) * 1024],
                                   wbcast=SM[:, 1024:2048],
                                   stats=F2[:, 2790 + 4 * k:2793 + 4 * k],
                                   scratch=F1[:, 2048 + k * 1024:3072 + k * 1024])
                              for k in range(2)])

            cmarks = {}
            for k in range(2):
                @blk.vector
                def _(e, tp=tp, k=k):
                    e.wait_ge(sv.h, sv.n)
                    if tp >= 1:
                        e.wait_ge(sb[k].h, sb[k].n)
                    sv.inc(e.tensor_copy(
                        F1.ap().bitcast(BF16)[:, 10240 + k * 1024:11264 + k * 1024],
                        F1[:, k * 1024:(k + 1) * 1024]))
                    cmarks[k] = c.sem("sv").n

                transpose_to_T(c, blk,
                               F1.ap().bitcast(BF16)[:, 10240 + k * 1024:11264 + k * 1024],
                               A1, (2 * tp + k) * 128, 8, psum_bank=4 + k,
                               wait_mark=cmarks[k])

                @blk.gpsimd
                def _(e, tp=tp, k=k, cm=None):
                    e.wait_ge(sv.h, cmarks[k])
                    sb[k].inc(e.dma_start(
                        out=c.d["d_srcb"].ap()[(2 * tp + k) * 128:(2 * tp + k + 1) * 128, :],
                        in_=F1.ap().bitcast(BF16)[:, 10240 + k * 1024:11264 + k * 1024]), 16)
                    ss[k].inc(e.dma_start(
                        out=c.d["d_src"].ap()[(2 * tp + k) * 128:(2 * tp + k + 1) * 128, :],
                        in_=F1[:, k * 1024:(k + 1) * 1024]), 16)

        @blk.gpsimd
        def _(e):
            e.wait_ge(ss[0].h, ss[0].n)
            e.wait_ge(ss[1].h, ss[1].n)
            e.wait_ge(sb[0].h, sb[0].n)
            e.wait_ge(sb[1].h, sb[1].n)

def emit_moe(c):
    nc = c.nc
    sd, sg, sp, sv, sa = (c.sem(n) for n in ["sd", "sg", "sp", "sv", "sa"])
    sl = [c.sem("sl0"), c.sem("sl1")]
    ss = [c.sem("ss0"), c.sem("ss1")]
    sge = [c.sem("sge0"), c.sem("sge1")]
    scc = c.sem("scc")
    W = c.d
    A1, A2, A3, A4, A5 = c.A1, c.A2, c.A3, c.A4, c.A5
    F1, F2, SM = c.F1, c.F2, c.SM
    F1b = c.F1.ap().bitcast(BF16)
    F2b = c.F2.ap().bitcast(BF16)
    PS_A, PS_B = c.PS_A, c.PS_B
    SI = c.SI

    BIG = 8.0 * S_MAX

    with nc.Block() as blk:
        @blk.gpsimd
        def _(e):
            sg.inc(e.dma_start(
                out=A5.ap()[:, 7168:7232].rearrange("p (dc ee) -> p dc ee", dc=8),
                in_=W["L1_ffn_gate_w"].ap().rearrange("(dc p) ee -> p dc ee", p=128)), 16)

        @blk.sync
        def _(e):
            sd.inc(e.dma_start(out=F2[0:1, 640:1664],
                               in_=W["L1_ffn_norm2_w"].ap().rearrange("(a b) -> a b", a=1)), 16)
            sd.inc(e.dma_start(out=F2[0:NCORES, 2700:2701], in_=c.d["c_prefmask"].ap()), 16)

        row_to_bcast(c, blk, F2[0:1, 640:1664], SM[:, 1024:2048], 1024,
                     waits=[(sd, sd.n)])

        @blk.vector
        def _(e):
            e.wait_ge(sv.h, sv.n)
            e.memset(F1[:, 0:1024], 0.0)
            e.memset(F2[0:1, 400:408], 0.0)
            e.memset(F2[0:1, 424:432], 0.0)
            sv.inc(e.tensor_scalar(F2[:, 248:256], c.CMISC[:, 0:8], 999.0, None,
                                   ALU.subtract))
            e.wait_ge(sv.h, sv.n)
            sv.inc(e.memset(F2[:, 620:624], 0.0))
            e.wait_ge(sv.h, sv.n)
            sv.inc(e.memset(F2[:, 622:623], 2.0 * T))
            e.wait_ge(sv.h, sv.n)
            e.tensor_copy(bass.AP(F2, 1538, [[4096, 128], [4, 32], [0, 1]]),
                          bass.AP(c.CMISC, 8, [[48, 128], [1, 32], [0, 1]]))
            e.tensor_scalar(bass.AP(F2, 1536, [[4096, 128], [8, 16], [0, 1]]),
                            bass.AP(c.CMISC, 8, [[48, 128], [2, 16], [0, 1]]),
                            0.5, None, ALU.mult)
            e.tensor_scalar(bass.AP(F2, 1540, [[4096, 128], [8, 16], [0, 1]]),
                            bass.AP(c.CMISC, 9, [[48, 128], [2, 16], [0, 1]]),
                            0.5, -0.5, ALU.mult, ALU.add)
            sv.inc(e.memset(bass.AP(F2, 1539, [[4096, 128], [4, 32], [0, 1]]), 0.0))

        @blk.gpsimd
        def _(e):
            e.wait_ge(sv.h, sv.n)
            for i in range(32):
                sg.inc(e.dma_start(out=c.d["d_moeout"].ap()[i * 128:(i + 1) * 128, :],
                                   in_=F1[:, 0:1024]), 16)
            for i in range(8 * S_MAX // 128):
                sg.inc(e.dma_start(out=c.d["d_slotmeta"].ap()[i * 128:(i + 1) * 128, :],
                                   in_=F2[:, 620:624]), 16)

    with nc.Block() as blk:
        if MODE != "fused":
            for t in range(NT):
                slot = t % 2
                xt = F1[:, 1024 + slot * 1024:2048 + slot * 1024]

                @blk.sync
                def _(e, t=t, xt=xt, slot=slot):
                    if t >= 2:
                        e.wait_ge(sv.h, sv.n)
                    sl[slot].inc(e.dma_start(out=xt,
                                             in_=c.d["d_src"].ap()[t * 128:(t + 1) * 128, :]), 16)

                @blk.vector
                def _(e, xt=xt, slot=slot):
                    e.wait_ge(sl[slot].h, sl[slot].n)
                    e.wait_ge(sv.h, sv.n)
                    sv.inc(e.tensor_copy(F1b[:, 14336:15360], xt))

                transpose_to_T(c, blk, F1b[:, 14336:15360], A1, t * 128, 8, psum_bank=t % 2)

        @blk.tensor
        def _(e):
            e.wait_ge(sv.h, sv.n)
            e.wait_ge(sg.h, sg.n)
            for dc in range(8):
                for j in range(4):
                    sp.inc(e.matmul(PS_B[0:8, j * 512:(j + 1) * 512],
                                    A5[:, 7168 + dc * 8:7168 + (dc + 1) * 8],
                                    A1[:, dc * 2048 + j * 512:dc * 2048 + (j + 1) * 512],
                                    start=(dc == 0), stop=(dc == 7)))

        A5f = c.A5.ap().bitcast(F32)
        @blk.vector
        def _(e):
            e.wait_ge(sp.h, sp.n)
            sv.inc(e.tensor_copy(A5f[0:8, 0:2048], PS_B[0:8, 0:2048]))

        for tl in range(NT):
            @blk.tensor
            def _(e, tl=tl):
                e.wait_ge(sv.h, sv.n)
                sp.inc(e.transpose(PS_A[0:128, 8 * (tl % 2):8 * (tl % 2) + 8],
                                   A5f[0:8, tl * 128:(tl + 1) * 128],
                                   c.CIDENTF[0:8, 0:8]))

            @blk.vector
            def _(e, tl=tl):
                e.wait_ge(sp.h, sp.n)
                sv.inc(e.tensor_copy(F2[:, 8 * tl:8 * tl + 8],
                                     PS_A[:, 8 * (tl % 2):8 * (tl % 2) + 8]))

    def slab(e_idx, hh, which):
        buf = (A1 if which == 0 else A2) if e_idx % 2 == 0 else (A3 if which == 0 else A4)
        return buf, hh * 8192

    sgg = [c.sem("sgg0"), c.sem("sgg1")]
    sgs = [c.sem("sgs0"), c.sem("sgs1")]
    smm = [c.sem("smm0"), c.sem("smm1")]

    def emit_slab_loads(e, e_idx):
        pe_par = e_idx % 2
        for hh in range(2):
            b1, o1 = slab(e_idx, hh, 0)
            sge[pe_par].inc(e.dma_start(
                out=b1.ap()[:, o1:o1 + 8192].rearrange("p (dc h) -> p dc h", dc=8),
                in_=W["L1_ffn_w1"].ap()[e_idx].rearrange("(dc p) h -> p dc h", p=128)
                [:, :, hh * 1024:(hh + 1) * 1024]), 16)
            b2, o2 = slab(e_idx, hh, 1)
            sge[pe_par].inc(e.dma_start(
                out=b2.ap()[:, o2:o2 + 8192].rearrange("p (hc d) -> p hc d", hc=8),
                in_=W["L1_ffn_w2"].ap()[e_idx].rearrange("(hc p) d -> p hc d", p=128)
                [:, hh * 8:(hh + 1) * 8, :]), 16)

    def blk3(buf, w, off, n1=16, n2=8):
        return bass.AP(buf, off, [[w, 128], [n2, n1], [1, n2]])

    def exp3i(buf, w, off, n1=16, n2=8):
        return bass.AP(buf, off, [[w, 128], [1, n1], [0, n2]])

    def exp3o(buf, w, off, n1=16, n2=8):
        return bass.AP(buf, off, [[w, 128], [0, n1], [1, n2]])

    SMw, F2w = 2048, 4096
    probs3 = blk3(SM, 2048, 0)
    with nc.Block() as blk:
        @blk.gpsimd
        def _(e):
            emit_slab_loads(e, 0)

        @blk.scalar
        def _(e):
            e.wait_ge(sv.h, sv.n)
            sa.inc(e.activation(F2[:, 0:128], F2[:, 0:128], AF.Sigmoid))

        pr3 = blk3(F2, F2w, 0)
        t83 = blk3(F2, F2w, 1024)
        io3 = exp3o(F2, F2w, 248)
        i83 = exp3o(c.CMISC, 48, 0)

        @blk.vector
        def _(e):
            e.wait_ge(sa.h, sa.n)
            sv.inc(e.tensor_reduce(blk3(SM, SMw, 256, 16, 1), pr3, AX.X, ALU.max))
            e.wait_ge(sv.h, sv.n)
            sv.inc(e.tensor_tensor(t83, pr3, exp3i(SM, SMw, 256), ALU.is_ge))
            e.wait_ge(sv.h, sv.n)
            sv.inc(e.tensor_tensor(t83, t83, io3, ALU.mult))
            e.wait_ge(sv.h, sv.n)
            sv.inc(e.tensor_scalar(F2[:, 1024:1152], F2[:, 1024:1152], 999.0,
                                   None, ALU.add))
            e.wait_ge(sv.h, sv.n)
            sv.inc(e.tensor_reduce(blk3(SM, SMw, 320, 16, 1), t83, AX.X, ALU.min))
            e.wait_ge(sv.h, sv.n)
            sv.inc(e.tensor_tensor(blk3(SM, SMw, 0), i83,
                                   exp3i(SM, SMw, 320), ALU.is_equal))
            e.wait_ge(sv.h, sv.n)
            sv.inc(e.scalar_tensor_tensor(F2[:, 1024:1152], SM[:, 0:128], -1e9,
                                          F2[:, 0:128], ALU.mult, ALU.add))
            e.wait_ge(sv.h, sv.n)
            sv.inc(e.tensor_reduce(blk3(SM, SMw, 272, 16, 1), t83, AX.X, ALU.max))
            e.wait_ge(sv.h, sv.n)
            sv.inc(e.tensor_tensor(t83, t83, exp3i(SM, SMw, 272), ALU.is_ge))
            e.wait_ge(sv.h, sv.n)
            sv.inc(e.tensor_tensor(t83, t83, io3, ALU.mult))
            e.wait_ge(sv.h, sv.n)
            sv.inc(e.tensor_scalar(F2[:, 1024:1152], F2[:, 1024:1152], 999.0,
                                   None, ALU.add))
            e.wait_ge(sv.h, sv.n)
            sv.inc(e.tensor_reduce(blk3(SM, SMw, 336, 16, 1), t83, AX.X, ALU.min))
            e.wait_ge(sv.h, sv.n)
            sv.inc(e.tensor_tensor(blk3(SM, SMw, 128), i83,
                                   exp3i(SM, SMw, 336), ALU.is_equal))
            e.wait_ge(sv.h, sv.n)
            sv.inc(e.tensor_tensor(SM[:, 288:304], SM[:, 256:272],
                                   SM[:, 272:288], ALU.add))
            e.wait_ge(sv.h, sv.n)
            sv.inc(e.tensor_scalar(SM[:, 288:304], SM[:, 288:304], 1e-6,
                                   None, ALU.add))
            e.wait_ge(sv.h, sv.n)
            sv.inc(e.reciprocal(SM[:, 304:320], SM[:, 288:304]))
            e.wait_ge(sv.h, sv.n)
            sv.inc(e.tensor_tensor(SM[:, 288:304], SM[:, 256:272],
                                   SM[:, 304:320], ALU.mult))
            e.wait_ge(sv.h, sv.n)
            sv.inc(e.tensor_tensor(SM[:, 304:320], SM[:, 272:288],
                                   SM[:, 304:320], ALU.mult))
            e.wait_ge(sv.h, sv.n)
            sv.inc(e.tensor_tensor(SM[:, 384:512], SM[:, 0:128],
                                   SM[:, 128:256], ALU.add))

        @blk.tensor
        def _(e):
            e.wait_ge(sv.h, sv.n)
            sp.inc(e.matmul(PS_A[0:1, 128:256], c.CMISC[:, 42:43],
                            SM[:, 384:512], start=True, stop=True))

        @blk.vector
        def _(e):
            e.wait_ge(sp.h, sp.n)
            sv.inc(e.tensor_copy(F2[0:1, 440:568], PS_A[0:1, 128:256]))
            e.wait_ge(sv.h, sv.n)
            sv.inc(e.tensor_reduce(bass.AP(F2, 400, [[F2w, 1], [1, 8], [0, 1]]),
                                   bass.AP(F2, 440, [[F2w, 1], [1, 8], [8, 16]]),
                                   AX.X, ALU.add))

    with nc.Block() as blk:
        @blk.gpsimd
        def _(e):
            e.wait_ge(c.sem("sv").h, c.sem("sv").n)
            sg.inc(e.dma_start(out=c.d["d_cin"].ap(), in_=F2[0:1, 400:408]), 16)
            e.wait_ge(sg.h, sg.n)
            e.collective_compute(
                "AllGather", ALU.bypass, replica_groups=[list(range(NCORES))],
                ins=[c.d["d_cin"].ap().opt()], outs=[c.d["d_cout"].ap().opt()],
            ).then_inc(scc.h, 1)
            scc.n += 1
            e.wait_ge(scc.h, scc.n)
            sg.inc(e.dma_start(out=F2[0:NCORES, 408:416], in_=c.d["d_cout"].ap()), 16)

        @blk.tensor
        def _(e):
            e.wait_ge(sg.h, sg.n)
            e.wait_ge(sd.h, sd.n)
            sp.inc(e.matmul(PS_A[0:1, 24:32], F2[0:NCORES, 2700:2701],
                            F2[0:NCORES, 408:416], start=True, stop=True))

        @blk.vector
        def _(e):
            e.wait_ge(sp.h, sp.n)
            sv.inc(e.tensor_copy(F2[0:1, 416:424], PS_A[0:1, 24:32]))

        @blk.tensor
        def _(e):
            e.wait_ge(sv.h, sv.n)
            sp.inc(e.matmul(PS_A[:, 32:40], c.CONES[0:1, 0:128], F2[0:1, 416:424],
                            start=True, stop=True))

        @blk.vector
        def _(e):
            e.wait_ge(sp.h, sp.n)
            sv.inc(e.tensor_copy(F2[:, 432:440], PS_A[:, 32:40]))

    with nc.Block() as blk:
        @blk.tensor
        def _(e):
            sp.inc(e.transpose(PS_A[:, 384:385], F2[0:1, 440:568],
                               c.CIDENTF[0:1, 0:1]))

        @blk.vector
        def _(e):
            e.wait_ge(sp.h, sp.n)
            sv.inc(e.tensor_copy(F2[:, 1408:1409], PS_A[:, 384:385]))

        @blk.tensor
        def _(e):
            e.wait_ge(sv.h, sv.n)
            sp.inc(e.matmul(PS_A[0:1, 256:384], F2[:, 1408:1409], c.CSU8[:],
                            start=True, stop=True))

        @blk.vector
        def _(e):
            e.wait_ge(sp.h, sp.n)
            sv.inc(e.tensor_copy(F2[0:1, 1152:1280], PS_A[0:1, 256:384]))

        @blk.tensor
        def _(e):
            e.wait_ge(sv.h, sv.n)
            e.matmul(PS_A[:, 0:128], c.CSU[:], SM[:, 384:512],
                     start=True, stop=False, skip_group_check=True)
            sp.inc(e.matmul(PS_A[:, 0:128], c.CONES[0:1, 0:128],
                            F2[0:1, 1152:1280], start=False, stop=True,
                            skip_group_check=True))

        t8f = F2[:, 1024:1152]
        cpb3 = exp3o(F2, F2w, 432)

        @blk.vector
        def _(e):
            e.wait_ge(sp.h, sp.n)
            sv.inc(e.tensor_tensor(t8f, PS_A[:, 0:128], SM[:, 0:128], ALU.mult))
            e.wait_ge(sv.h, sv.n)
            sv.inc(e.tensor_reduce(blk3(SM, SMw, 512, 16, 1),
                                   blk3(F2, F2w, 1024), AX.X, ALU.add))
            e.wait_ge(sv.h, sv.n)
            sv.inc(e.tensor_tensor(t8f, PS_A[:, 0:128], SM[:, 128:256], ALU.mult))
            e.wait_ge(sv.h, sv.n)
            sv.inc(e.tensor_reduce(blk3(SM, SMw, 528, 16, 1),
                                   blk3(F2, F2w, 1024), AX.X, ALU.add))
            e.wait_ge(sv.h, sv.n)
            sv.inc(e.tensor_tensor(blk3(F2, F2w, 1024), cpb3,
                                   blk3(SM, SMw, 0), ALU.mult))
            e.wait_ge(sv.h, sv.n)
            sv.inc(e.tensor_reduce(blk3(SM, SMw, 544, 16, 1),
                                   blk3(F2, F2w, 1024), AX.X, ALU.add))
            e.wait_ge(sv.h, sv.n)
            sv.inc(e.tensor_tensor(blk3(F2, F2w, 1024), cpb3,
                                   blk3(SM, SMw, 128), ALU.mult))
            e.wait_ge(sv.h, sv.n)
            sv.inc(e.tensor_reduce(blk3(SM, SMw, 560, 16, 1),
                                   blk3(F2, F2w, 1024), AX.X, ALU.add))
            e.wait_ge(sv.h, sv.n)
            e.tensor_tensor(SM[:, 576:592], SM[:, 512:528], SM[:, 544:560],
                            ALU.add)
            sv.inc(e.tensor_tensor(SM[:, 592:608], SM[:, 528:544],
                                   SM[:, 560:576], ALU.add))
            e.wait_ge(sv.h, sv.n)
            sv.inc(e.tensor_scalar(SM[:, 608:624], SM[:, 576:592],
                                   float(CAPACITY), None, ALU.is_lt))
            e.wait_ge(sv.h, sv.n)
            sv.inc(e.tensor_tensor(SM[:, 352:368], SM[:, 288:304],
                                   SM[:, 608:624], ALU.mult))
            e.wait_ge(sv.h, sv.n)
            sv.inc(e.tensor_scalar(SM[:, 608:624], SM[:, 592:608],
                                   float(CAPACITY), None, ALU.is_lt))
            e.wait_ge(sv.h, sv.n)
            sv.inc(e.tensor_tensor(SM[:, 368:384], SM[:, 304:320],
                                   SM[:, 608:624], ALU.mult))
            e.wait_ge(sv.h, sv.n)
            BIG = 8.0 * S_MAX
            e.scalar_tensor_tensor(SM[:, 624:640], SM[:, 320:336],
                                   float(S_MAX), SM[:, 512:528],
                                   ALU.mult, ALU.add)
            sv.inc(e.scalar_tensor_tensor(SM[:, 640:656], SM[:, 336:352],
                                          float(S_MAX), SM[:, 528:544],
                                          ALU.mult, ALU.add))
            e.wait_ge(sv.h, sv.n)
            e.tensor_scalar(SM[:, 656:672], SM[:, 512:528], float(S_MAX),
                            None, ALU.is_lt)
            sv.inc(e.tensor_scalar(SM[:, 608:624], SM[:, 528:544],
                                   float(S_MAX), None, ALU.is_lt))
            e.wait_ge(sv.h, sv.n)
            e.tensor_scalar(SM[:, 624:640], SM[:, 624:640], BIG, None,
                            ALU.subtract)
            sv.inc(e.tensor_scalar(SM[:, 640:656], SM[:, 640:656], BIG, None,
                                   ALU.subtract))
            e.wait_ge(sv.h, sv.n)
            e.scalar_tensor_tensor(SM[:, 624:640], SM[:, 624:640], 1.0,
                                   SM[:, 656:672], ALU.mult, ALU.mult)
            sv.inc(e.scalar_tensor_tensor(SM[:, 640:656], SM[:, 640:656], 1.0,
                                          SM[:, 608:624], ALU.mult, ALU.mult))
            e.wait_ge(sv.h, sv.n)
            e.tensor_scalar(SM[:, 624:640], SM[:, 624:640], BIG, None, ALU.add)
            sv.inc(e.tensor_scalar(SM[:, 640:656], SM[:, 640:656], BIG, None,
                                   ALU.add))
            e.wait_ge(sv.h, sv.n)
            e.tensor_copy(SI[:, 16:32], SM[:, 624:640])
            sv.inc(e.tensor_copy(SI[:, 32:48], SM[:, 640:656]))
            e.wait_ge(sv.h, sv.n)
            e.tensor_copy(bass.AP(F2, 1537, [[F2w, 128], [8, 16], [0, 1]]),
                          blk3(SM, SMw, 352, 16, 1))
            sv.inc(e.tensor_copy(bass.AP(F2, 1541, [[F2w, 128], [8, 16], [0, 1]]),
                                 blk3(SM, SMw, 368, 16, 1)))

        @blk.gpsimd
        def _(e):
            e.wait_ge(sv.h, sv.n)
            for s in range(32):
                tl, k = s // 2, s % 2
                if s > 0:
                    e.wait_ge(sg.h, sg.n)
                sg.inc(e.indirect_dma_start(
                    c.d["d_slotmeta"].ap(),
                    IndirectOffsetOnAxis(SI[:, 16 + 16 * k + tl:17 + 16 * k + tl], 0),
                    F2[:, 1536 + 4 * s:1540 + 4 * s], None), 16)
            e.wait_ge(sg.h, sg.n)

    marks = {}
    xgt_ready = {}
    mslot_mark = {}
    xgb_mark = {}
    prepev = {}
    NIT = 12

    def emit_prep(blk, e_next, st, meta_at, gather_at, tr_at, cur_e):
        mslot = st % 2
        mbuf = F2[:, 604 + mslot * 4:608 + mslot * 4]

        def meta(e, e_next=e_next, st=st, mslot=mslot, mbuf=mbuf):
            if mslot in mslot_mark:
                e.wait_ge(sv.h, mslot_mark[mslot])
            e.wait_ge(sg.h, sg.n)
            smm[mslot].inc(e.dma_start(
                out=mbuf, in_=c.d["d_slotmeta"].ap()
                [e_next * S_MAX + st * 128:e_next * S_MAX + (st + 1) * 128, :]), 16)

        def extract(e, e_next=e_next, st=st, mslot=mslot, mbuf=mbuf):
            e.wait_ge(smm[mslot].h, smm[mslot].n)
            if ("stt", e_next - 1, st) in marks:
                e.wait_ge(sv.h, marks[("stt", e_next - 1, st)])
            if ("scat", e_next - 1, st) in marks:
                e.wait_ge(sgs[st % 2].h, marks[("scat", e_next - 1, st)])
            e.tensor_copy(SI[:, 8 + st:9 + st], mbuf[:, 0:1])
            e.tensor_copy(F2[:, 624 + st:625 + st], mbuf[:, 1:2])
            marks[("mx", e_next, st)] = sv.inc(
                e.tensor_copy(SI[:, 2 + st:3 + st], mbuf[:, 2:3]))
            mslot_mark[mslot] = marks[("mx", e_next, st)]

        def gather(e, e_next=e_next, st=st):
            e.wait_ge(sv.h, marks[("mx", e_next, st)])
            gslot = st % 2
            if gslot in xgb_mark:
                e.wait_ge(sp.h, xgb_mark[gslot])
            sgg[gslot].inc(e.indirect_dma_start(
                F1b[:, gslot * 1024:(gslot + 1) * 1024], None,
                c.d["d_srcb"].ap(), IndirectOffsetOnAxis(SI[:, 8 + st:9 + st], 0)), 16)

        return meta, extract, gather

    with nc.Block() as blk:
        emit_warmup(c, blk)

        for st in range(6):
            meta, extract, gather = emit_prep(blk, 0, st, 0, 0, 0, -1)
            blk.sync(lambda e, meta=meta: meta(e))
            blk.vector(lambda e, extract=extract: extract(e))
            blk.gpsimd(lambda e, gather=gather: gather(e))

            @blk.tensor
            def _(e, st=st):
                e.wait_ge(sgg[st % 2].h, sgg[st % 2].n)
                last = None
                for dc in range(8):
                    last = e.transpose(psab(c)[:, 1024 + dc * 128:1024 + (dc + 1) * 128],
                                       F1b[:, (st % 2) * 1024 + dc * 128:(st % 2) * 1024 + (dc + 1) * 128],
                                       c.CIDENT[:])
                marks[("ptr", 0, st)] = sp.inc(last)
                xgb_mark[st % 2] = marks[("ptr", 0, st)]

            @blk.vector
            def _(e, st=st):
                e.wait_ge(sp.h, marks[("ptr", 0, st)])
                xgt_ready[(0, st)] = sv.inc(
                    e.tensor_copy(A5[:, st * 1024:(st + 1) * 1024], psab(c)[:, 1024:2048]))
                prepev[0] = xgt_ready[(0, st)]

        for e_idx in range(8):
            pe_par = e_idx % 2

            @blk.gpsimd
            def _(e, e_idx=e_idx):
                if e_idx + 1 < 8:
                    if ("mm2", e_idx - 1, NIT - 1) in marks:
                        e.wait_ge(sp.h, marks[("mm2", e_idx - 1, NIT - 1)])
                    emit_slab_loads(e, e_idx + 1)

            for it in range(NIT + 3):
                hh, st = divmod(it, 6) if it < NIT else (None, None)

                @blk.tensor
                def _(e, e_idx=e_idx, it=it, hh=hh, st=st, pe_par=pe_par):
                    half = it % 2
                    if it < NIT:
                        b1, o1 = slab(e_idx, hh, 0)
                        e.wait_ge(sge[pe_par].h, sge[pe_par].n)
                        e.wait_ge(sv.h, xgt_ready[(e_idx, st)])
                        if ("hbev", e_idx, it - 2) in marks:
                            e.wait_ge(sv.h, marks[("hbev", e_idx, it - 2)])
                        if half in prepev:
                            e.wait_ge(sv.h, prepev[half])
                        last = None
                        for ns in range(2):
                            for dc in range(8):
                                last = e.matmul(
                                    PS_A[:, half * 1024 + ns * 512:half * 1024 + (ns + 1) * 512],
                                    A5[:, st * 1024 + dc * 128:st * 1024 + (dc + 1) * 128],
                                    b1[:, o1 + dc * 1024 + ns * 512:o1 + dc * 1024 + (ns + 1) * 512],
                                    start=(dc == 0), stop=(dc == 7))
                        marks[("mm1", e_idx, it)] = sp.inc(last)
                    if 0 <= it - 1 < NIT:
                        e.wait_ge(sa.h, marks[("gelu", e_idx, it - 1)])
                        if (it - 1) % 2 in prepev:
                            e.wait_ge(sv.h, prepev[(it - 1) % 2])
                        h1 = (it - 1) % 2
                        last = None
                        for dc in range(8):
                            last = e.transpose(
                                psab(c)[:, h1 * 2048 + 1024 + dc * 128:h1 * 2048 + 1024 + (dc + 1) * 128],
                                F2b[:, 3328 + h1 * 1024 + dc * 128:3328 + h1 * 1024 + (dc + 1) * 128],
                                c.CIDENT[:])
                        marks[("trH", e_idx, it - 1)] = sp.inc(last)
                    if 0 <= it - 2 < NIT:
                        it2 = it - 2
                        hh2, st2 = divmod(it2, 6)
                        b2, o2 = slab(e_idx, hh2, 1)
                        h2 = it2 % 2
                        e.wait_ge(sv.h, marks[("hbev", e_idx, it2)])
                        if ("outev", e_idx, it2 - 2) in marks:
                            e.wait_ge(sv.h, marks[("outev", e_idx, it2 - 2)])
                        last = None
                        for ns in range(2):
                            for hcc in range(8):
                                last = e.matmul(
                                    PS_B[:, h2 * 1024 + ns * 512:h2 * 1024 + (ns + 1) * 512],
                                    A5[:, 6144 + h2 * 1024 + hcc * 128:6144 + h2 * 1024 + (hcc + 1) * 128],
                                    b2[:, o2 + hcc * 1024 + ns * 512:o2 + hcc * 1024 + (ns + 1) * 512],
                                    start=(hcc == 0), stop=(hcc == 7))
                        marks[("mm2", e_idx, it2)] = sp.inc(last)

                @blk.scalar
                def _(e, e_idx=e_idx, it=it, hh=hh, st=st):
                    if it < NIT:
                        half = it % 2
                        e.wait_ge(sp.h, marks[("mm1", e_idx, it)])
                        if ("trH", e_idx, it - 2) in marks:
                            e.wait_ge(sp.h, marks[("trH", e_idx, it - 2)])
                        e.activation(F2b[:, 3328 + half * 1024:3328 + half * 1024 + 512],
                                     PS_A[:, half * 1024:half * 1024 + 512], AF.Gelu)
                        marks[("gelu", e_idx, it)] = sa.inc(e.activation(
                            F2b[:, 3328 + half * 1024 + 512:3328 + half * 1024 + 1024],
                            PS_A[:, half * 1024 + 512:half * 1024 + 1024], AF.Gelu))

                if e_idx + 1 < 8 and 9 <= it < 15 and it - 9 < 6:
                    stp2 = it - 9

                    @blk.tensor
                    def _(e, e_idx=e_idx, it=it, stp2=stp2):
                        e.wait_ge(sgg[stp2 % 2].h, sgg[stp2 % 2].n)
                        e.wait_ge(sp.h, marks.get(("mm1", e_idx, 6 + stp2), 0))
                        if it < NIT:
                            e.wait_ge(sa.h, marks[("gelu", e_idx, it)])
                        half = it % 2
                        last = None
                        for dc in range(8):
                            last = e.transpose(
                                psab(c)[:, half * 2048 + 1024 + dc * 128:
                                        half * 2048 + 1024 + (dc + 1) * 128],
                                F1b[:, (stp2 % 2) * 1024 + dc * 128:(stp2 % 2) * 1024 + (dc + 1) * 128],
                                c.CIDENT[:])
                        marks[("ptr", e_idx + 1, stp2)] = sp.inc(last)
                        xgb_mark[stp2 % 2] = marks[("ptr", e_idx + 1, stp2)]

                    @blk.vector
                    def _(e, e_idx=e_idx, it=it, stp2=stp2):
                        half = it % 2
                        e.wait_ge(sp.h, marks[("ptr", e_idx + 1, stp2)])
                        xgt_ready[(e_idx + 1, stp2)] = sv.inc(e.tensor_copy(
                            A5[:, stp2 * 1024:(stp2 + 1) * 1024],
                            psab(c)[:, half * 2048 + 1024:half * 2048 + 2048]))
                        prepev[half] = xgt_ready[(e_idx + 1, stp2)]

                @blk.vector
                def _(e, e_idx=e_idx, it=it):
                    if 0 <= it - 1 < NIT:
                        h1 = (it - 1) % 2
                        e.wait_ge(sp.h, marks[("trH", e_idx, it - 1)])
                        if ("mm2", e_idx, it - 3) in marks:
                            e.wait_ge(sp.h, marks[("mm2", e_idx, it - 3)])
                        marks[("hbev", e_idx, it - 1)] = sv.inc(e.tensor_copy(
                            A5[:, 6144 + h1 * 1024:6144 + (h1 + 1) * 1024],
                            psab(c)[:, h1 * 2048 + 1024:h1 * 2048 + 2048]))
                    if 0 <= it - 2 < NIT:
                        it2 = it - 2
                        hh2, st2 = divmod(it2, 6)
                        h2 = it2 % 2
                        oacc = F1[:, 2048 + st2 * 1024:3072 + st2 * 1024]
                        e.wait_ge(sp.h, marks[("mm2", e_idx, it2)])
                        if hh2 == 0:
                            if ("scat", e_idx - 1, st2) in marks:
                                e.wait_ge(sgs[st2 % 2].h, marks[("scat", e_idx - 1, st2)])
                            marks[("outev", e_idx, it2)] = sv.inc(e.tensor_scalar(
                                oacc, PS_B[:, h2 * 1024:(h2 + 1) * 1024],
                                F2[:, 624 + st2:625 + st2], None, ALU.mult))
                        else:
                            marks[("stt", e_idx, st2)] = sv.inc(e.scalar_tensor_tensor(
                                oacc, PS_B[:, h2 * 1024:(h2 + 1) * 1024],
                                F2[:, 624 + st2:625 + st2], oacc, ALU.mult, ALU.add))
                            marks[("outev", e_idx, it2)] = marks[("stt", e_idx, st2)]

                @blk.gpsimd
                def _(e, e_idx=e_idx, it=it):
                    if 0 <= it - 2 < NIT:
                        it2 = it - 2
                        hh2, st2 = divmod(it2, 6)
                        if hh2 == 1:
                            e.wait_ge(sv.h, marks[("stt", e_idx, st2)])
                            e.wait_ge(sgs[0].h, sgs[0].n)
                            e.wait_ge(sgs[1].h, sgs[1].n)
                            marks[("scat", e_idx, st2)] = sgs[st2 % 2].inc(
                                e.indirect_dma_start(
                                    c.d["d_moeout"].ap(),
                                    IndirectOffsetOnAxis(SI[:, 2 + st2:3 + st2], 0),
                                    F1[:, 2048 + st2 * 1024:3072 + st2 * 1024], None), 16)

                if e_idx + 1 < 8 and 8 <= it < 14:
                    stp = it - 8
                    meta, extract, gather = emit_prep(blk, e_idx + 1, stp, 0, 0, 0, e_idx)
                    blk.sync(lambda e, meta=meta: meta(e))
                    blk.vector(lambda e, extract=extract: extract(e))
                    blk.gpsimd(lambda e, gather=gather: gather(e))

        @blk.gpsimd
        def _(e):
            e.wait_ge(sgs[0].h, sgs[0].n)
            e.wait_ge(sgs[1].h, sgs[1].n)

    sb2 = [c.sem("sgg0"), c.sem("sgg1")]
    with nc.Block() as blk:
        bufs = [dict(x1t=F1[:, 0:1024], at1=F1[:, 1024:2048],
                     srct=F1[:, 2048:3072], m0=F1[:, 3072:4096],
                     m1t=F1[:, 4096:5120], scr=F1[:, 5120:6144],
                     stats=SM[:, 0:4]),
                dict(x1t=F1[:, 6144:7168], at1=F1[:, 7168:8192],
                     srct=F2[:, 0:1024], m0=F2[:, 1024:2048],
                     m1t=F2[:, 2048:3072], scr=F2[:, 3072:4096],
                     stats=SM[:, 4:8])]
        for tp in range(NT // 2):
            jobs = []
            for k in range(2):
                t = 2 * tp + k
                b = bufs[k]

                @blk.sync
                def _(e, t=t, b=b, k=k):
                    e.wait_ge(ss[k].h, ss[k].n)
                    for dst, nm in [(b["x1t"], "d_x1"), (b["at1"], "d_attn1"),
                                    (b["srct"], "d_src")]:
                        sl[k].inc(e.dma_start(out=dst,
                                              in_=c.d[nm].ap()[t * 128:(t + 1) * 128, :]), 16)

                @blk.gpsimd
                def _(e, t=t, b=b, k=k):
                    e.wait_ge(ss[k].h, ss[k].n)
                    for m, dst in [(0, b["m0"]), (1, b["m1t"])]:
                        sb2[k].inc(e.dma_start(
                            out=dst,
                            in_=bass.AP(c.d["d_moeout"], (2 * t * 128 + m) * 1024,
                                        [[2048, 128], [1, 1024]])), 16)

            @blk.vector
            def _(e, tp=tp):
                for k in range(2):
                    e.wait_ge(sl[k].h, sl[k].n)
                    e.wait_ge(sb2[k].h, sb2[k].n)
                e.wait_ge(sv.h, sv.n)
                last = None
                for k in range(2):
                    last = e.tensor_tensor(bufs[k]["m0"], bufs[k]["m0"],
                                           bufs[k]["m1t"], ALU.add)
                sv.inc(last)
                e.wait_ge(sv.h, sv.n)
                last = None
                for k in range(2):
                    last = e.tensor_tensor(bufs[k]["srct"], bufs[k]["srct"],
                                           bufs[k]["m0"], ALU.add)
                sv.inc(last)

            rms_jobs(c, blk, [dict(xin=b["srct"], out=b["srct"],
                                   wbcast=SM[:, 1024:2048], stats=b["stats"],
                                   scratch=b["scr"]) for b in bufs])

            @blk.vector
            def _(e):
                e.wait_ge(sv.h, sv.n)
                last = None
                for k in range(2):
                    last = e.tensor_tensor(bufs[k]["x1t"], bufs[k]["x1t"],
                                           bufs[k]["at1"], ALU.add)
                sv.inc(last)
                e.wait_ge(sv.h, sv.n)
                last = None
                for k in range(2):
                    last = e.tensor_tensor(bufs[k]["x1t"], bufs[k]["x1t"],
                                           bufs[k]["srct"], ALU.add)
                sv.inc(last)

            for k in range(2):
                @blk.gpsimd
                def _(e, tp=tp, k=k):
                    e.wait_ge(c.sem("sv").h, c.sem("sv").n)
                    ss[k].inc(e.dma_start(
                        out=c.d["out"].ap()[(2 * tp + k) * 128:(2 * tp + k + 1) * 128, :],
                        in_=bufs[k]["x1t"]), 16)

        @blk.gpsimd
        def _(e):
            e.wait_ge(ss[0].h, ss[0].n)
            e.wait_ge(ss[1].h, ss[1].n)





def stage_launch_lists():
    if MODE == "fused":
        return [STAGES_ALL]
    return [[s] for s in STAGES_ALL]


def kernel(**inputs):
    ncores = 8
    x = np.ascontiguousarray(np.asarray(inputs["L0_attn_x"], dtype=np.float32))
    consts = [host_consts(cid) for cid in range(ncores)]
    outs = None
    bound = {}
    for stages in stage_launch_lists():
        nc = build_program(stages)
        in_maps = []
        for cid in range(ncores):
            m = {"x_in": x[cid]}
            m.update(consts[cid])
            for s in stages:
                for w in STAGE_WEIGHTS[s]:
                    m[w] = np.ascontiguousarray(np.asarray(inputs[w], dtype=np.float32))
            for s in stages:
                for b in BOUNDARY_NEED[s]:
                    if BOUNDARY[b] not in stages:
                        m[b] = bound[b][cid]
            in_maps.append(m)
        res = run_bass_kernel_spmd(nc, in_maps, core_ids=list(range(ncores)))
        for name in BOUNDARY:
            if BOUNDARY[name] in stages and MODE == "staged":
                bound[name] = [res.results[cid][name] for cid in range(ncores)]
        if "moe" in stages:
            outs = np.stack([res.results[cid]["out"] for cid in range(ncores)])
    return outs
